# version 19
# baseline (speedup 1.0000x reference)
import numpy as np
import ml_dtypes
from contextlib import ExitStack

import concourse.bass as bass
import concourse.mybir as mybir
from concourse.bass_utils import run_bass_kernel_spmd

F32 = mybir.dt.float32
BF16 = mybir.dt.bfloat16
I32 = mybir.dt.int32
I16 = mybir.dt.int16
U16 = mybir.dt.uint16
AF = mybir.ActivationFunctionType
ALU = mybir.AluOpType
AX = mybir.AxisListType
BF = ml_dtypes.bfloat16

B, S, D = 4, 4096, 1024
H, HD = 16, 64
W = 16
NCORE = 8
T = 2048
PAD = 16
TP = 2176
NT = TP // 128
NTR = T // 128
QB, KB = 96, 128
NBLK = 22
TQ = QB * NBLK
FF = 3072
NFC = FF // 128
KD = D // 128
UPS = FF // NCORE
DNS = D // NCORE
AGCOLS = 9216
AGW = 128 * AGCOLS
EXPB = -4.0
QA = 32767.0
EPS = 1e-5
TGS = [512, 512, 512, 512, 128]
TGO = [0, 512, 1024, 1536, 2048]
NTG = 5
NARE = 25
C_SX, C_ISX, C_SG, C_ISG, C_SU, C_ISU, C_SG2, C_ISG2, C_SO, C_ISO = range(10)
M_X, M_G, M_U, M_G2, M_O = range(5)
SHUF = [4 * (h % 4) + h // 4 for h in range(H)]


def build(dbg=()):
    nc = bass.Bass("TRN2", target_bir_lowering=False)
    top = ExitStack()
    ap = lambda t: t.ap()

    xp = ap(nc.dram_tensor("xp", [TP, D], F32, kind="ExternalInput"))
    wqk = ap(nc.dram_tensor("wqk", [D, 2 * D], F32, kind="ExternalInput"))
    wvw = ap(nc.dram_tensor("wvw", [D, D], F32, kind="ExternalInput"))
    bqkT = ap(nc.dram_tensor("bqkT", [128, 16], F32, kind="ExternalInput"))
    bvin = ap(nc.dram_tensor("bvin", [1, D], F32, kind="ExternalInput"))
    woin = ap(nc.dram_tensor("woin", [D, D], F32, kind="ExternalInput"))
    boin = ap(nc.dram_tensor("boin", [1, D], F32, kind="ExternalInput"))
    wup_s = ap(nc.dram_tensor("wup_s", [UPS, D], F32, kind="ExternalInput"))
    wgate_s = ap(nc.dram_tensor("wgate_s", [UPS, D], F32, kind="ExternalInput"))
    wdown_s = ap(nc.dram_tensor("wdown_s", [DNS, FF], F32, kind="ExternalInput"))
    bandmask = ap(nc.dram_tensor("bandmask", [KB, NBLK * QB], BF16, kind="ExternalInput"))
    idb = ap(nc.dram_tensor("idb", [128, 128], BF16, kind="ExternalInput"))
    idf = ap(nc.dram_tensor("idf", [128, 128], F32, kind="ExternalInput"))
    out = ap(nc.dram_tensor("out", [T, D], F32, kind="ExternalOutput"))

    qh = ap(nc.dram_tensor("qh", [128, KD * TP], BF16))
    kh = ap(nc.dram_tensor("kh", [128, KD * TP], BF16))
    vh = ap(nc.dram_tensor("vh", [TP, H * 65], BF16))
    x2h = ap(nc.dram_tensor("x2h", [T, D], F32))
    uph = ap(nc.dram_tensor("uph", [NFC, 128 * T], BF16))
    agin = ap(nc.dram_tensor("agin", [1, AGW], BF16))
    agout = ap(nc.dram_tensor("agout", [NCORE, AGW], BF16))
    cc = [(ap(nc.dram_tensor(f"ccin{i}", [1, 2], F32)),
           ap(nc.dram_tensor(f"ccout{i}", [1, 2], F32))) for i in range(4)]
    ccw = (ap(nc.dram_tensor("ccwin", [1, 2], F32)),
           ap(nc.dram_tensor("ccwout", [1, 2], F32)))

    dbgd = {}
    def mkdbg(name, shape, dt):
        if name in dbg:
            dbgd[name] = ap(nc.dram_tensor("dbg_" + name, shape, dt, kind="ExternalOutput"))
    mkdbg("qh", [128, KD * TP], BF16)
    mkdbg("kh", [128, KD * TP], BF16)
    mkdbg("vh", [TP, H * 65], BF16)
    mkdbg("ag", [NCORE, AGW], BF16)
    mkdbg("attnoT", [128, KD * TQ], BF16)
    mkdbg("x2", [T, D], F32)
    mkdbg("xqT", [128, KD * T], BF16)
    mkdbg("sc", [128, 16], F32)
    mkdbg("arena", [128, NARE * T], BF16)
    mkdbg("parena", [128, NARE * T], BF16)
    mkdbg("oarena", [128, NARE * T], BF16)
    mkdbg("uph", [NFC, 128 * T], BF16)

    eL = top.enter_context
    sbL = lambda n_, s_, d_=BF16: eL(nc.sbuf_tensor(n_, s_, d_, side="left"))
    idb_s = sbL("idb_s", [128, 128])
    idf_s = sbL("idf_s", [128, 128], F32)
    bqkT_s = sbL("bqkT_s", [128, 16], F32)
    bv_s = sbL("bv_s", [1, D])
    bo_s = sbL("bo_s", [1, D])
    onesb = sbL("onesb", [1, 512])
    onesf = sbL("onesf", [1, 128], F32)
    svec = sbL("svec", [128, 16], F32)
    sscal = sbL("sscal", [1, 16], F32)
    stats = sbL("stats", [128, 4 * 14], F32)
    lncol = sbL("lncol", [128, 2 * 6], F32)
    macc = sbL("macc", [128, 8], F32)
    gcol = sbL("gcol", [128, 4], F32)
    cstv = sbL("cstv", [128, 2], F32)
    masks = sbL("masks", [KB, NBLK * QB])

    pS = eL(nc.psum_tensor("pS", [128, 2048], F32))
    pV = eL(nc.psum_tensor("pV", [128, 1024], F32))
    pT = eL(nc.psum_tensor("pT", [128, 2048], BF16))

    class NS:
        pass
    sm = NS()
    for n_ in ["cst", "cbv", "gld0", "gld1", "wqv", "wqt", "wqe", "agw0", "agw1", "agc",
               "xin0", "xin1", "lnv1", "lns1", "lnv2", "lns2", "xtt", "xte",
               "qkl0", "qkl1", "qkl2", "qkl3", "wvl", "qkm", "qke",
               "qss0", "qss1", "vm", "ve", "vss0", "vss1", "vss2", "vss3",
               "abi0", "abi1", "qkb", "exps", "pnv", "ptt", "pte", "pvm", "pve",
               "wol", "xrd0", "xrd1", "wom", "l3v1", "l3s1", "l3v2", "l3s2",
               "x2a", "x2w0", "x2w1", "l2v1", "l2s1", "l2v2", "l2s2", "mxv",
               "sp_t1", "sp_v1", "sp_g", "sp_c", "sp_v2", "sp_t2", "sp_v3",
               "xqk", "xqq", "xqt", "xqe", "qkev",
               "wgl0", "wgl1", "wgl2", "wgl3", "gm", "gev", "gmx",
               "upw0", "upw1",
               "bv1", "bs1", "bv2",
               "uprd0", "uprd1", "cs1", "cv2",
               "wdl", "ov2", "os1", "dm", "dlv1", "dls1", "dlv2", "dls2",
               "x2rd0", "x2rd1", "oad", "ost0", "ost1", "dbgs"]:
        setattr(sm, n_, eL(nc.semaphore(n_)))
    gld = [sm.gld0, sm.gld1]
    agwS = [sm.agw0, sm.agw1]
    xinS = [sm.xin0, sm.xin1]
    qklS = [sm.qkl0, sm.qkl1, sm.qkl2, sm.qkl3]
    qssS = [sm.qss0, sm.qss1]
    vssS = [sm.vss0, sm.vss1, sm.vss2, sm.vss3]
    abiS = [sm.abi0, sm.abi1]
    xrdS = [sm.xrd0, sm.xrd1]
    x2wS = [sm.x2w0, sm.x2w1]
    wglS = [sm.wgl0, sm.wgl1, sm.wgl2, sm.wgl3]
    upwS = [sm.upw0, sm.upw1]
    uprdS = [sm.uprd0, sm.uprd1]
    x2rdS = [sm.x2rd0, sm.x2rd1]
    ostS = [sm.ost0, sm.ost1]
    ndbg = [0]
    marks = []
    def mark(name):
        n = sum(len(b.instructions) for b in nc.m.functions[0].blocks)
        marks.append((name, n))
    build.marks = marks

    blk = eL(nc.Block())


    right0 = ExitStack()
    eR = right0.enter_context
    sbR = lambda n_, s_, d_=BF16: eR(nc.sbuf_tensor(n_, s_, d_, side="right"))
    xlnT = sbR("xlnT", [128, KD * TP])
    xring = sbR("xring", [128, 2 * D], F32)
    lring = sbR("lring", [128, 2 * D])
    wkr = sbR("wkr", [128, 4 * KD * 128])
    wvr = sbR("wvr", [128, KD * D])
    qkst = sbR("qkst", [128, 2 * TP])
    vst_s = sbR("vst_s", [128, 4 * H * 65])
    r1 = ExitStack()
    eR1 = r1.enter_context
    sbR1 = lambda n_, s_, d_=BF16: eR1(nc.sbuf_tensor(n_, s_, d_, side="right"))
    wqr = sbR1("wqr", [128, 2 * 1024], F32)
    wqq = sbR1("wqq", [128, 9 * 1024])
    agst = sbR1("agst", [128, 2 * 1024])
    wqs = sbR1("wqs", [128, 2 * 64], F32)
    kq1 = sbR1("kq1", [128, 2 * 1024], I32)

    @blk.sync
    def _(sync):
        sync.dma_start(out=idb_s[:, :], in_=idb).then_inc(sm.cst, 16)
        sync.dma_start(out=idf_s[:, :], in_=idf).then_inc(sm.cst, 16)
        sync.dma_start(out=bqkT_s[:, :], in_=bqkT).then_inc(sm.cst, 16)
        sync.dma_start(out=masks[:, :], in_=bandmask).then_inc(sm.cst, 16)

    @blk.gpsimd
    def _(g):
        g.memset(onesb[:, :], 1.0)
        g.memset(onesf[:, :], 1.0)
        g.memset(macc[:, :], 0.0)
        g.memset(sscal[:, :], 0.0)
        g.memset(cstv[:, 0:1], EPS)
        g.memset(cstv[:, 1:2], EXPB)
        g.memset(vst_s.rearrange("p (s h c) -> p s h c", s=4, c=65)[:, :, :, 64:65], 1.0)
        g.engine_nop().then_inc(sm.cst, 1)
        g.dma_start(out=bv_s[:, :], in_=bvin).then_inc(sm.cbv, 16)
        g.dma_start(out=bo_s[:, :], in_=boin).then_inc(sm.cbv, 16)
        for gi in range(4):
            g.dma_start(out=wkr[:, gi * (KD * 128):(gi + 1) * (KD * 128)].rearrange("p (a b) -> p a b", a=KD),
                        in_=wqk.rearrange("(a p) b -> p a b", p=128)[:, :, gi * 128:(gi + 1) * 128]
                        ).then_inc(qklS[gi], 16)
        for ti in range(2):
            g.dma_start(out=wqr[:, ti * 1024:(ti + 1) * 1024],
                        in_=wup_s[ti * 128:(ti + 1) * 128, :]).then_inc(gld[ti], 16)

    mark("Pm1")
    @blk.sync
    def _(sync):
        for ti in range(NT):
            if ti >= 2:
                sync.wait_ge(sm.lns2, ti - 1)
            sync.dma_start(out=xring[:, (ti % 2) * D:(ti % 2 + 1) * D],
                           in_=xp[ti * 128:(ti + 1) * 128, :]).then_inc(xinS[ti % 2], 16)

    def _ln_evac(v, ti):
        r = ti % 2
        v.wait_ge(sm.xtt, ti + 1)
        v.tensor_copy(xlnT.rearrange("p (a b) -> p a b", a=KD)[:, :, ti * 128:(ti + 1) * 128],
                      pT[:, r * 1024:(r + 1) * 1024].rearrange("p (a b) -> p a b", a=KD))
        v.drain()
        v.engine_nop().then_inc(sm.xte, 1)

    @blk.vector
    def _(v):
        for ti in range(NT):
            r = ti % 2
            st = stats[:, r * 14:r * 14 + 14]
            v.wait_ge(xinS[ti % 2], 16 * (ti // 2 + 1))
            xt_ = xring[:, r * D:(r + 1) * D]
            v.bn_stats(st[:, 0:6], xt_[:, 0:512])
            v.bn_stats(st[:, 6:12], xt_[:, 512:1024])
            v.drain()
            v.bn_aggr(st[:, 12:14], st[:, 0:12])
            v.drain()
            v.engine_nop().then_inc(sm.lnv1, 1)
            v.wait_ge(sm.lns1, ti + 1)
            v.scalar_tensor_tensor(lncol[:, r * 6 + 1:r * 6 + 2], st[:, 12:13],
                                   -1.0, lncol[:, r * 6:r * 6 + 1], ALU.mult, ALU.mult)
            v.drain()
            v.engine_nop().then_inc(sm.lnv2, 1)
            if ti >= 1:
                _ln_evac(v, ti - 1)
        _ln_evac(v, NT - 1)

    @blk.scalar
    def _(s):
        s.wait_ge(sm.cst, 65)
        for ti in range(NT):
            r = ti % 2
            st = stats[:, r * 14:r * 14 + 14]
            s.wait_ge(sm.lnv1, ti + 1)
            s.activation(lncol[:, r * 6 + 2:r * 6 + 3], st[:, 13:14], AF.Ln, bias=cstv[:, 0:1])
            s.drain()
            s.activation(lncol[:, r * 6:r * 6 + 1], lncol[:, r * 6 + 2:r * 6 + 3],
                         AF.Exp, scale=-0.5).then_inc(sm.lns1, 1)
            s.wait_ge(sm.lnv2, ti + 1)
            if ti >= 2:
                s.wait_ge(sm.xtt, ti - 1)
            s.activation(lring[:, r * D:(r + 1) * D], xring[:, r * D:(r + 1) * D],
                         AF.Identity, bias=lncol[:, r * 6 + 1:r * 6 + 2],
                         scale=lncol[:, r * 6:r * 6 + 1]).then_inc(sm.lns2, 1)

    TGTILES = [4, 8, 12, 16, 17]

    def _qk0_tg(t, tg):
        t.wait_ge(sm.xte, TGTILES[tg])
        for k in range(KD):
            wch = wkr[:, k * 128:(k + 1) * 128]
            dst = pS[:, TGO[tg]:TGO[tg] + TGS[tg]] if tg < 4 else pV[:, 0:128]
            ins_ = t.matmul(dst, wch,
                            xlnT.rearrange("p (a b) -> p a b", a=KD)[:, k, TGO[tg]:TGO[tg] + TGS[tg]],
                            start=(k == 0), stop=(k == KD - 1))
        return ins_

    @blk.tensor
    def _(t):
        t.wait_ge(sm.cst, 65)
        for ti in range(NT):
            r = ti % 2
            t.wait_ge(sm.lns2, ti + 1)
            if ti >= 2:
                t.wait_ge(sm.xte, ti - 1)
            for j in range(8):
                ins_ = t.transpose(pT[:, r * 1024 + j * 128:r * 1024 + (j + 1) * 128],
                                   lring[:, r * D + j * 128:r * D + (j + 1) * 128],
                                   idb_s[:, :])
            ins_.then_inc(sm.xtt, 1)
            if ti == 5:
                t.wait_ge(qklS[0], 16)
                _qk0_tg(t, 0)
            elif ti == 9:
                _qk0_tg(t, 1)
            elif ti == 13:
                _qk0_tg(t, 2)

    def _gld_load(g, ti):
        if ti < 3:
            srcw = wup_s[ti * 128:(ti + 1) * 128, :]
        elif ti < 6:
            srcw = wgate_s[(ti - 3) * 128:(ti - 2) * 128, :]
        else:
            srcw = wdown_s[:, (ti - 6) * 1024:(ti - 5) * 1024]
        g.wait_ge(sm.wqv, ti - 1)
        g.dma_start(out=wqr[:, (ti % 2) * 1024:(ti % 2 + 1) * 1024],
                    in_=srcw).then_inc(gld[ti % 2], 16)

    @blk.gpsimd
    def _(g):
        for gi in range(4, 16):
            g.wait_ge(sm.qkm, gi - 3)
            g.dma_start(out=wkr[:, (gi % 4) * (KD * 128):(gi % 4 + 1) * (KD * 128)].rearrange("p (a b) -> p a b", a=KD),
                        in_=wqk.rearrange("(a p) b -> p a b", p=128)[:, :, gi * 128:(gi + 1) * 128]
                        ).then_inc(qklS[gi % 4], 16)
            if 4 <= gi <= 10:
                _gld_load(g, gi - 2)
            if 5 <= gi <= 12:
                ti = gi - 5
                g.wait_ge(sm.wqe, ti + 1)
                g.dma_start(out=bass.AP(agin.tensor, ti * 1024, [[AGCOLS, 128], [1, 1024]]),
                            in_=agst[:, (ti % 2) * 1024:(ti % 2 + 1) * 1024]).then_inc(agwS[ti % 2], 16)
        for ti in range(8, 9):
            g.wait_ge(sm.wqe, ti + 1)
            g.dma_start(out=bass.AP(agin.tensor, ti * 1024, [[AGCOLS, 128], [1, 1024]]),
                        in_=agst[:, (ti % 2) * 1024:(ti % 2 + 1) * 1024]).then_inc(agwS[ti % 2], 16)
        for k in range(KD):
            g.dma_start(out=wvr[:, k * D:(k + 1) * D],
                        in_=wvw[k * 128:(k + 1) * 128, :]).then_inc(sm.wvl, 16)
        for ti in range(1, NT, 2):
            j = (ti - 1) // 2
            g.wait_ge(sm.ve, ti + 1)
            g.dma_start(out=vh[ti * 128:(ti + 1) * 128, :],
                        in_=vst_s[:, (ti % 4) * (H * 65):(ti % 4 + 1) * (H * 65)]
                        ).then_inc(gld[j % 2], 16)


    @blk.tensor
    def _(t):
        ins_ = _qk0_tg(t, 3)
        ins_ = _qk0_tg(t, 4)
        ins_.then_inc(sm.qkm, 1)
        for gi in range(1, 16):
            t.wait_ge(qklS[gi % 4], 16 * (gi // 4 + 1))
            t.wait_ge(sm.qke, gi * 3)
            t.wait_ge(sm.qkev, gi * 2)
            for k in range(KD):
                wch = wkr[:, (gi % 4) * (KD * 128) + k * 128:(gi % 4) * (KD * 128) + (k + 1) * 128]
                for tg in range(NTG):
                    dst = pS[:, TGO[tg]:TGO[tg] + TGS[tg]] if tg < 4 else pV[:, 0:128]
                    ins_ = t.matmul(dst, wch,
                                    xlnT.rearrange("p (a b) -> p a b", a=KD)[:, k, TGO[tg]:TGO[tg] + TGS[tg]],
                                    start=(k == 0), stop=(k == KD - 1))
            ins_.then_inc(sm.qkm, 1)
            if 2 <= gi <= 10:
                ti = gi - 2
                if ti == 0:
                    t.wait_ge(sm.xte, NT)
                t.wait_ge(sm.wqv, ti + 1)
                if ti >= 2:
                    t.wait_ge(sm.wqe, ti - 1)
                for j in range(8):
                    ins_ = t.transpose(pT[:, (ti % 2) * 1024 + j * 128:(ti % 2) * 1024 + (j + 1) * 128],
                                       wqq[:, ti * 1024 + j * 128:ti * 1024 + (j + 1) * 128],
                                       idb_s[:, :])
                ins_.then_inc(sm.wqt, 1)
        t.wait_ge(sm.cbv, 32)
        t.wait_ge(sm.qke, 16 * 3)
        t.wait_ge(sm.qkev, 16 * 2)
        for ti in range(NT):
            t.wait_ge(sm.wvl, 16 * KD)
            t.wait_ge(sm.ve, ti)
            t.wait_ge(sm.qkm, 16)
            for k in range(KD):
                for half in range(2):
                    t.matmul(pV[:, half * 512:(half + 1) * 512],
                             xlnT.rearrange("p (a b) -> p a b", a=KD)[:, k, ti * 128:(ti + 1) * 128],
                             wvr[:, k * D + half * 512:k * D + (half + 1) * 512],
                             start=(k == 0), stop=False)
            for half in range(2):
                ins_ = t.matmul(pV[:, half * 512:(half + 1) * 512],
                                onesb[:, 0:128], bv_s[:, half * 512:(half + 1) * 512],
                                start=False, stop=True)
            ins_.then_inc(sm.vm, 1)

    @blk.scalar
    def _(s):
        for gi in range(16):
            r = gi % 2
            if gi >= 2:
                s.wait_ge(qssS[gi % 2], 16 * ((gi - 2) // 2 + 1))
            s.wait_ge(sm.qkm, gi + 1)
            for tg in range(2, NTG):
                src = pS[:, TGO[tg]:TGO[tg] + TGS[tg]] if tg < 4 else pV[:, 0:128]
                s.activation(qkst[:, r * TP + TGO[tg]:r * TP + TGO[tg] + TGS[tg]],
                             src, AF.Identity, bias=bqkT_s[:, gi:gi + 1]).then_inc(sm.qke, 1)
            if 2 <= gi <= 10:
                ti = gi - 2
                s.wait_ge(sm.wqt, ti + 1)
                if ti >= 2:
                    s.wait_ge(agwS[ti % 2], 16 * ((ti - 2) // 2 + 1))
                s.activation(agst[:, (ti % 2) * 1024:(ti % 2 + 1) * 1024],
                             pT[:, (ti % 2) * 1024:(ti % 2 + 1) * 1024],
                             AF.Copy).then_inc(sm.wqe, 1)

    @blk.vector
    def _(v):
        for gi in range(16):
            r = gi % 2
            if gi >= 2:
                v.wait_ge(qssS[gi % 2], 16 * ((gi - 2) // 2 + 1))
            v.wait_ge(sm.qkm, gi + 1)
            for tg in range(2):
                v.tensor_scalar(qkst[:, r * TP + TGO[tg]:r * TP + TGO[tg] + TGS[tg]],
                                pS[:, TGO[tg]:TGO[tg] + TGS[tg]],
                                bqkT_s[:, gi:gi + 1], None, ALU.add)
                v.drain()
                v.engine_nop().then_inc(sm.qkev, 1)
            if gi < 9:
                ti = gi
                v.wait_ge(gld[ti % 2], 16 * (ti // 2 + 1))
                srcw = wqr[:, (ti % 2) * 1024:(ti % 2 + 1) * 1024]
                bm = wqs[:, (ti % 2) * 64:(ti % 2) * 64 + 32]
                iv = wqs[:, (ti % 2) * 64 + 32:(ti % 2) * 64 + 64]
                v.tensor_reduce(bm, srcw.rearrange("p (a b) -> p a b", b=32),
                                AX.X, ALU.max, apply_absolute_value=True)
                v.drain()
                v.tensor_scalar(bm, bm, 1.0 / 7.0, 1e-12, ALU.mult, ALU.max)
                v.drain()
                v.reciprocal(iv, bm)
                v.drain()
                v.tensor_tensor(kq1[:, (ti % 2) * 1024:(ti % 2 + 1) * 1024], srcw,
                                iv.rearrange("p a -> p a ()").broadcast_to([128, 32, 32]),
                                ALU.mult)
                v.drain()
                v.tensor_tensor(wqq[:, ti * 1024:(ti + 1) * 1024],
                                kq1[:, (ti % 2) * 1024:(ti % 2 + 1) * 1024],
                                bm.rearrange("p a -> p a ()").broadcast_to([128, 32, 32]),
                                ALU.mult)
                v.drain()
                v.engine_nop().then_inc(sm.wqv, 1)

    @blk.sync
    def _(sync):
        for gi in range(16):
            r = gi % 2
            sync.wait_ge(sm.qke, gi * 3 + 3)
            sync.wait_ge(sm.qkev, gi * 2 + 2)
            dst = qh if gi < KD else kh
            gl = gi % KD
            sync.dma_start(out=bass.AP(dst.tensor, gl * TP, [[KD * TP, 128], [1, TP]]),
                           in_=qkst[:, r * TP:(r + 1) * TP]).then_inc(qssS[gi % 2], 16)
        sync.wait_ge(sm.cst, 65)
        for ti in range(0, NT, 2):
            sync.wait_ge(sm.ve, ti + 1)
            sync.dma_start(out=vh[ti * 128:(ti + 1) * 128, :],
                           in_=vst_s[:, (ti % 4) * (H * 65):(ti % 4 + 1) * (H * 65)]).then_inc(vssS[ti % 4], 16)

    @blk.vector
    def _(v):
        VW = H * 65
        for ti in range(NT):
            r = ti % 4
            v.wait_ge(sm.vm, ti + 1)
            if ti >= 4:
                tp = ti - 4
                if tp % 2 == 0:
                    v.wait_ge(vssS[tp % 4], 16 * (tp // 4 + 1))
                else:
                    jp = (tp - 1) // 2
                    v.wait_ge(gld[jp % 2], 16 * (5 - jp % 2) + 16 * (jp // 2 + 1))
            v.tensor_copy(vst_s[:, r * VW:(r + 1) * VW].rearrange("p (h c) -> p h c", c=65)[:, :, 0:64],
                          pV[:, 0:1024].rearrange("p (h c) -> p h c", c=64))
            v.drain()
            v.engine_nop().then_inc(sm.ve, 1)

    for nm_, src_ in (("qh", qh), ("kh", kh), ("vh", vh)):
        if nm_ in dbg:
            @blk.sync
            def _(sync, nm_=nm_, src_=src_):
                sync.wait_ge(sm.qss0, 16 * 8)
                sync.wait_ge(sm.qss1, 16 * 8)
                sync.wait_ge(sm.vss0, 16 * 5)
                sync.wait_ge(sm.vss1, 16 * 4)
                sync.wait_ge(sm.vss2, 16 * 4)
                sync.wait_ge(sm.vss3, 16 * 4)
                sync.dma_start(out=dbgd[nm_], in_=src_).then_inc(sm.dbgs, 16)
            ndbg[0] += 16
    if "ag" in dbg:
        @blk.sync
        def _(sync):
            sync.wait_ge(sm.agc, 1)
            sync.dma_start(out=dbgd["ag"], in_=agout).then_inc(sm.dbgs, 16)
        ndbg[0] += 16

    r1.close()
    right0.close()
    right1 = ExitStack()
    eR = right1.enter_context
    sbR = lambda n_, s_, d_=BF16: eR(nc.sbuf_tensor(n_, s_, d_, side="right"))
    qring = sbR("qring", [128, 2 * KD * QB])
    kring = sbR("kring", [128, 2 * KD * KB])
    vring = sbR("vring", [128, 2 * H * 65])
    prb = sbR("prb", [128, 2 * H * QB])
    atm = sbR("atm", [128, 2 * H * 64])
    rcp = sbR("rcp", [128, 4 * 16], F32)

    left1 = ExitStack()
    xln2 = left1.enter_context(nc.sbuf_tensor("xln2", [128, NTR * D], BF16, side="left"))
    left2 = ExitStack()
    eL2 = left2.enter_context
    attnoT = eL2(nc.sbuf_tensor("attnoT", [128, KD * TQ], BF16, side="left"))
    wor = eL2(nc.sbuf_tensor("wor", [128, KD * D], BF16, side="left"))
    f3r = eL2(nc.sbuf_tensor("f3r", [128, 2 * D], F32, side="left"))
    x2r = eL2(nc.sbuf_tensor("x2r", [128, 2 * D], F32, side="left"))

    @blk.sync
    def _(sync):
        sync.wait_ge(sm.qss0, 16 * 8)
        sync.wait_ge(sm.qss1, 16 * 8)
        vm_seen = -1
        for b in range(NBLK):
            r = b % 2
            if b >= 2:
                sync.wait_ge(sm.qkb, b - 1)
                sync.wait_ge(sm.pvm, b - 1)
            m_ = min((QB * b + KB - 1) // 128, NT - 1)
            while vm_seen < m_:
                vm_seen += 1
                tv = vm_seen
                if tv % 2 == 0:
                    sync.wait_ge(vssS[tv % 4], 16 * (tv // 4 + 1))
                else:
                    jv = (tv - 1) // 2
                    sync.wait_ge(gld[jv % 2], 16 * (5 - jv % 2) + 16 * (jv // 2 + 1))
            sync.dma_start(out=qring[:, r * KD * QB:(r + 1) * KD * QB].rearrange("p (a b) -> p a b", a=KD),
                           in_=bass.AP(qh.tensor, QB * b + PAD,
                                       [[KD * TP, 128], [TP, KD], [1, QB]])).then_inc(abiS[b % 2], 16)
            sync.dma_start(out=kring[:, r * KD * KB:(r + 1) * KD * KB].rearrange("p (a b) -> p a b", a=KD),
                           in_=bass.AP(kh.tensor, QB * b,
                                       [[KD * TP, 128], [TP, KD], [1, KB]])).then_inc(abiS[b % 2], 16)
            sync.dma_start(out=vring[:, r * (H * 65):(r + 1) * (H * 65)],
                           in_=vh[QB * b:QB * b + KB, :]).then_inc(abiS[b % 2], 16)

    SC = [512 * (h % 3) + 96 * (h // 3) if h < 15 else 1536 for h in range(H)]
    PC = [480 * (h % 3) + 96 * (h // 3) if h < 15 else 1440 for h in range(H)]
    PVSEQ = "ABABABCABABABCAB"
    _pvslot = {}
    _cnt = {"A": 0, "B": 0, "C": 0}
    for _h, _rg in enumerate(PVSEQ):
        _pvslot[_h] = (_rg, _cnt[_rg])
        _cnt[_rg] += 1

    def _pvc(h):
        rg, sl = _pvslot[h]
        if rg == "A":
            return pV, 65 * sl
        if rg == "B":
            return pV, 512 + 65 * sl
        return pS, 1664 + 65 * sl
    STIX = {h: {"A": 0, "B": 7, "C": 14}[rg] + sl for h, (rg, sl) in _pvslot.items()}
    PVW = H * 65

    @blk.tensor
    def _(t):
        t.wait_ge(sm.qke, 16 * 3)
        t.wait_ge(sm.qkev, 16 * 2)
        for i in range(NBLK + 2):
            if i < NBLK:
                b = i
                r = b % 2
                t.wait_ge(abiS[r], 48 * (b // 2 + 1))
                if b >= 1:
                    t.wait_ge(sm.exps, b)
                for h in range(H):
                    pp = 64 * (h % 2)
                    g = h // 2
                    ins_ = t.matmul(pS[:, SC[h]:SC[h] + QB],
                                    kring[pp:pp + 64, r * KD * KB + g * KB:r * KD * KB + (g + 1) * KB],
                                    qring[pp:pp + 64, r * KD * QB + g * QB:r * KD * QB + (g + 1) * QB],
                                    start=True, stop=True)
                ins_.then_inc(sm.qkb, 1)
            if 1 <= i <= NBLK:
                b = i - 1
                r = b % 2
                t.wait_ge(sm.pnv, b + 1)
                if b >= 1:
                    t.wait_ge(sm.pve, b)
                for h in range(H):
                    pt_, pc_ = _pvc(h)
                    ins_ = t.matmul(pt_[0:QB, pc_:pc_ + 65],
                                    prb[:, r * (H * QB) + PC[h]:r * (H * QB) + PC[h] + QB],
                                    vring[:, r * PVW + h * 65:r * PVW + (h + 1) * 65],
                                    start=True, stop=True)
                ins_.then_inc(sm.pvm, 1)
            if i >= 2:
                b = i - 2
                r = b % 2
                t.wait_ge(sm.pve, b + 1)
                if b >= 1:
                    t.wait_ge(sm.pte, b)
                for h in range(H):
                    ac = r * (H * 64) + STIX[h] * 64
                    ins_ = t.transpose(pT[64 * (h % 2):64 * (h % 2) + 64, (h // 2) * QB:(h // 2 + 1) * QB],
                                       atm[0:QB, ac:ac + 64],
                                       idb_s[0:QB, 0:QB],
                                       tile_position=(0, 64 * (h % 2)))
                ins_.then_inc(sm.ptt, 1)

    @blk.scalar
    def _(s):
        for b in range(NBLK):
            r = b % 2
            s.wait_ge(sm.qkb, b + 1)
            if b >= 2:
                s.wait_ge(sm.pvm, b - 1)
            for bk in range(3):
                s.activation(prb[:, r * (H * QB) + 480 * bk:r * (H * QB) + 480 * (bk + 1)],
                             pS[:, 512 * bk:512 * bk + 480],
                             AF.Exp, scale=0.125, bias=cstv[:, 1:2])
            s.activation(prb[:, r * (H * QB) + 1440:r * (H * QB) + 1536],
                         pS[:, 1536:1632],
                         AF.Exp, scale=0.125, bias=cstv[:, 1:2]).then_inc(sm.exps, 1)

    @blk.vector
    def _(v):
        v.wait_ge(sm.cst, 65)
        for i in range(NBLK + 2):
            if i < NBLK:
                b = i
                r = b % 2
                v.wait_ge(sm.exps, b + 1)
                v.tensor_tensor(prb[:, r * (H * QB):(r + 1) * (H * QB)].rearrange("p (a b) -> p a b", a=H),
                                prb[:, r * (H * QB):(r + 1) * (H * QB)].rearrange("p (a b) -> p a b", a=H),
                                masks[:, b * QB:(b + 1) * QB].rearrange("p b -> p () b").broadcast_to([KB, H, QB]),
                                ALU.mult)
                v.drain()
                v.engine_nop().then_inc(sm.pnv, 1)
            if 1 <= i <= NBLK:
                b = i - 1
                r = b % 2
                v.wait_ge(sm.pvm, b + 1)
                sg_ = rcp[0:QB, 32 + r * 16:32 + r * 16 + 16]
                v.tensor_copy(sg_[:, 0:7].rearrange("p (h c) -> p h c", c=1),
                              pV[0:QB, 0:455].rearrange("p (h c) -> p h c", c=65)[:, :, 64:65])
                v.tensor_copy(sg_[:, 7:14].rearrange("p (h c) -> p h c", c=1),
                              pV[0:QB, 512:967].rearrange("p (h c) -> p h c", c=65)[:, :, 64:65])
                v.tensor_copy(sg_[:, 14:16].rearrange("p (h c) -> p h c", c=1),
                              pS[0:QB, 1664:1794].rearrange("p (h c) -> p h c", c=65)[:, :, 64:65])
                v.drain()
                v.reciprocal(rcp[0:QB, r * 16:r * 16 + 16], sg_)
                v.drain()
                ab = r * (H * 64)
                v.tensor_tensor(atm[0:QB, ab:ab + 448].rearrange("p (h c) -> p h c", c=64),
                                pV[0:QB, 0:455].rearrange("p (h c) -> p h c", c=65)[:, :, 0:64],
                                rcp[0:QB, r * 16:r * 16 + 7].rearrange("p h -> p h ()").broadcast_to([QB, 7, 64]),
                                ALU.mult)
                v.tensor_tensor(atm[0:QB, ab + 448:ab + 896].rearrange("p (h c) -> p h c", c=64),
                                pV[0:QB, 512:967].rearrange("p (h c) -> p h c", c=65)[:, :, 0:64],
                                rcp[0:QB, r * 16 + 7:r * 16 + 14].rearrange("p h -> p h ()").broadcast_to([QB, 7, 64]),
                                ALU.mult)
                v.tensor_tensor(atm[0:QB, ab + 896:ab + 1024].rearrange("p (h c) -> p h c", c=64),
                                pS[0:QB, 1664:1794].rearrange("p (h c) -> p h c", c=65)[:, :, 0:64],
                                rcp[0:QB, r * 16 + 14:r * 16 + 16].rearrange("p h -> p h ()").broadcast_to([QB, 2, 64]),
                                ALU.mult)
                v.drain()
                v.engine_nop().then_inc(sm.pve, 1)
            if i >= 2:
                b = i - 2
                v.wait_ge(sm.ptt, b + 1)
                v.tensor_copy(attnoT.rearrange("p (a b) -> p a b", a=KD)[:, :, b * QB:(b + 1) * QB],
                              pT[:, 0:768].rearrange("p (a b) -> p a b", a=KD))
                v.drain()
                v.engine_nop().then_inc(sm.pte, 1)

    if "attnoT" in dbg:
        @blk.sync
        def _(sync):
            sync.wait_ge(sm.pte, NBLK)
            sync.dma_start(out=dbgd["attnoT"], in_=attnoT[:, :]).then_inc(sm.dbgs, 16)
        ndbg[0] += 16

    @blk.gpsimd
    def _(g):
        g.wait_ge(sm.agw0, 16 * 5)
        g.wait_ge(sm.agw1, 16 * 4)
        for k in range(KD):
            g.dma_start(out=wor[:, k * D:(k + 1) * D],
                        in_=woin[k * 128:(k + 1) * 128, :]).then_inc(sm.wol, 16)
        g.wait_ge(sm.cst, 65)
        g.dma_start(out=ccw[0], in_=sscal[0:1, 0:2]).then_inc(sm.sp_g, 16)
        g.wait_ge(sm.sp_g, 16)
        g.wait_ge(sm.agw0, 16 * 5)
        g.wait_ge(sm.agw1, 16 * 4)
        g.wait_ge(sm.qkb, 16)
        g.collective_compute("AllReduce", ALU.max,
                             replica_groups=[list(range(NCORE))],
                             ins=[ccw[0]], outs=[ccw[1]]).then_inc(sm.sp_c, 1)
        g.collective_compute("AllGather", ALU.bypass,
                             replica_groups=[list(range(NCORE))],
                             ins=[agin], outs=[agout]).then_inc(sm.agc, 1)

    @blk.sync
    def _(sync):
        for ti in range(NTR):
            if ti >= 2:
                sync.wait_ge(sm.x2a, ti - 1)
            sync.dma_start(out=xring[:, (ti % 2) * D:(ti % 2 + 1) * D],
                           in_=xp[PAD + ti * 128:PAD + (ti + 1) * 128, :]).then_inc(xrdS[ti % 2], 16)
            if ti >= 1:
                sync.wait_ge(sm.x2a, ti)
                sync.dma_start(out=x2h[(ti - 1) * 128:ti * 128, :],
                               in_=x2r[:, ((ti - 1) % 2) * D:((ti - 1) % 2 + 1) * D]).then_inc(x2wS[(ti - 1) % 2], 16)
        sync.wait_ge(sm.x2a, NTR)
        sync.dma_start(out=x2h[(NTR - 1) * 128:NTR * 128, :],
                       in_=x2r[:, ((NTR - 1) % 2) * D:((NTR - 1) % 2 + 1) * D]).then_inc(x2wS[(NTR - 1) % 2], 16)

    @blk.tensor
    def _(t):
        t.wait_ge(sm.wol, 16 * KD)
        t.wait_ge(sm.pte, NBLK)
        t.wait_ge(sm.pve, NBLK)
        t.wait_ge(sm.exps, NBLK)
        for ti in range(NTR):
            r = ti % 2
            if ti >= 2:
                t.wait_ge(sm.l3v1, ti - 1)
                t.wait_ge(sm.l3s2, ti - 1)
            for k in range(KD):
                for half in range(2):
                    t.matmul(pS[:, r * 1024 + half * 512:r * 1024 + (half + 1) * 512],
                             attnoT.rearrange("p (a b) -> p a b", a=KD)[:, k, ti * 128:(ti + 1) * 128],
                             wor[:, k * D + half * 512:k * D + (half + 1) * 512],
                             start=(k == 0), stop=False)
            for half in range(2):
                ins_ = t.matmul(pS[:, r * 1024 + half * 512:r * 1024 + (half + 1) * 512],
                                onesb[:, 0:128], bo_s[:, half * 512:(half + 1) * 512],
                                start=False, stop=True)
            ins_.then_inc(sm.wom, 1)

    @blk.vector
    def _(v):
        for t in range(NTR + 2):
            if t < NTR:
                r = t % 2
                st = stats[:, r * 14:r * 14 + 14]
                v.wait_ge(sm.wom, t + 1)
                if t >= 2:
                    v.wait_ge(sm.l3v2, t - 1)
                srcp = pS[:, r * 1024:(r + 1) * 1024]
                v.bn_stats(st[:, 0:6], srcp[:, 0:512])
                v.bn_stats(st[:, 6:12], srcp[:, 512:1024])
                v.drain()
                v.bn_aggr(st[:, 12:14], st[:, 0:12])
                v.drain()
                v.engine_nop().then_inc(sm.l3v1, 1)
            if 1 <= t <= NTR:
                u = t - 1
                r = u % 2
                st = stats[:, r * 14:r * 14 + 14]
                st2 = stats[:, (2 + r) * 14:(2 + r) * 14 + 14]
                v.wait_ge(sm.l3s1, u + 1)
                v.scalar_tensor_tensor(lncol[:, r * 6 + 1:r * 6 + 2], st[:, 12:13],
                                       -1.0, lncol[:, r * 6:r * 6 + 1], ALU.mult, ALU.mult)
                v.drain()
                v.engine_nop().then_inc(sm.l3v2, 1)
                v.wait_ge(sm.l3s2, u + 1)
                v.wait_ge(xrdS[r], 16 * (u // 2 + 1))
                if u >= 2:
                    v.wait_ge(x2wS[r], 16 * ((u - 2) // 2 + 1))
                    v.wait_ge(sm.l2s2, u - 1)
                v.tensor_tensor(x2r[:, r * D:(r + 1) * D], f3r[:, r * D:(r + 1) * D],
                                xring[:, r * D:(r + 1) * D], ALU.add)
                v.drain()
                v.engine_nop().then_inc(sm.x2a, 1)
                v.bn_stats(st2[:, 0:6], x2r[:, r * D:r * D + 512])
                v.bn_stats(st2[:, 6:12], x2r[:, r * D + 512:r * D + 1024])
                v.drain()
                v.bn_aggr(st2[:, 12:14], st2[:, 0:12])
                v.drain()
                v.engine_nop().then_inc(sm.l2v1, 1)
            if t >= 2:
                u = t - 2
                r = u % 2
                st2 = stats[:, (2 + r) * 14:(2 + r) * 14 + 14]
                v.wait_ge(sm.l2s1, u + 1)
                v.scalar_tensor_tensor(lncol[:, r * 6 + 4:r * 6 + 5], st2[:, 12:13],
                                       -1.0, lncol[:, r * 6 + 3:r * 6 + 4], ALU.mult, ALU.mult)
                v.drain()
                v.engine_nop().then_inc(sm.l2v2, 1)
                v.wait_ge(sm.l2s2, u + 1)
                v.tensor_reduce(gcol[:, 0:1], xln2[:, u * D:(u + 1) * D], AX.X,
                                ALU.max, apply_absolute_value=True)
                v.drain()
                v.tensor_tensor(macc[:, M_X:M_X + 1], macc[:, M_X:M_X + 1],
                                gcol[:, 0:1], ALU.max)
                v.drain()
                v.engine_nop().then_inc(sm.mxv, 1)

    @blk.scalar
    def _(s):
        for t in range(NTR + 2):
            if t < NTR:
                r = t % 2
                st = stats[:, r * 14:r * 14 + 14]
                s.wait_ge(sm.l3v1, t + 1)
                s.activation(lncol[:, r * 6 + 2:r * 6 + 3], st[:, 13:14], AF.Ln, bias=cstv[:, 0:1])
                s.drain()
                s.activation(lncol[:, r * 6:r * 6 + 1], lncol[:, r * 6 + 2:r * 6 + 3],
                             AF.Exp, scale=-0.5).then_inc(sm.l3s1, 1)
            if 1 <= t <= NTR:
                u = t - 1
                r = u % 2
                st2 = stats[:, (2 + r) * 14:(2 + r) * 14 + 14]
                s.wait_ge(sm.l3v2, u + 1)
                if u >= 2:
                    s.wait_ge(sm.x2a, u - 1)
                s.activation(f3r[:, r * D:(r + 1) * D], pS[:, r * 1024:(r + 1) * 1024],
                             AF.Identity, bias=lncol[:, r * 6 + 1:r * 6 + 2],
                             scale=lncol[:, r * 6:r * 6 + 1]).then_inc(sm.l3s2, 1)
                s.wait_ge(sm.l2v1, u + 1)
                s.activation(lncol[:, r * 6 + 5:r * 6 + 6], st2[:, 13:14], AF.Ln, bias=cstv[:, 0:1])
                s.drain()
                s.activation(lncol[:, r * 6 + 3:r * 6 + 4], lncol[:, r * 6 + 5:r * 6 + 6],
                             AF.Exp, scale=-0.5).then_inc(sm.l2s1, 1)
            if t >= 2:
                u = t - 2
                r = u % 2
                s.wait_ge(sm.l2v2, u + 1)
                s.activation(xln2[:, u * D:(u + 1) * D], x2r[:, r * D:(r + 1) * D],
                             AF.Identity, bias=lncol[:, r * 6 + 4:r * 6 + 5],
                             scale=lncol[:, r * 6 + 3:r * 6 + 4]).then_inc(sm.l2s2, 1)

    if "x2" in dbg:
        @blk.sync
        def _(sync):
            sync.wait_ge(sm.x2w0, 16 * 8)
            sync.wait_ge(sm.x2w1, 16 * 8)
            sync.dma_start(out=dbgd["x2"], in_=x2h).then_inc(sm.dbgs, 16)
        ndbg[0] += 16

    spc = dict(t1=0, v1=0, g=16, c=1, v2=0, t2=0, v3=0)

    def sp_t1_emit(t, pairs):
        for i, (mc, _, _) in enumerate(pairs):
            ins_ = t.transpose(pV[0:1, 512 + i * 128:512 + (i + 1) * 128],
                               macc[:, mc:mc + 1], idf_s[:, :])
        spc["t1"] += 1
        return ins_

    def sp_t2_emit(t, pairs):
        lo = min(min(sc_, ic_) for _, sc_, ic_ in pairs)
        hi = max(max(sc_, ic_) for _, sc_, ic_ in pairs) + 1
        ins_ = t.matmul(pV[:, 512 + lo:512 + hi], onesf[:, :], sscal[0:1, lo:hi],
                        start=True, stop=True)
        spc["t2"] += 1
        return ins_

    def sync_point(spi, pairs, wait_sem=None, wait_val=None, t1val=None, t2val=None):
        n = len(pairs)
        cin, cout = cc[spi]
        inline_pe = t1val is not None
        if not inline_pe:
            @blk.tensor
            def _(t):
                t.wait_ge(wait_sem, wait_val)
                ins_ = sp_t1_emit(t, pairs)
                ins_.then_inc(sm.sp_t1, 1)
            t1val = spc["t1"]

        @blk.vector
        def _(v):
            v.wait_ge(sm.sp_t1, t1val)
            for i in range(n):
                v.tensor_reduce(sscal[0:1, 12 + i:13 + i],
                                pV[0:1, 512 + i * 128:512 + (i + 1) * 128],
                                AX.X, ALU.max)
            v.drain()
            v.engine_nop().then_inc(sm.sp_v1, 1)
        spc["v1"] += 1

        @blk.gpsimd
        def _(g):
            g.wait_ge(sm.sp_v1, spc["v1"])
            g.dma_start(out=cin, in_=sscal[0:1, 12:14]).then_inc(sm.sp_g, 16)
            g.wait_ge(sm.sp_g, spc["g"] + 16)
            g.collective_compute("AllReduce", ALU.max,
                                 replica_groups=[list(range(NCORE))],
                                 ins=[cin], outs=[cout]).then_inc(sm.sp_c, 1)
            g.wait_ge(sm.sp_c, spc["c"] + 1)
            g.dma_start(out=sscal[0:1, 12:14], in_=cout).then_inc(sm.sp_g, 16)
        spc["g"] += 32
        spc["c"] += 1

        @blk.vector
        def _(v):
            v.wait_ge(sm.sp_g, spc["g"])
            for i, (_, sc_, ic_) in enumerate(pairs):
                v.tensor_scalar(sscal[0:1, sc_:sc_ + 1], sscal[0:1, 12 + i:13 + i],
                                1.0 / QA, 1e-12, ALU.mult, ALU.max)
                v.drain()
                v.reciprocal(sscal[0:1, ic_:ic_ + 1], sscal[0:1, sc_:sc_ + 1])
                v.drain()
            v.engine_nop().then_inc(sm.sp_v2, 1)
        spc["v2"] += 1

        if not inline_pe:
            @blk.tensor
            def _(t):
                t.wait_ge(sm.sp_v2, spc["v2"])
                ins_ = sp_t2_emit(t, pairs)
                ins_.then_inc(sm.sp_t2, 1)
            t2val = spc["t2"]

        @blk.vector
        def _(v):
            v.wait_ge(sm.sp_t2, t2val)
            lo = min(min(sc_, ic_) for _, sc_, ic_ in pairs)
            hi = max(max(sc_, ic_) for _, sc_, ic_ in pairs) + 1
            v.tensor_copy(svec[:, lo:hi], pV[:, 512 + lo:512 + hi])
            v.drain()
            v.engine_nop().then_inc(sm.sp_v3, 1)
        spc["v3"] += 1

    sync_point(0, [(M_X, C_SX, C_ISX)], sm.mxv, NTR)

    mark("P0")
    right1.close()
    left2.close()
    right2 = ExitStack()
    xqT = right2.enter_context(nc.sbuf_tensor("xqT", [128, KD * T], BF16, side="right"))
    kq3 = right2.enter_context(nc.sbuf_tensor("kq3", [128, 1024], I16, side="right"))
    xqst = right2.enter_context(nc.sbuf_tensor("xqst", [128, 2 * D], BF16, side="right"))

    def _xq_evac(v, ti):
        r = ti % 2
        v.wait_ge(sm.xqt, ti + 1)
        v.tensor_copy(xqT.rearrange("p (a b) -> p a b", a=KD)[:, :, ti * 128:(ti + 1) * 128],
                      pT[:, r * 1024:(r + 1) * 1024].rearrange("p (a b) -> p a b", a=KD))
        v.drain()
        v.engine_nop().then_inc(sm.xqe, 1)

    @blk.vector
    def _(v):
        v.wait_ge(sm.sp_v3, spc["v3"])
        for ti in range(NTR):
            if ti >= 1:
                v.wait_ge(sm.xqq, ti)
            v.tensor_scalar(kq3[:, :], xln2[:, ti * D:(ti + 1) * D],
                            svec[:, C_ISX:C_ISX + 1], None, ALU.mult)
            v.drain()
            v.engine_nop().then_inc(sm.xqk, 1)
            if ti >= 1:
                _xq_evac(v, ti - 1)
        _xq_evac(v, NTR - 1)

    @blk.scalar
    def _(s):
        for ti in range(NTR):
            r = ti % 2
            s.wait_ge(sm.xqk, ti + 1)
            if ti >= 2:
                s.wait_ge(sm.xqt, ti - 1)
            s.activation(xqst[:, r * D:(r + 1) * D], kq3[:, :],
                         AF.Copy, scale=svec[:, C_SX:C_SX + 1]).then_inc(sm.xqq, 1)

    @blk.tensor
    def _(t):
        for ti in range(NTR):
            r = ti % 2
            t.wait_ge(sm.xqq, ti + 1)
            if ti >= 2:
                t.wait_ge(sm.xqe, ti - 1)
            for j in range(8):
                ins_ = t.transpose(pT[:, r * 1024 + j * 128:r * 1024 + (j + 1) * 128],
                                   xqst[:, r * D + j * 128:r * D + (j + 1) * 128],
                                   idb_s[:, :])
            ins_.then_inc(sm.xqt, 1)

    if "xqT" in dbg:
        @blk.sync
        def _(sync):
            sync.wait_ge(sm.xqe, NTR)
            sync.dma_start(out=dbgd["xqT"], in_=xqT[:, :]).then_inc(sm.dbgs, 16)
        ndbg[0] += 16

    mark("P1P2S1")
    left1.close()
    left4 = ExitStack()
    eL4 = left4.enter_context
    arena = eL4(nc.sbuf_tensor("arena", [128, NARE * T], BF16, side="left"))
    kscr = eL4(nc.sbuf_tensor("kscr", [128, 2 * T], I16, side="left"))
    kscr3 = eL4(nc.sbuf_tensor("kscr3", [128, 2 * T], I16, side="left"))
    left4b = ExitStack()
    eL4b = left4b.enter_context
    kscru = eL4b(nc.sbuf_tensor("kscru", [128, 2 * T], U16, side="left"))
    upsb = eL4b(nc.sbuf_tensor("upsb", [128, 2 * T], BF16, side="left"))
    sscr = eL4b(nc.sbuf_tensor("sscr", [128, 2 * T], F32, side="left"))
    wgr = eL4b(nc.sbuf_tensor("wgr", [128, 4 * 1024], BF16, side="left"))

    PS5 = lambda i: (i + NARE - 1) % NARE
    HT = T // 2
    @blk.sync
    def _(sync):
        sync.wait_ge(sm.agc, 1)
        for f in range(48):
            if f >= 4:
                sync.wait_ge(sm.gm, f - 3)
            fl = f % NFC
            s_ = fl // 3
            a_ = fl % 3
            tile = (3 + a_) if f < NFC else a_
            sync.dma_start(out=wgr[:, (f % 4) * 1024:(f % 4 + 1) * 1024],
                           in_=bass.AP(agout.tensor, s_ * AGW + tile * 1024,
                                       [[AGCOLS, 128], [1, 1024]])).then_inc(wglS[f % 4], 16)

    S2A_PAIRS = [(M_G, C_SG, C_ISG)]
    S2B_PAIRS = [(M_U, C_SU, C_ISU), (M_G2, C_SG2, C_ISG2)]
    s2a_v2v = spc["v2"] + 1
    s2b_v2v = spc["v2"] + 2
    s2v = {}

    @blk.tensor
    def _(t):
        t.wait_ge(sm.xqe, NTR)
        for f in range(48):
            t.wait_ge(wglS[f % 4], 16 * (f // 4 + 1))
            for tg in range(4):
                if f >= 1:
                    t.wait_ge(sm.gev, (f - 1) * 4 + tg + 1)
                for k in range(KD):
                    ins_ = t.matmul(pS[:, tg * 512:(tg + 1) * 512],
                                    wgr[:, (f % 4) * 1024 + k * 128:(f % 4) * 1024 + (k + 1) * 128],
                                    xqT.rearrange("p (a b) -> p a b", a=KD)[:, k, tg * 512:(tg + 1) * 512],
                                    start=(k == 0), stop=(k == KD - 1))
            ins_.then_inc(sm.gm, 1)
            if f == NFC - 1:
                t.wait_ge(sm.gmx, NFC)
                sp_t1_emit(t, S2A_PAIRS).then_inc(sm.sp_t1, 1)
                s2v["a_t1"] = spc["t1"]
            elif f == NFC + 2:
                t.wait_ge(sm.sp_v2, s2a_v2v)
                sp_t2_emit(t, S2A_PAIRS).then_inc(sm.sp_t2, 1)
                s2v["a_t2"] = spc["t2"]
        t.wait_ge(sm.gmx, 48)
        t.wait_ge(sm.bv2, NFC)
        sp_t1_emit(t, S2B_PAIRS).then_inc(sm.sp_t1, 1)
        s2v["b_t1"] = spc["t1"]
        t.wait_ge(sm.sp_v2, s2b_v2v)
        sp_t2_emit(t, S2B_PAIRS).then_inc(sm.sp_t2, 1)
        s2v["b_t2"] = spc["t2"]

    @blk.scalar
    def _(s):
        s.wait_ge(sm.x2w0, 16 * 8)
        s.wait_ge(sm.x2w1, 16 * 8)
        for f in range(NFC):
            s.wait_ge(sm.gm, f + 1)
            for tg in range(4):
                s.activation(arena[:, f * T + tg * 512:f * T + (tg + 1) * 512],
                             pS[:, tg * 512:(tg + 1) * 512],
                             AF.Copy).then_inc(sm.gev, 1)

    @blk.vector
    def _(v):
        for f in range(NFC):
            v.wait_ge(sm.gev, f * 4 + 4)
            v.tensor_reduce(gcol[:, 1:2], arena[:, f * T:(f + 1) * T], AX.X,
                            ALU.max, apply_absolute_value=True)
            v.drain()
            v.tensor_tensor(macc[:, M_G:M_G + 1], macc[:, M_G:M_G + 1],
                            gcol[:, 1:2], ALU.max)
            v.drain()
            v.engine_nop().then_inc(sm.gmx, 1)

    sync_point(1, S2A_PAIRS, t1val=s2v["a_t1"], t2val=s2v["a_t2"])
    v3_s2a = spc["v3"]

    @blk.gpsimd
    def _(g):
        for fu in range(NFC):
            g.wait_ge(sm.gmx, NFC + fu + 1)
            g.dma_start(out=bass.AP(uph.tensor, fu * 128 * T, [[T, 128], [1, T]]),
                        in_=upsb[:, (fu % 2) * T:(fu % 2 + 1) * T]).then_inc(upwS[fu % 2], 16)

    mark("P3P4")
    @blk.scalar
    def _(s):
        for jj in range(NFC + 1):
            if jj < NFC:
                j = jj
                s.wait_ge(sm.gm, NFC + j + 1)
                if j >= 2:
                    s.wait_ge(upwS[j % 2], 16 * ((j - 2) // 2 + 1))
                for tg in range(4):
                    s.activation(upsb[:, (j % 2) * T + tg * 512:(j % 2) * T + (tg + 1) * 512],
                                 pS[:, tg * 512:(tg + 1) * 512],
                                 AF.Copy).then_inc(sm.gev, 1)
            if jj == 0:
                continue
            j = jj - 1
            if j == 0:
                s.wait_ge(sm.sp_v3, v3_s2a)
            s.wait_ge(sm.bv1, j + 1)
            if j >= 2:
                s.wait_ge(sm.bv2, j - 1)
            s.activation(arena[:, j * T:(j + 1) * T], kscr[:, (j % 2) * T:(j % 2 + 1) * T],
                         AF.Copy, scale=svec[:, C_SG:C_SG + 1])
            s.drain()
            s.activation(sscr[:, (j % 2) * T:(j % 2 + 1) * T], arena[:, j * T:(j + 1) * T],
                         AF.Sigmoid)
            s.drain()
            s.activation(kscru[:, (j % 2) * T:(j % 2 + 1) * T],
                         sscr[:, (j % 2) * T:(j % 2 + 1) * T],
                         AF.Copy, scale=65536.0).then_inc(sm.bs1, 1)

    @blk.vector
    def _(v):
        for jj in range(NFC + 1):
            if jj < NFC:
                j = jj
                v.wait_ge(sm.gev, (NFC + j) * 4 + 4)
                v.tensor_reduce(gcol[:, 1:2], upsb[:, (j % 2) * T:(j % 2 + 1) * T],
                                AX.X, ALU.max, apply_absolute_value=True)
                v.drain()
                v.tensor_tensor(macc[:, M_U:M_U + 1], macc[:, M_U:M_U + 1],
                                gcol[:, 1:2], ALU.max)
                v.drain()
                v.engine_nop().then_inc(sm.gmx, 1)
                if j == 0:
                    v.wait_ge(sm.sp_v3, v3_s2a)
                if j >= 2:
                    v.wait_ge(sm.bs1, j - 1)
                v.tensor_scalar(kscr[:, (j % 2) * T:(j % 2 + 1) * T],
                                arena[:, j * T:(j + 1) * T],
                                svec[:, C_ISG:C_ISG + 1], None, ALU.mult)
                v.drain()
                v.engine_nop().then_inc(sm.bv1, 1)
            if jj == 0:
                continue
            j = jj - 1
            v.wait_ge(sm.bs1, j + 1)
            sl = PS5(j)
            v.scalar_tensor_tensor(arena[:, sl * T:(sl + 1) * T],
                                   kscru[:, (j % 2) * T:(j % 2 + 1) * T], 1.0 / 65536.0,
                                   arena[:, j * T:(j + 1) * T], ALU.mult, ALU.mult)
            v.drain()
            v.tensor_reduce(gcol[:, 1:2], arena[:, sl * T:(sl + 1) * T], AX.X,
                            ALU.max, apply_absolute_value=True)
            v.drain()
            v.tensor_tensor(macc[:, M_G2:M_G2 + 1], macc[:, M_G2:M_G2 + 1],
                            gcol[:, 1:2], ALU.max)
            v.drain()
            v.engine_nop().then_inc(sm.bv2, 1)

    if "arena" in dbg:
        @blk.sync
        def _(sync):
            sync.wait_ge(sm.gmx, 48)
            sync.dma_start(out=dbgd["arena"], in_=arena[:, :]).then_inc(sm.dbgs, 16)
        ndbg[0] += 16
    if "uph" in dbg:
        @blk.sync
        def _(sync):
            sync.wait_ge(sm.upw0, 16 * 12)
            sync.wait_ge(sm.upw1, 16 * 12)
            sync.dma_start(out=dbgd["uph"], in_=uph).then_inc(sm.dbgs, 16)
        ndbg[0] += 16

    if "parena" in dbg:
        @blk.sync
        def _(sync):
            sync.wait_ge(sm.bv2, NFC)
            sync.dma_start(out=dbgd["parena"], in_=arena[:, :]).then_inc(sm.dbgs, 16)
        ndbg[0] += 16

    sync_point(2, S2B_PAIRS, t1val=s2v["b_t1"], t2val=s2v["b_t2"])
    v3_s2b = spc["v3"]

    mark("P5")
    @blk.sync
    def _(sync):
        sync.wait_ge(sm.upw0, 16 * 12)
        sync.wait_ge(sm.upw1, 16 * 12)
        for i in range(NFC):
            if i >= 2:
                sync.wait_ge(sm.cv2, i - 1)
            sync.dma_start(out=upsb[:, (i % 2) * T:(i % 2 + 1) * T],
                           in_=bass.AP(uph.tensor, i * 128 * T, [[T, 128], [1, T]])
                           ).then_inc(uprdS[i % 2], 16)

    @blk.vector
    def _(v):
        for jj in range(NFC + 1):
            if jj < NFC:
                i = jj
                sl = PS5(i)
                v.tensor_scalar(kscr[:, (i % 2) * T:(i % 2) * T + HT],
                                arena[:, sl * T:sl * T + HT],
                                svec[:, C_ISG2:C_ISG2 + 1], None, ALU.mult)
                v.drain()
            if jj == 0:
                continue
            i = jj - 1
            r = i % 2
            sl = PS5(i)
            v.wait_ge(sm.cs1, i + 1)
            v.scalar_tensor_tensor(arena[:, sl * T:(sl + 1) * T],
                                   kscr[:, r * T:(r + 1) * T], svec[:, C_SG2:C_SG2 + 1],
                                   upsb[:, r * T:(r + 1) * T], ALU.mult, ALU.mult)
            v.drain()
            v.tensor_reduce(gcol[:, 1:2], arena[:, sl * T:(sl + 1) * T], AX.X,
                            ALU.max, apply_absolute_value=True)
            v.drain()
            v.tensor_tensor(macc[:, M_O:M_O + 1], macc[:, M_O:M_O + 1],
                            gcol[:, 1:2], ALU.max)
            v.drain()
            v.engine_nop().then_inc(sm.cv2, 1)

    @blk.scalar
    def _(s):
        s.wait_ge(sm.sp_v3, v3_s2b)
        for i in range(NFC):
            r = i % 2
            if i >= 2:
                s.wait_ge(sm.cv2, i - 1)
            sl = PS5(i)
            s.activation(kscr[:, r * T + HT:(r + 1) * T], arena[:, sl * T + HT:(sl + 1) * T],
                         AF.Copy, scale=svec[:, C_ISG2:C_ISG2 + 1])
            s.drain()
            s.wait_ge(uprdS[r], 16 * (i // 2 + 1))
            s.activation(kscr3[:, r * T:(r + 1) * T], upsb[:, r * T:(r + 1) * T],
                         AF.Copy, scale=svec[:, C_ISU:C_ISU + 1])
            s.drain()
            s.activation(upsb[:, r * T:(r + 1) * T], kscr3[:, r * T:(r + 1) * T],
                         AF.Copy, scale=svec[:, C_SU:C_SU + 1]).then_inc(sm.cs1, 1)

    if "oarena" in dbg:
        @blk.sync
        def _(sync):
            sync.wait_ge(sm.cv2, NFC)
            sync.dma_start(out=dbgd["oarena"], in_=arena[:, :]).then_inc(sm.dbgs, 16)
        ndbg[0] += 16

    sync_point(3, [(M_O, C_SO, C_ISO)], sm.cv2, NFC)

    if "sc" in dbg:
        @blk.sync
        def _(sync):
            sync.wait_ge(sm.sp_v3, spc["v3"])
            sync.dma_start(out=dbgd["sc"], in_=svec[:, :]).then_inc(sm.dbgs, 16)
        ndbg[0] += 16

    mark("P6")
    right2.close()
    left4b.close()
    right3 = ExitStack()
    wdn = right3.enter_context(nc.sbuf_tensor("wdn", [128, NFC * D], BF16, side="right"))
    left5 = ExitStack()
    eL5 = left5.enter_context
    f4r = eL5(nc.sbuf_tensor("f4r", [128, 2 * D], F32, side="left"))
    x2rr = eL5(nc.sbuf_tensor("x2rr", [128, 2 * D], F32, side="left"))
    orng = eL5(nc.sbuf_tensor("orng", [128, 2 * D], F32, side="left"))

    @blk.gpsimd
    def _(g):
        g.wait_ge(sm.agc, 1)
        g.wait_ge(sm.gm, 48)
        for i in range(NFC):
            g.dma_start(out=wdn[:, i * D:(i + 1) * D].rearrange("p (a b) -> p a b", a=8),
                        in_=bass.AP(agout.tensor, (48 + i) * 128,
                                    [[AGCOLS, 128], [AGW, 8], [1, 128]])
                        ).then_inc(sm.wdl, 16)


    @blk.vector
    def _(v):
        v.wait_ge(sm.sp_v3, spc["v3"])
        for i in range(NFC):
            r = i % 2
            sl = PS5(i)
            v.tensor_scalar(kscr[:, r * T:r * T + HT], arena[:, sl * T:sl * T + HT],
                            svec[:, C_ISO:C_ISO + 1], None, ALU.mult)
            v.drain()
            v.tensor_scalar(arena[:, sl * T:sl * T + HT], kscr[:, r * T:r * T + HT],
                            svec[:, C_SO:C_SO + 1], None, ALU.mult)
            v.drain()
            v.engine_nop().then_inc(sm.ov2, 1)

    @blk.scalar
    def _(s):
        s.wait_ge(sm.sp_v3, spc["v3"])
        for i in range(NFC):
            r = i % 2
            sl = PS5(i)
            s.activation(kscr3[:, r * T:r * T + HT], arena[:, sl * T + HT:(sl + 1) * T],
                         AF.Copy, scale=svec[:, C_ISO:C_ISO + 1])
            s.drain()
            s.activation(arena[:, sl * T + HT:(sl + 1) * T], kscr3[:, r * T:r * T + HT],
                         AF.Copy, scale=svec[:, C_SO:C_SO + 1]).then_inc(sm.os1, 1)

    @blk.tensor
    def _(t):
        t.wait_ge(sm.wdl, 16 * NFC)
        for ti in range(NTR):
            r = ti % 2
            if ti >= 2:
                t.wait_ge(sm.dlv1, ti - 1)
                t.wait_ge(sm.dls2, ti - 1)
            for i in range(NFC):
                if ti == 0:
                    t.wait_ge(sm.os1, i + 1)
                    t.wait_ge(sm.ov2, i + 1)
                sl = PS5(i)
                for half in range(2):
                    ins_ = t.matmul(pS[:, r * 1024 + half * 512:r * 1024 + (half + 1) * 512],
                                    arena[:, sl * T + ti * 128:sl * T + (ti + 1) * 128],
                                    wdn[:, i * D + half * 512:i * D + (half + 1) * 512],
                                    start=(i == 0), stop=(i == NFC - 1))
            ins_.then_inc(sm.dm, 1)

    @blk.sync
    def _(sync):
        sync.wait_ge(sm.cv2, NFC)
        for ti in range(NTR):
            if ti >= 2:
                sync.wait_ge(sm.oad, ti - 1)
            sync.dma_start(out=x2rr[:, (ti % 2) * D:(ti % 2 + 1) * D],
                           in_=x2h[ti * 128:(ti + 1) * 128, :]).then_inc(x2rdS[ti % 2], 16)
            if ti >= 1:
                sync.wait_ge(sm.oad, ti)
                sync.dma_start(out=out[(ti - 1) * 128:ti * 128, :],
                               in_=orng[:, ((ti - 1) % 2) * D:((ti - 1) % 2 + 1) * D]).then_inc(ostS[(ti - 1) % 2], 16)
        sync.wait_ge(sm.oad, NTR)
        sync.dma_start(out=out[(NTR - 1) * 128:NTR * 128, :],
                       in_=orng[:, ((NTR - 1) % 2) * D:((NTR - 1) % 2 + 1) * D]).then_inc(ostS[(NTR - 1) % 2], 16)
        sync.wait_ge(sm.ost0, 16 * 8)
        sync.wait_ge(sm.ost1, 16 * 8)
        if ndbg[0]:
            sync.wait_ge(sm.dbgs, ndbg[0])

    @blk.vector
    def _(v):
        for ti in range(NTR):
            r = ti % 2
            st = stats[:, r * 14:r * 14 + 14]
            v.wait_ge(sm.dm, ti + 1)
            src = pS[:, r * 1024:(r + 1) * 1024]
            v.bn_stats(st[:, 0:6], src[:, 0:512])
            v.bn_stats(st[:, 6:12], src[:, 512:1024])
            v.drain()
            v.bn_aggr(st[:, 12:14], st[:, 0:12])
            v.drain()
            v.engine_nop().then_inc(sm.dlv1, 1)
            v.wait_ge(sm.dls1, ti + 1)
            v.scalar_tensor_tensor(lncol[:, r * 6 + 1:r * 6 + 2], st[:, 12:13],
                                   -1.0, lncol[:, r * 6:r * 6 + 1], ALU.mult, ALU.mult)
            v.drain()
            v.engine_nop().then_inc(sm.dlv2, 1)
            v.wait_ge(sm.dls2, ti + 1)
            v.wait_ge(x2rdS[ti % 2], 16 * (ti // 2 + 1))
            if ti >= 2:
                v.wait_ge(ostS[ti % 2], 16 * ((ti - 2) // 2 + 1))
            v.tensor_tensor(orng[:, r * D:(r + 1) * D], f4r[:, r * D:(r + 1) * D],
                            x2rr[:, r * D:(r + 1) * D], ALU.add)
            v.drain()
            v.engine_nop().then_inc(sm.oad, 1)

    @blk.scalar
    def _(s):
        for ti in range(NTR):
            r = ti % 2
            st = stats[:, r * 14:r * 14 + 14]
            s.wait_ge(sm.dlv1, ti + 1)
            s.activation(lncol[:, r * 6 + 2:r * 6 + 3], st[:, 13:14], AF.Ln, bias=cstv[:, 0:1])
            s.drain()
            s.activation(lncol[:, r * 6:r * 6 + 1], lncol[:, r * 6 + 2:r * 6 + 3],
                         AF.Exp, scale=-0.5).then_inc(sm.dls1, 1)
            s.wait_ge(sm.dlv2, ti + 1)
            if ti >= 2:
                s.wait_ge(sm.oad, ti - 1)
            s.activation(f4r[:, r * D:(r + 1) * D], pS[:, r * 1024:(r + 1) * 1024],
                         AF.Identity, bias=lncol[:, r * 6 + 1:r * 6 + 2],
                         scale=lncol[:, r * 6:r * 6 + 1]).then_inc(sm.dls2, 1)

    mark("P7")
    left5.close()
    left4.close()
    right3.close()
    top.close()
    return nc, dbgd


def _host_prepare(L0_attn_x, L0_attn_Wqkv, L0_attn_bqkv, L0_attn_Wo, L0_attn_bo,
                  L0_ffn_w_gate, L0_ffn_w_up, L0_ffn_w_down):
    x = np.asarray(L0_attn_x, dtype=np.float32)
    Wqkv = np.asarray(L0_attn_Wqkv, dtype=np.float32)
    bqkv = np.asarray(L0_attn_bqkv, dtype=np.float32)
    Wo = np.ascontiguousarray(np.asarray(L0_attn_Wo, dtype=np.float32))
    bo = np.asarray(L0_attn_bo, dtype=np.float32)
    w_gate = np.asarray(L0_ffn_w_gate, dtype=np.float32)
    w_up = np.asarray(L0_ffn_w_up, dtype=np.float32)
    w_down = np.asarray(L0_ffn_w_down, dtype=np.float32)

    qcols = np.concatenate([np.arange(192 * h, 192 * h + 64) for h in range(H)])
    kcols = np.concatenate([np.arange(192 * h + 64, 192 * h + 128) for h in range(H)])
    vcols = np.concatenate([np.arange(192 * h + 128, 192 * h + 192) for h in range(H)])
    wqk = np.ascontiguousarray(Wqkv[:, np.concatenate([qcols, kcols])])
    wv = np.ascontiguousarray(Wqkv[:, vcols])
    bqk = bqkv[np.concatenate([qcols, kcols])]
    bqkT = np.ascontiguousarray(bqk.reshape(16, 128).T)
    bv = np.ascontiguousarray(bqkv[vcols]).reshape(1, D)
    identb = np.eye(128, dtype=BF)
    identf = np.eye(128, dtype=np.float32)

    in_maps = []
    for c in range(NCORE):
        b, hh = c // 2, c % 2
        xpad = np.zeros((TP, D), np.float32)
        xpad[PAD:PAD + T] = x[b, 2048 * hh:2048 * hh + T]
        if hh == 0:
            xpad[PAD + T:PAD + T + 16] = x[b, 2048:2064]
        else:
            xpad[0:PAD] = x[b, 2032:2048]
        m = np.zeros((KB, NBLK * QB), np.float32)
        k_ = np.arange(KB)[:, None]
        q_ = np.arange(QB)[None, :]
        band = (k_ >= q_) & (k_ <= q_ + 2 * W)
        for blki in range(NBLK):
            pk = blki * QB + k_
            gk = 2048 * hh + pk - PAD
            valid = band & (gk >= 0) & (gk < S)
            qreal = (blki * QB + q_) < T
            mm = (valid & qreal).astype(np.float32)
            mm[0, ~qreal[0, :]] = 1.0
            m[:, blki * QB:(blki + 1) * QB] = mm
        in_maps.append(dict(
            xp=xpad, wqk=wqk, wvw=wv, bqkT=bqkT, bvin=bv, woin=Wo,
            boin=bo.reshape(1, D),
            wup_s=np.ascontiguousarray(w_up[UPS * c:UPS * (c + 1)]),
            wgate_s=np.ascontiguousarray(w_gate[UPS * c:UPS * (c + 1)]),
            wdown_s=np.ascontiguousarray(w_down[DNS * c:DNS * (c + 1)]),
            bandmask=m.astype(BF), idb=identb, idf=identf,
        ))
    return in_maps


_BUILT = [None]


def kernel(**inputs):
    in_maps = _host_prepare(**inputs)
    if _BUILT[0] is None:
        _BUILT[0] = build()[0]
    nc = _BUILT[0]
    res = run_bass_kernel_spmd(nc, in_maps, core_ids=list(range(NCORE)))
    y = np.zeros((B, S, D), np.float32)
    for c in range(NCORE):
        b, hh = c // 2, c % 2
        y[b, 2048 * hh:2048 * hh + T] = res.results[c]["out"]
    return y


# version 20
# speedup vs baseline: 1.0058x; 1.0058x over previous
import numpy as np
import ml_dtypes
from contextlib import ExitStack

import concourse.bass as bass
import concourse.mybir as mybir
from concourse.bass_utils import run_bass_kernel_spmd

F32 = mybir.dt.float32
BF16 = mybir.dt.bfloat16
I32 = mybir.dt.int32
I16 = mybir.dt.int16
U16 = mybir.dt.uint16
AF = mybir.ActivationFunctionType
ALU = mybir.AluOpType
AX = mybir.AxisListType
BF = ml_dtypes.bfloat16

B, S, D = 4, 4096, 1024
H, HD = 16, 64
W = 16
NCORE = 8
T = 2048
PAD = 16
TP = 2176
NT = TP // 128
NTR = T // 128
QB, KB = 96, 128
NBLK = 22
TQ = QB * NBLK
FF = 3072
NFC = FF // 128
KD = D // 128
UPS = FF // NCORE
DNS = D // NCORE
AGCOLS = 9216
AGW = 128 * AGCOLS
EXPB = -4.0
QA = 32767.0
EPS = 1e-5
TGS = [512, 512, 512, 512, 128]
TGO = [0, 512, 1024, 1536, 2048]
NTG = 5
NARE = 25
C_SX, C_ISX, C_SG, C_ISG, C_SU, C_ISU, C_SG2, C_ISG2, C_SO, C_ISO = range(10)
M_X, M_G, M_U, M_G2, M_O = range(5)
SHUF = [4 * (h % 4) + h // 4 for h in range(H)]


def build(dbg=()):
    nc = bass.Bass("TRN2", target_bir_lowering=False)
    top = ExitStack()
    ap = lambda t: t.ap()

    xp = ap(nc.dram_tensor("xp", [TP, D], F32, kind="ExternalInput"))
    wqk = ap(nc.dram_tensor("wqk", [D, 2 * D], F32, kind="ExternalInput"))
    wvw = ap(nc.dram_tensor("wvw", [D, D], F32, kind="ExternalInput"))
    bqkT = ap(nc.dram_tensor("bqkT", [128, 16], F32, kind="ExternalInput"))
    bvin = ap(nc.dram_tensor("bvin", [1, D], F32, kind="ExternalInput"))
    woin = ap(nc.dram_tensor("woin", [D, D], F32, kind="ExternalInput"))
    boin = ap(nc.dram_tensor("boin", [1, D], F32, kind="ExternalInput"))
    wup_s = ap(nc.dram_tensor("wup_s", [UPS, D], F32, kind="ExternalInput"))
    wgate_s = ap(nc.dram_tensor("wgate_s", [UPS, D], F32, kind="ExternalInput"))
    wdown_s = ap(nc.dram_tensor("wdown_s", [DNS, FF], F32, kind="ExternalInput"))
    bandmask = ap(nc.dram_tensor("bandmask", [KB, NBLK * QB], BF16, kind="ExternalInput"))
    idb = ap(nc.dram_tensor("idb", [128, 128], BF16, kind="ExternalInput"))
    idf = ap(nc.dram_tensor("idf", [128, 128], F32, kind="ExternalInput"))
    out = ap(nc.dram_tensor("out", [T, D], F32, kind="ExternalOutput"))

    qh = ap(nc.dram_tensor("qh", [128, KD * TP], BF16))
    kh = ap(nc.dram_tensor("kh", [128, KD * TP], BF16))
    vh = ap(nc.dram_tensor("vh", [TP, H * 65], BF16))
    x2h = ap(nc.dram_tensor("x2h", [T, D], F32))
    uph = ap(nc.dram_tensor("uph", [NFC, 128 * T], BF16))
    agin = ap(nc.dram_tensor("agin", [1, AGW], BF16))
    agout = ap(nc.dram_tensor("agout", [NCORE, AGW], BF16))
    cc = [(ap(nc.dram_tensor(f"ccin{i}", [1, 2], F32)),
           ap(nc.dram_tensor(f"ccout{i}", [1, 2], F32))) for i in range(4)]
    ccw = (ap(nc.dram_tensor("ccwin", [1, 2], F32)),
           ap(nc.dram_tensor("ccwout", [1, 2], F32)))

    dbgd = {}
    def mkdbg(name, shape, dt):
        if name in dbg:
            dbgd[name] = ap(nc.dram_tensor("dbg_" + name, shape, dt, kind="ExternalOutput"))
    mkdbg("qh", [128, KD * TP], BF16)
    mkdbg("kh", [128, KD * TP], BF16)
    mkdbg("vh", [TP, H * 65], BF16)
    mkdbg("ag", [NCORE, AGW], BF16)
    mkdbg("attnoT", [128, KD * TQ], BF16)
    mkdbg("x2", [T, D], F32)
    mkdbg("xqT", [128, KD * T], BF16)
    mkdbg("sc", [128, 16], F32)
    mkdbg("arena", [128, NARE * T], BF16)
    mkdbg("parena", [128, NARE * T], BF16)
    mkdbg("oarena", [128, NARE * T], BF16)
    mkdbg("uph", [NFC, 128 * T], BF16)

    eL = top.enter_context
    sbL = lambda n_, s_, d_=BF16: eL(nc.sbuf_tensor(n_, s_, d_, side="left"))
    idb_s = sbL("idb_s", [128, 128])
    idf_s = sbL("idf_s", [128, 128], F32)
    bqkT_s = sbL("bqkT_s", [128, 16], F32)
    bv_s = sbL("bv_s", [1, D])
    bo_s = sbL("bo_s", [1, D])
    onesb = sbL("onesb", [1, 512])
    onesf = sbL("onesf", [1, 128], F32)
    svec = sbL("svec", [128, 16], F32)
    sscal = sbL("sscal", [1, 16], F32)
    stats = sbL("stats", [128, 4 * 14], F32)
    lncol = sbL("lncol", [128, 2 * 6], F32)
    macc = sbL("macc", [128, 8], F32)
    gcol = sbL("gcol", [128, 4], F32)
    cstv = sbL("cstv", [128, 2], F32)
    masks = sbL("masks", [KB, NBLK * QB])

    pS = eL(nc.psum_tensor("pS", [128, 2048], F32))
    pV = eL(nc.psum_tensor("pV", [128, 1024], F32))
    pT = eL(nc.psum_tensor("pT", [128, 2048], BF16))

    class NS:
        pass
    sm = NS()
    for n_ in ["cst", "cbv", "gld0", "gld1", "wqv", "wqt", "wqe", "agw0", "agw1", "agc",
               "xin0", "xin1", "lnv1", "lns1", "lnv2", "lns2", "xtt", "xte",
               "qkl0", "qkl1", "qkl2", "qkl3", "wvl", "qkm", "qke",
               "qss0", "qss1", "vm", "ve", "vss0", "vss1", "vss2", "vss3",
               "abi0", "abi1", "qkb", "exps", "pnv", "ptt", "pte", "pvm", "pve",
               "wol", "xrd0", "xrd1", "wom", "l3v1", "l3s1", "l3v2", "l3s2",
               "x2a", "x2w0", "x2w1", "l2v1", "l2s1", "l2v2", "l2s2", "mxv",
               "sp_t1", "sp_v1", "sp_g", "sp_c", "sp_v2", "sp_t2", "sp_v3",
               "xqk", "xqq", "xqt", "xqe", "qkev",
               "wgl0", "wgl1", "wgl2", "wgl3", "gm", "gev", "gmx",
               "upw0", "upw1",
               "bv1", "bs1", "bv2",
               "uprd0", "uprd1", "cs1", "cv2",
               "wdl", "ov2", "os1", "dm", "dlv1", "dls1", "dlv2", "dls2",
               "x2rd0", "x2rd1", "oad", "ost0", "ost1", "dbgs"]:
        setattr(sm, n_, eL(nc.semaphore(n_)))
    gld = [sm.gld0, sm.gld1]
    agwS = [sm.agw0, sm.agw1]
    xinS = [sm.xin0, sm.xin1]
    qklS = [sm.qkl0, sm.qkl1, sm.qkl2, sm.qkl3]
    qssS = [sm.qss0, sm.qss1]
    vssS = [sm.vss0, sm.vss1, sm.vss2, sm.vss3]
    abiS = [sm.abi0, sm.abi1]
    xrdS = [sm.xrd0, sm.xrd1]
    x2wS = [sm.x2w0, sm.x2w1]
    wglS = [sm.wgl0, sm.wgl1, sm.wgl2, sm.wgl3]
    upwS = [sm.upw0, sm.upw1]
    uprdS = [sm.uprd0, sm.uprd1]
    x2rdS = [sm.x2rd0, sm.x2rd1]
    ostS = [sm.ost0, sm.ost1]
    ndbg = [0]
    marks = []
    def mark(name):
        n = sum(len(b.instructions) for b in nc.m.functions[0].blocks)
        marks.append((name, n))
    build.marks = marks

    blk = eL(nc.Block())


    right0 = ExitStack()
    eR = right0.enter_context
    sbR = lambda n_, s_, d_=BF16: eR(nc.sbuf_tensor(n_, s_, d_, side="right"))
    xlnT = sbR("xlnT", [128, KD * TP])
    xring = sbR("xring", [128, 2 * D], F32)
    lring = sbR("lring", [128, 2 * D])
    wkr = sbR("wkr", [128, 4 * KD * 128])
    wvr = sbR("wvr", [128, KD * D])
    qkst = sbR("qkst", [128, 2 * TP])
    vst_s = sbR("vst_s", [128, 4 * H * 65])
    r1 = ExitStack()
    eR1 = r1.enter_context
    sbR1 = lambda n_, s_, d_=BF16: eR1(nc.sbuf_tensor(n_, s_, d_, side="right"))
    wqr = sbR1("wqr", [128, 2 * 1024], F32)
    wqq = sbR1("wqq", [128, 9 * 1024])
    agst = sbR1("agst", [128, 2 * 1024])
    wqs = sbR1("wqs", [128, 2 * 64], F32)
    kq1 = sbR1("kq1", [128, 2 * 1024], I32)

    @blk.sync
    def _(sync):
        sync.dma_start(out=idb_s[:, :], in_=idb).then_inc(sm.cst, 16)
        sync.dma_start(out=idf_s[:, :], in_=idf).then_inc(sm.cst, 16)
        sync.dma_start(out=bqkT_s[:, :], in_=bqkT).then_inc(sm.cst, 16)
        sync.dma_start(out=masks[:, :], in_=bandmask).then_inc(sm.cst, 16)

    @blk.gpsimd
    def _(g):
        g.memset(onesb[:, :], 1.0)
        g.memset(onesf[:, :], 1.0)
        g.memset(macc[:, :], 0.0)
        g.memset(sscal[:, :], 0.0)
        g.memset(cstv[:, 0:1], EPS)
        g.memset(cstv[:, 1:2], EXPB)
        g.memset(vst_s.rearrange("p (s h c) -> p s h c", s=4, c=65)[:, :, :, 64:65], 1.0)
        g.engine_nop().then_inc(sm.cst, 1)
        g.dma_start(out=bv_s[:, :], in_=bvin).then_inc(sm.cbv, 16)
        g.dma_start(out=bo_s[:, :], in_=boin).then_inc(sm.cbv, 16)
        for gi in range(4):
            g.dma_start(out=wkr[:, gi * (KD * 128):(gi + 1) * (KD * 128)].rearrange("p (a b) -> p a b", a=KD),
                        in_=wqk.rearrange("(a p) b -> p a b", p=128)[:, :, gi * 128:(gi + 1) * 128]
                        ).then_inc(qklS[gi], 16)
        for ti in range(2):
            g.dma_start(out=wqr[:, ti * 1024:(ti + 1) * 1024],
                        in_=wup_s[ti * 128:(ti + 1) * 128, :]).then_inc(gld[ti], 16)

    mark("Pm1")
    @blk.sync
    def _(sync):
        for ti in range(NT):
            if ti >= 2:
                sync.wait_ge(sm.lns2, ti - 1)
            sync.dma_start(out=xring[:, (ti % 2) * D:(ti % 2 + 1) * D],
                           in_=xp[ti * 128:(ti + 1) * 128, :]).then_inc(xinS[ti % 2], 16)

    def _ln_evac(v, ti):
        r = ti % 2
        v.wait_ge(sm.xtt, ti + 1)
        v.tensor_copy(xlnT.rearrange("p (a b) -> p a b", a=KD)[:, :, ti * 128:(ti + 1) * 128],
                      pT[:, r * 1024:(r + 1) * 1024].rearrange("p (a b) -> p a b", a=KD))
        v.drain()
        v.engine_nop().then_inc(sm.xte, 1)

    @blk.vector
    def _(v):
        for ti in range(NT):
            r = ti % 2
            st = stats[:, r * 14:r * 14 + 14]
            v.wait_ge(xinS[ti % 2], 16 * (ti // 2 + 1))
            xt_ = xring[:, r * D:(r + 1) * D]
            v.bn_stats(st[:, 0:6], xt_[:, 0:512])
            v.bn_stats(st[:, 6:12], xt_[:, 512:1024])
            v.drain()
            v.bn_aggr(st[:, 12:14], st[:, 0:12])
            v.drain()
            v.engine_nop().then_inc(sm.lnv1, 1)
            v.wait_ge(sm.lns1, ti + 1)
            v.scalar_tensor_tensor(lncol[:, r * 6 + 1:r * 6 + 2], st[:, 12:13],
                                   -1.0, lncol[:, r * 6:r * 6 + 1], ALU.mult, ALU.mult)
            v.drain()
            v.engine_nop().then_inc(sm.lnv2, 1)
            if ti >= 1:
                _ln_evac(v, ti - 1)
        _ln_evac(v, NT - 1)

    @blk.scalar
    def _(s):
        s.wait_ge(sm.cst, 65)
        for ti in range(NT):
            r = ti % 2
            st = stats[:, r * 14:r * 14 + 14]
            s.wait_ge(sm.lnv1, ti + 1)
            s.activation(lncol[:, r * 6 + 2:r * 6 + 3], st[:, 13:14], AF.Ln, bias=cstv[:, 0:1])
            s.drain()
            s.activation(lncol[:, r * 6:r * 6 + 1], lncol[:, r * 6 + 2:r * 6 + 3],
                         AF.Exp, scale=-0.5).then_inc(sm.lns1, 1)
            s.wait_ge(sm.lnv2, ti + 1)
            if ti >= 2:
                s.wait_ge(sm.xtt, ti - 1)
            s.activation(lring[:, r * D:(r + 1) * D], xring[:, r * D:(r + 1) * D],
                         AF.Identity, bias=lncol[:, r * 6 + 1:r * 6 + 2],
                         scale=lncol[:, r * 6:r * 6 + 1]).then_inc(sm.lns2, 1)

    TGTILES = [4, 8, 12, 16, 17]

    def _qk0_tg(t, tg):
        t.wait_ge(sm.xte, TGTILES[tg])
        for k in range(KD):
            wch = wkr[:, k * 128:(k + 1) * 128]
            dst = pS[:, TGO[tg]:TGO[tg] + TGS[tg]] if tg < 4 else pV[:, 0:128]
            ins_ = t.matmul(dst, wch,
                            xlnT.rearrange("p (a b) -> p a b", a=KD)[:, k, TGO[tg]:TGO[tg] + TGS[tg]],
                            start=(k == 0), stop=(k == KD - 1))
        return ins_

    @blk.tensor
    def _(t):
        t.wait_ge(sm.cst, 65)
        for ti in range(NT):
            r = ti % 2
            t.wait_ge(sm.lns2, ti + 1)
            if ti >= 2:
                t.wait_ge(sm.xte, ti - 1)
            for j in range(8):
                ins_ = t.transpose(pT[:, r * 1024 + j * 128:r * 1024 + (j + 1) * 128],
                                   lring[:, r * D + j * 128:r * D + (j + 1) * 128],
                                   idb_s[:, :])
            ins_.then_inc(sm.xtt, 1)
            if ti == 5:
                t.wait_ge(qklS[0], 16)
                _qk0_tg(t, 0)
            elif ti == 9:
                _qk0_tg(t, 1)
            elif ti == 13:
                _qk0_tg(t, 2)

    def _gld_load(g, ti):
        if ti < 3:
            srcw = wup_s[ti * 128:(ti + 1) * 128, :]
        elif ti < 6:
            srcw = wgate_s[(ti - 3) * 128:(ti - 2) * 128, :]
        else:
            srcw = wdown_s[:, (ti - 6) * 1024:(ti - 5) * 1024]
        g.wait_ge(sm.wqv, ti - 1)
        g.dma_start(out=wqr[:, (ti % 2) * 1024:(ti % 2 + 1) * 1024],
                    in_=srcw).then_inc(gld[ti % 2], 16)

    @blk.gpsimd
    def _(g):
        for gi in range(4, 16):
            g.wait_ge(sm.qkm, gi - 3)
            g.dma_start(out=wkr[:, (gi % 4) * (KD * 128):(gi % 4 + 1) * (KD * 128)].rearrange("p (a b) -> p a b", a=KD),
                        in_=wqk.rearrange("(a p) b -> p a b", p=128)[:, :, gi * 128:(gi + 1) * 128]
                        ).then_inc(qklS[gi % 4], 16)
            if 4 <= gi <= 10:
                _gld_load(g, gi - 2)
            if 5 <= gi <= 12:
                ti = gi - 5
                g.wait_ge(sm.wqe, ti + 1)
                g.dma_start(out=bass.AP(agin.tensor, ti * 1024, [[AGCOLS, 128], [1, 1024]]),
                            in_=agst[:, (ti % 2) * 1024:(ti % 2 + 1) * 1024]).then_inc(agwS[ti % 2], 16)
        for ti in range(8, 9):
            g.wait_ge(sm.wqe, ti + 1)
            g.dma_start(out=bass.AP(agin.tensor, ti * 1024, [[AGCOLS, 128], [1, 1024]]),
                        in_=agst[:, (ti % 2) * 1024:(ti % 2 + 1) * 1024]).then_inc(agwS[ti % 2], 16)
        for k in range(KD):
            g.dma_start(out=wvr[:, k * D:(k + 1) * D],
                        in_=wvw[k * 128:(k + 1) * 128, :]).then_inc(sm.wvl, 16)


    @blk.tensor
    def _(t):
        ins_ = _qk0_tg(t, 3)
        ins_ = _qk0_tg(t, 4)
        ins_.then_inc(sm.qkm, 1)
        for gi in range(1, 16):
            t.wait_ge(qklS[gi % 4], 16 * (gi // 4 + 1))
            t.wait_ge(sm.qke, gi * 3)
            t.wait_ge(sm.qkev, gi * 2)
            for k in range(KD):
                wch = wkr[:, (gi % 4) * (KD * 128) + k * 128:(gi % 4) * (KD * 128) + (k + 1) * 128]
                for tg in range(NTG):
                    dst = pS[:, TGO[tg]:TGO[tg] + TGS[tg]] if tg < 4 else pV[:, 0:128]
                    ins_ = t.matmul(dst, wch,
                                    xlnT.rearrange("p (a b) -> p a b", a=KD)[:, k, TGO[tg]:TGO[tg] + TGS[tg]],
                                    start=(k == 0), stop=(k == KD - 1))
            ins_.then_inc(sm.qkm, 1)
            if 2 <= gi <= 10:
                ti = gi - 2
                if ti == 0:
                    t.wait_ge(sm.xte, NT)
                t.wait_ge(sm.wqv, ti + 1)
                if ti >= 2:
                    t.wait_ge(sm.wqe, ti - 1)
                for j in range(8):
                    ins_ = t.transpose(pT[:, (ti % 2) * 1024 + j * 128:(ti % 2) * 1024 + (j + 1) * 128],
                                       wqq[:, ti * 1024 + j * 128:ti * 1024 + (j + 1) * 128],
                                       idb_s[:, :])
                ins_.then_inc(sm.wqt, 1)
        t.wait_ge(sm.cbv, 32)
        t.wait_ge(sm.qke, 16 * 3)
        t.wait_ge(sm.qkev, 16 * 2)
        for ti in range(NT):
            t.wait_ge(sm.wvl, 16 * KD)
            t.wait_ge(sm.ve, ti)
            t.wait_ge(sm.qkm, 16)
            for k in range(KD):
                for half in range(2):
                    t.matmul(pV[:, half * 512:(half + 1) * 512],
                             xlnT.rearrange("p (a b) -> p a b", a=KD)[:, k, ti * 128:(ti + 1) * 128],
                             wvr[:, k * D + half * 512:k * D + (half + 1) * 512],
                             start=(k == 0), stop=False)
            for half in range(2):
                ins_ = t.matmul(pV[:, half * 512:(half + 1) * 512],
                                onesb[:, 0:128], bv_s[:, half * 512:(half + 1) * 512],
                                start=False, stop=True)
            ins_.then_inc(sm.vm, 1)

    @blk.scalar
    def _(s):
        for gi in range(16):
            r = gi % 2
            if gi >= 2:
                s.wait_ge(qssS[gi % 2], 16 * ((gi - 2) // 2 + 1))
            s.wait_ge(sm.qkm, gi + 1)
            for tg in range(2, NTG):
                src = pS[:, TGO[tg]:TGO[tg] + TGS[tg]] if tg < 4 else pV[:, 0:128]
                s.activation(qkst[:, r * TP + TGO[tg]:r * TP + TGO[tg] + TGS[tg]],
                             src, AF.Identity, bias=bqkT_s[:, gi:gi + 1]).then_inc(sm.qke, 1)
            if 2 <= gi <= 10:
                ti = gi - 2
                s.wait_ge(sm.wqt, ti + 1)
                if ti >= 2:
                    s.wait_ge(agwS[ti % 2], 16 * ((ti - 2) // 2 + 1))
                s.activation(agst[:, (ti % 2) * 1024:(ti % 2 + 1) * 1024],
                             pT[:, (ti % 2) * 1024:(ti % 2 + 1) * 1024],
                             AF.Copy).then_inc(sm.wqe, 1)

    @blk.vector
    def _(v):
        for gi in range(16):
            r = gi % 2
            if gi >= 2:
                v.wait_ge(qssS[gi % 2], 16 * ((gi - 2) // 2 + 1))
            v.wait_ge(sm.qkm, gi + 1)
            for tg in range(2):
                v.tensor_scalar(qkst[:, r * TP + TGO[tg]:r * TP + TGO[tg] + TGS[tg]],
                                pS[:, TGO[tg]:TGO[tg] + TGS[tg]],
                                bqkT_s[:, gi:gi + 1], None, ALU.add)
                v.drain()
                v.engine_nop().then_inc(sm.qkev, 1)
            if gi < 9:
                ti = gi
                v.wait_ge(gld[ti % 2], 16 * (ti // 2 + 1))
                srcw = wqr[:, (ti % 2) * 1024:(ti % 2 + 1) * 1024]
                bm = wqs[:, (ti % 2) * 64:(ti % 2) * 64 + 32]
                iv = wqs[:, (ti % 2) * 64 + 32:(ti % 2) * 64 + 64]
                v.tensor_reduce(bm, srcw.rearrange("p (a b) -> p a b", b=32),
                                AX.X, ALU.max, apply_absolute_value=True)
                v.drain()
                v.tensor_scalar(bm, bm, 1.0 / 7.0, 1e-12, ALU.mult, ALU.max)
                v.drain()
                v.reciprocal(iv, bm)
                v.drain()
                v.tensor_tensor(kq1[:, (ti % 2) * 1024:(ti % 2 + 1) * 1024], srcw,
                                iv.rearrange("p a -> p a ()").broadcast_to([128, 32, 32]),
                                ALU.mult)
                v.drain()
                v.tensor_tensor(wqq[:, ti * 1024:(ti + 1) * 1024],
                                kq1[:, (ti % 2) * 1024:(ti % 2 + 1) * 1024],
                                bm.rearrange("p a -> p a ()").broadcast_to([128, 32, 32]),
                                ALU.mult)
                v.drain()
                v.engine_nop().then_inc(sm.wqv, 1)

    @blk.sync
    def _(sync):
        for gi in range(16):
            r = gi % 2
            sync.wait_ge(sm.qke, gi * 3 + 3)
            sync.wait_ge(sm.qkev, gi * 2 + 2)
            dst = qh if gi < KD else kh
            gl = gi % KD
            sync.dma_start(out=bass.AP(dst.tensor, gl * TP, [[KD * TP, 128], [1, TP]]),
                           in_=qkst[:, r * TP:(r + 1) * TP]).then_inc(qssS[gi % 2], 16)
        sync.wait_ge(sm.cst, 65)
        for ti in range(NT):
            sync.wait_ge(sm.ve, ti + 1)
            sync.dma_start(out=vh[ti * 128:(ti + 1) * 128, :],
                           in_=vst_s[:, (ti % 4) * (H * 65):(ti % 4 + 1) * (H * 65)]).then_inc(vssS[ti % 4], 16)

    @blk.vector
    def _(v):
        VW = H * 65
        for ti in range(NT):
            r = ti % 4
            v.wait_ge(sm.vm, ti + 1)
            if ti >= 4:
                v.wait_ge(vssS[(ti - 4) % 4], 16 * ((ti - 4) // 4 + 1))
            v.tensor_copy(vst_s[:, r * VW:(r + 1) * VW].rearrange("p (h c) -> p h c", c=65)[:, :, 0:64],
                          pV[:, 0:1024].rearrange("p (h c) -> p h c", c=64))
            v.drain()
            v.engine_nop().then_inc(sm.ve, 1)

    for nm_, src_ in (("qh", qh), ("kh", kh), ("vh", vh)):
        if nm_ in dbg:
            @blk.sync
            def _(sync, nm_=nm_, src_=src_):
                sync.wait_ge(sm.qss0, 16 * 8)
                sync.wait_ge(sm.qss1, 16 * 8)
                sync.wait_ge(sm.vss0, 16 * 5)
                sync.wait_ge(sm.vss1, 16 * 4)
                sync.wait_ge(sm.vss2, 16 * 4)
                sync.wait_ge(sm.vss3, 16 * 4)
                sync.dma_start(out=dbgd[nm_], in_=src_).then_inc(sm.dbgs, 16)
            ndbg[0] += 16
    if "ag" in dbg:
        @blk.sync
        def _(sync):
            sync.wait_ge(sm.agc, 1)
            sync.dma_start(out=dbgd["ag"], in_=agout).then_inc(sm.dbgs, 16)
        ndbg[0] += 16

    r1.close()
    right0.close()
    right1 = ExitStack()
    eR = right1.enter_context
    sbR = lambda n_, s_, d_=BF16: eR(nc.sbuf_tensor(n_, s_, d_, side="right"))
    qring = sbR("qring", [128, 2 * KD * QB])
    kring = sbR("kring", [128, 2 * KD * KB])
    vring = sbR("vring", [128, 2 * H * 65])
    prb = sbR("prb", [128, 2 * H * QB])
    atm = sbR("atm", [128, 2 * H * 64])
    rcp = sbR("rcp", [128, 4 * 16], F32)

    left1 = ExitStack()
    xln2 = left1.enter_context(nc.sbuf_tensor("xln2", [128, NTR * D], BF16, side="left"))
    left2 = ExitStack()
    eL2 = left2.enter_context
    attnoT = eL2(nc.sbuf_tensor("attnoT", [128, KD * TQ], BF16, side="left"))
    wor = eL2(nc.sbuf_tensor("wor", [128, KD * D], BF16, side="left"))
    f3r = eL2(nc.sbuf_tensor("f3r", [128, 2 * D], F32, side="left"))
    x2r = eL2(nc.sbuf_tensor("x2r", [128, 2 * D], F32, side="left"))

    @blk.sync
    def _(sync):
        sync.wait_ge(sm.qss0, 16 * 8)
        sync.wait_ge(sm.qss1, 16 * 8)
        vm_seen = -1
        for b in range(NBLK):
            r = b % 2
            if b >= 2:
                sync.wait_ge(sm.qkb, b - 1)
                sync.wait_ge(sm.pvm, b - 1)
            m_ = min((QB * b + KB - 1) // 128, NT - 1)
            while vm_seen < m_:
                vm_seen += 1
                sync.wait_ge(vssS[vm_seen % 4], 16 * (vm_seen // 4 + 1))
            sync.dma_start(out=qring[:, r * KD * QB:(r + 1) * KD * QB].rearrange("p (a b) -> p a b", a=KD),
                           in_=bass.AP(qh.tensor, QB * b + PAD,
                                       [[KD * TP, 128], [TP, KD], [1, QB]])).then_inc(abiS[b % 2], 16)
            sync.dma_start(out=kring[:, r * KD * KB:(r + 1) * KD * KB].rearrange("p (a b) -> p a b", a=KD),
                           in_=bass.AP(kh.tensor, QB * b,
                                       [[KD * TP, 128], [TP, KD], [1, KB]])).then_inc(abiS[b % 2], 16)
            sync.dma_start(out=vring[:, r * (H * 65):(r + 1) * (H * 65)],
                           in_=vh[QB * b:QB * b + KB, :]).then_inc(abiS[b % 2], 16)

    SC = [512 * (h % 3) + 96 * (h // 3) if h < 15 else 1536 for h in range(H)]
    PC = [480 * (h % 3) + 96 * (h // 3) if h < 15 else 1440 for h in range(H)]
    PVSEQ = "ABABABCABABABCAB"
    _pvslot = {}
    _cnt = {"A": 0, "B": 0, "C": 0}
    for _h, _rg in enumerate(PVSEQ):
        _pvslot[_h] = (_rg, _cnt[_rg])
        _cnt[_rg] += 1

    def _pvc(h):
        rg, sl = _pvslot[h]
        if rg == "A":
            return pV, 65 * sl
        if rg == "B":
            return pV, 512 + 65 * sl
        return pS, 1664 + 65 * sl
    STIX = {h: {"A": 0, "B": 7, "C": 14}[rg] + sl for h, (rg, sl) in _pvslot.items()}
    PVW = H * 65

    @blk.tensor
    def _(t):
        t.wait_ge(sm.qke, 16 * 3)
        t.wait_ge(sm.qkev, 16 * 2)
        for i in range(NBLK + 2):
            if i < NBLK:
                b = i
                r = b % 2
                t.wait_ge(abiS[r], 48 * (b // 2 + 1))
                if b >= 1:
                    t.wait_ge(sm.exps, b)
                for h in range(H):
                    pp = 64 * (h % 2)
                    g = h // 2
                    ins_ = t.matmul(pS[:, SC[h]:SC[h] + QB],
                                    kring[pp:pp + 64, r * KD * KB + g * KB:r * KD * KB + (g + 1) * KB],
                                    qring[pp:pp + 64, r * KD * QB + g * QB:r * KD * QB + (g + 1) * QB],
                                    start=True, stop=True)
                ins_.then_inc(sm.qkb, 1)
            if 1 <= i <= NBLK:
                b = i - 1
                r = b % 2
                t.wait_ge(sm.pnv, b + 1)
                if b >= 1:
                    t.wait_ge(sm.pve, b)
                for h in range(H):
                    pt_, pc_ = _pvc(h)
                    ins_ = t.matmul(pt_[0:QB, pc_:pc_ + 65],
                                    prb[:, r * (H * QB) + PC[h]:r * (H * QB) + PC[h] + QB],
                                    vring[:, r * PVW + h * 65:r * PVW + (h + 1) * 65],
                                    start=True, stop=True)
                ins_.then_inc(sm.pvm, 1)
            if i >= 2:
                b = i - 2
                r = b % 2
                t.wait_ge(sm.pve, b + 1)
                if b >= 1:
                    t.wait_ge(sm.pte, b)
                for h in range(H):
                    ac = r * (H * 64) + STIX[h] * 64
                    ins_ = t.transpose(pT[64 * (h % 2):64 * (h % 2) + 64, (h // 2) * QB:(h // 2 + 1) * QB],
                                       atm[0:QB, ac:ac + 64],
                                       idb_s[0:QB, 0:QB],
                                       tile_position=(0, 64 * (h % 2)))
                ins_.then_inc(sm.ptt, 1)

    @blk.scalar
    def _(s):
        for b in range(NBLK):
            r = b % 2
            s.wait_ge(sm.qkb, b + 1)
            if b >= 2:
                s.wait_ge(sm.pvm, b - 1)
            for bk in range(3):
                s.activation(prb[:, r * (H * QB) + 480 * bk:r * (H * QB) + 480 * (bk + 1)],
                             pS[:, 512 * bk:512 * bk + 480],
                             AF.Exp, scale=0.125, bias=cstv[:, 1:2])
            s.activation(prb[:, r * (H * QB) + 1440:r * (H * QB) + 1536],
                         pS[:, 1536:1632],
                         AF.Exp, scale=0.125, bias=cstv[:, 1:2]).then_inc(sm.exps, 1)

    @blk.vector
    def _(v):
        v.wait_ge(sm.cst, 65)
        for i in range(NBLK + 2):
            if i < NBLK:
                b = i
                r = b % 2
                v.wait_ge(sm.exps, b + 1)
                v.tensor_tensor(prb[:, r * (H * QB):(r + 1) * (H * QB)].rearrange("p (a b) -> p a b", a=H),
                                prb[:, r * (H * QB):(r + 1) * (H * QB)].rearrange("p (a b) -> p a b", a=H),
                                masks[:, b * QB:(b + 1) * QB].rearrange("p b -> p () b").broadcast_to([KB, H, QB]),
                                ALU.mult)
                v.drain()
                v.engine_nop().then_inc(sm.pnv, 1)
            if 1 <= i <= NBLK:
                b = i - 1
                r = b % 2
                v.wait_ge(sm.pvm, b + 1)
                sg_ = rcp[0:QB, 32 + r * 16:32 + r * 16 + 16]
                v.tensor_copy(sg_[:, 0:7].rearrange("p (h c) -> p h c", c=1),
                              pV[0:QB, 0:455].rearrange("p (h c) -> p h c", c=65)[:, :, 64:65])
                v.tensor_copy(sg_[:, 7:14].rearrange("p (h c) -> p h c", c=1),
                              pV[0:QB, 512:967].rearrange("p (h c) -> p h c", c=65)[:, :, 64:65])
                v.tensor_copy(sg_[:, 14:16].rearrange("p (h c) -> p h c", c=1),
                              pS[0:QB, 1664:1794].rearrange("p (h c) -> p h c", c=65)[:, :, 64:65])
                v.drain()
                v.reciprocal(rcp[0:QB, r * 16:r * 16 + 16], sg_)
                v.drain()
                ab = r * (H * 64)
                v.tensor_tensor(atm[0:QB, ab:ab + 448].rearrange("p (h c) -> p h c", c=64),
                                pV[0:QB, 0:455].rearrange("p (h c) -> p h c", c=65)[:, :, 0:64],
                                rcp[0:QB, r * 16:r * 16 + 7].rearrange("p h -> p h ()").broadcast_to([QB, 7, 64]),
                                ALU.mult)
                v.tensor_tensor(atm[0:QB, ab + 448:ab + 896].rearrange("p (h c) -> p h c", c=64),
                                pV[0:QB, 512:967].rearrange("p (h c) -> p h c", c=65)[:, :, 0:64],
                                rcp[0:QB, r * 16 + 7:r * 16 + 14].rearrange("p h -> p h ()").broadcast_to([QB, 7, 64]),
                                ALU.mult)
                v.tensor_tensor(atm[0:QB, ab + 896:ab + 1024].rearrange("p (h c) -> p h c", c=64),
                                pS[0:QB, 1664:1794].rearrange("p (h c) -> p h c", c=65)[:, :, 0:64],
                                rcp[0:QB, r * 16 + 14:r * 16 + 16].rearrange("p h -> p h ()").broadcast_to([QB, 2, 64]),
                                ALU.mult)
                v.drain()
                v.engine_nop().then_inc(sm.pve, 1)
            if i >= 2:
                b = i - 2
                v.wait_ge(sm.ptt, b + 1)
                v.tensor_copy(attnoT.rearrange("p (a b) -> p a b", a=KD)[:, :, b * QB:(b + 1) * QB],
                              pT[:, 0:768].rearrange("p (a b) -> p a b", a=KD))
                v.drain()
                v.engine_nop().then_inc(sm.pte, 1)

    if "attnoT" in dbg:
        @blk.sync
        def _(sync):
            sync.wait_ge(sm.pte, NBLK)
            sync.dma_start(out=dbgd["attnoT"], in_=attnoT[:, :]).then_inc(sm.dbgs, 16)
        ndbg[0] += 16

    @blk.gpsimd
    def _(g):
        g.wait_ge(sm.agw0, 16 * 5)
        g.wait_ge(sm.agw1, 16 * 4)
        for k in range(KD):
            g.dma_start(out=wor[:, k * D:(k + 1) * D],
                        in_=woin[k * 128:(k + 1) * 128, :]).then_inc(sm.wol, 16)
        g.wait_ge(sm.cst, 65)
        g.dma_start(out=ccw[0], in_=sscal[0:1, 0:2]).then_inc(sm.sp_g, 16)
        g.wait_ge(sm.sp_g, 16)
        g.wait_ge(sm.agw0, 16 * 5)
        g.wait_ge(sm.agw1, 16 * 4)
        g.collective_compute("AllReduce", ALU.max,
                             replica_groups=[list(range(NCORE))],
                             ins=[ccw[0]], outs=[ccw[1]]).then_inc(sm.sp_c, 1)
        g.collective_compute("AllGather", ALU.bypass,
                             replica_groups=[list(range(NCORE))],
                             ins=[agin], outs=[agout]).then_inc(sm.agc, 1)

    @blk.sync
    def _(sync):
        for ti in range(NTR):
            if ti >= 2:
                sync.wait_ge(sm.x2a, ti - 1)
            sync.dma_start(out=xring[:, (ti % 2) * D:(ti % 2 + 1) * D],
                           in_=xp[PAD + ti * 128:PAD + (ti + 1) * 128, :]).then_inc(xrdS[ti % 2], 16)
            if ti >= 1:
                sync.wait_ge(sm.x2a, ti)
                sync.dma_start(out=x2h[(ti - 1) * 128:ti * 128, :],
                               in_=x2r[:, ((ti - 1) % 2) * D:((ti - 1) % 2 + 1) * D]).then_inc(x2wS[(ti - 1) % 2], 16)
        sync.wait_ge(sm.x2a, NTR)
        sync.dma_start(out=x2h[(NTR - 1) * 128:NTR * 128, :],
                       in_=x2r[:, ((NTR - 1) % 2) * D:((NTR - 1) % 2 + 1) * D]).then_inc(x2wS[(NTR - 1) % 2], 16)

    @blk.tensor
    def _(t):
        t.wait_ge(sm.wol, 16 * KD)
        t.wait_ge(sm.pte, NBLK)
        t.wait_ge(sm.pve, NBLK)
        t.wait_ge(sm.exps, NBLK)
        for ti in range(NTR):
            r = ti % 2
            if ti >= 2:
                t.wait_ge(sm.l3v1, ti - 1)
                t.wait_ge(sm.l3s2, ti - 1)
            for k in range(KD):
                for half in range(2):
                    t.matmul(pS[:, r * 1024 + half * 512:r * 1024 + (half + 1) * 512],
                             attnoT.rearrange("p (a b) -> p a b", a=KD)[:, k, ti * 128:(ti + 1) * 128],
                             wor[:, k * D + half * 512:k * D + (half + 1) * 512],
                             start=(k == 0), stop=False)
            for half in range(2):
                ins_ = t.matmul(pS[:, r * 1024 + half * 512:r * 1024 + (half + 1) * 512],
                                onesb[:, 0:128], bo_s[:, half * 512:(half + 1) * 512],
                                start=False, stop=True)
            ins_.then_inc(sm.wom, 1)

    @blk.vector
    def _(v):
        for t in range(NTR + 2):
            if t < NTR:
                r = t % 2
                st = stats[:, r * 14:r * 14 + 14]
                v.wait_ge(sm.wom, t + 1)
                if t >= 2:
                    v.wait_ge(sm.l3v2, t - 1)
                srcp = pS[:, r * 1024:(r + 1) * 1024]
                v.bn_stats(st[:, 0:6], srcp[:, 0:512])
                v.bn_stats(st[:, 6:12], srcp[:, 512:1024])
                v.drain()
                v.bn_aggr(st[:, 12:14], st[:, 0:12])
                v.drain()
                v.engine_nop().then_inc(sm.l3v1, 1)
            if 1 <= t <= NTR:
                u = t - 1
                r = u % 2
                st = stats[:, r * 14:r * 14 + 14]
                st2 = stats[:, (2 + r) * 14:(2 + r) * 14 + 14]
                v.wait_ge(sm.l3s1, u + 1)
                v.scalar_tensor_tensor(lncol[:, r * 6 + 1:r * 6 + 2], st[:, 12:13],
                                       -1.0, lncol[:, r * 6:r * 6 + 1], ALU.mult, ALU.mult)
                v.drain()
                v.engine_nop().then_inc(sm.l3v2, 1)
                v.wait_ge(sm.l3s2, u + 1)
                v.wait_ge(xrdS[r], 16 * (u // 2 + 1))
                if u >= 2:
                    v.wait_ge(x2wS[r], 16 * ((u - 2) // 2 + 1))
                    v.wait_ge(sm.l2s2, u - 1)
                v.tensor_tensor(x2r[:, r * D:(r + 1) * D], f3r[:, r * D:(r + 1) * D],
                                xring[:, r * D:(r + 1) * D], ALU.add)
                v.drain()
                v.engine_nop().then_inc(sm.x2a, 1)
                v.bn_stats(st2[:, 0:6], x2r[:, r * D:r * D + 512])
                v.bn_stats(st2[:, 6:12], x2r[:, r * D + 512:r * D + 1024])
                v.drain()
                v.bn_aggr(st2[:, 12:14], st2[:, 0:12])
                v.drain()
                v.engine_nop().then_inc(sm.l2v1, 1)
            if t >= 2:
                u = t - 2
                r = u % 2
                st2 = stats[:, (2 + r) * 14:(2 + r) * 14 + 14]
                v.wait_ge(sm.l2s1, u + 1)
                v.scalar_tensor_tensor(lncol[:, r * 6 + 4:r * 6 + 5], st2[:, 12:13],
                                       -1.0, lncol[:, r * 6 + 3:r * 6 + 4], ALU.mult, ALU.mult)
                v.drain()
                v.engine_nop().then_inc(sm.l2v2, 1)
                v.wait_ge(sm.l2s2, u + 1)
                v.tensor_reduce(gcol[:, 0:1], xln2[:, u * D:(u + 1) * D], AX.X,
                                ALU.max, apply_absolute_value=True)
                v.drain()
                v.tensor_tensor(macc[:, M_X:M_X + 1], macc[:, M_X:M_X + 1],
                                gcol[:, 0:1], ALU.max)
                v.drain()
                v.engine_nop().then_inc(sm.mxv, 1)

    @blk.scalar
    def _(s):
        for t in range(NTR + 2):
            if t < NTR:
                r = t % 2
                st = stats[:, r * 14:r * 14 + 14]
                s.wait_ge(sm.l3v1, t + 1)
                s.activation(lncol[:, r * 6 + 2:r * 6 + 3], st[:, 13:14], AF.Ln, bias=cstv[:, 0:1])
                s.drain()
                s.activation(lncol[:, r * 6:r * 6 + 1], lncol[:, r * 6 + 2:r * 6 + 3],
                             AF.Exp, scale=-0.5).then_inc(sm.l3s1, 1)
            if 1 <= t <= NTR:
                u = t - 1
                r = u % 2
                st2 = stats[:, (2 + r) * 14:(2 + r) * 14 + 14]
                s.wait_ge(sm.l3v2, u + 1)
                if u >= 2:
                    s.wait_ge(sm.x2a, u - 1)
                s.activation(f3r[:, r * D:(r + 1) * D], pS[:, r * 1024:(r + 1) * 1024],
                             AF.Identity, bias=lncol[:, r * 6 + 1:r * 6 + 2],
                             scale=lncol[:, r * 6:r * 6 + 1]).then_inc(sm.l3s2, 1)
                s.wait_ge(sm.l2v1, u + 1)
                s.activation(lncol[:, r * 6 + 5:r * 6 + 6], st2[:, 13:14], AF.Ln, bias=cstv[:, 0:1])
                s.drain()
                s.activation(lncol[:, r * 6 + 3:r * 6 + 4], lncol[:, r * 6 + 5:r * 6 + 6],
                             AF.Exp, scale=-0.5).then_inc(sm.l2s1, 1)
            if t >= 2:
                u = t - 2
                r = u % 2
                s.wait_ge(sm.l2v2, u + 1)
                s.activation(xln2[:, u * D:(u + 1) * D], x2r[:, r * D:(r + 1) * D],
                             AF.Identity, bias=lncol[:, r * 6 + 4:r * 6 + 5],
                             scale=lncol[:, r * 6 + 3:r * 6 + 4]).then_inc(sm.l2s2, 1)

    if "x2" in dbg:
        @blk.sync
        def _(sync):
            sync.wait_ge(sm.x2w0, 16 * 8)
            sync.wait_ge(sm.x2w1, 16 * 8)
            sync.dma_start(out=dbgd["x2"], in_=x2h).then_inc(sm.dbgs, 16)
        ndbg[0] += 16

    spc = dict(t1=0, v1=0, g=16, c=1, v2=0, t2=0, v3=0)

    def sp_t1_emit(t, pairs):
        for i, (mc, _, _) in enumerate(pairs):
            ins_ = t.transpose(pV[0:1, 512 + i * 128:512 + (i + 1) * 128],
                               macc[:, mc:mc + 1], idf_s[:, :])
        spc["t1"] += 1
        return ins_

    def sp_t2_emit(t, pairs):
        lo = min(min(sc_, ic_) for _, sc_, ic_ in pairs)
        hi = max(max(sc_, ic_) for _, sc_, ic_ in pairs) + 1
        ins_ = t.matmul(pV[:, 512 + lo:512 + hi], onesf[:, :], sscal[0:1, lo:hi],
                        start=True, stop=True)
        spc["t2"] += 1
        return ins_

    def sync_point(spi, pairs, wait_sem=None, wait_val=None, t1val=None, t2val=None):
        n = len(pairs)
        cin, cout = cc[spi]
        inline_pe = t1val is not None
        if not inline_pe:
            @blk.tensor
            def _(t):
                t.wait_ge(wait_sem, wait_val)
                ins_ = sp_t1_emit(t, pairs)
                ins_.then_inc(sm.sp_t1, 1)
            t1val = spc["t1"]

        @blk.vector
        def _(v):
            v.wait_ge(sm.sp_t1, t1val)
            for i in range(n):
                v.tensor_reduce(sscal[0:1, 12 + i:13 + i],
                                pV[0:1, 512 + i * 128:512 + (i + 1) * 128],
                                AX.X, ALU.max)
            v.drain()
            v.engine_nop().then_inc(sm.sp_v1, 1)
        spc["v1"] += 1

        @blk.gpsimd
        def _(g):
            g.wait_ge(sm.sp_v1, spc["v1"])
            g.dma_start(out=cin, in_=sscal[0:1, 12:14]).then_inc(sm.sp_g, 16)
            g.wait_ge(sm.sp_g, spc["g"] + 16)
            g.collective_compute("AllReduce", ALU.max,
                                 replica_groups=[list(range(NCORE))],
                                 ins=[cin], outs=[cout]).then_inc(sm.sp_c, 1)
            g.wait_ge(sm.sp_c, spc["c"] + 1)
            g.dma_start(out=sscal[0:1, 12:14], in_=cout).then_inc(sm.sp_g, 16)
        spc["g"] += 32
        spc["c"] += 1

        @blk.vector
        def _(v):
            v.wait_ge(sm.sp_g, spc["g"])
            for i, (_, sc_, ic_) in enumerate(pairs):
                v.tensor_scalar(sscal[0:1, sc_:sc_ + 1], sscal[0:1, 12 + i:13 + i],
                                1.0 / QA, 1e-12, ALU.mult, ALU.max)
                v.drain()
                v.reciprocal(sscal[0:1, ic_:ic_ + 1], sscal[0:1, sc_:sc_ + 1])
                v.drain()
            v.engine_nop().then_inc(sm.sp_v2, 1)
        spc["v2"] += 1

        if not inline_pe:
            @blk.tensor
            def _(t):
                t.wait_ge(sm.sp_v2, spc["v2"])
                ins_ = sp_t2_emit(t, pairs)
                ins_.then_inc(sm.sp_t2, 1)
            t2val = spc["t2"]

        @blk.vector
        def _(v):
            v.wait_ge(sm.sp_t2, t2val)
            lo = min(min(sc_, ic_) for _, sc_, ic_ in pairs)
            hi = max(max(sc_, ic_) for _, sc_, ic_ in pairs) + 1
            v.tensor_copy(svec[:, lo:hi], pV[:, 512 + lo:512 + hi])
            v.drain()
            v.engine_nop().then_inc(sm.sp_v3, 1)
        spc["v3"] += 1

    sync_point(0, [(M_X, C_SX, C_ISX)], sm.mxv, NTR)

    mark("P0")
    right1.close()
    left2.close()
    right2 = ExitStack()
    xqT = right2.enter_context(nc.sbuf_tensor("xqT", [128, KD * T], BF16, side="right"))
    kq3 = right2.enter_context(nc.sbuf_tensor("kq3", [128, 1024], I16, side="right"))
    xqst = right2.enter_context(nc.sbuf_tensor("xqst", [128, 2 * D], BF16, side="right"))

    def _xq_evac(v, ti):
        r = ti % 2
        v.wait_ge(sm.xqt, ti + 1)
        v.tensor_copy(xqT.rearrange("p (a b) -> p a b", a=KD)[:, :, ti * 128:(ti + 1) * 128],
                      pT[:, r * 1024:(r + 1) * 1024].rearrange("p (a b) -> p a b", a=KD))
        v.drain()
        v.engine_nop().then_inc(sm.xqe, 1)

    @blk.vector
    def _(v):
        v.wait_ge(sm.sp_v3, spc["v3"])
        for ti in range(NTR):
            if ti >= 1:
                v.wait_ge(sm.xqq, ti)
            v.tensor_scalar(kq3[:, :], xln2[:, ti * D:(ti + 1) * D],
                            svec[:, C_ISX:C_ISX + 1], None, ALU.mult)
            v.drain()
            v.engine_nop().then_inc(sm.xqk, 1)
            if ti >= 1:
                _xq_evac(v, ti - 1)
        _xq_evac(v, NTR - 1)

    @blk.scalar
    def _(s):
        for ti in range(NTR):
            r = ti % 2
            s.wait_ge(sm.xqk, ti + 1)
            if ti >= 2:
                s.wait_ge(sm.xqt, ti - 1)
            s.activation(xqst[:, r * D:(r + 1) * D], kq3[:, :],
                         AF.Copy, scale=svec[:, C_SX:C_SX + 1]).then_inc(sm.xqq, 1)

    @blk.tensor
    def _(t):
        for ti in range(NTR):
            r = ti % 2
            t.wait_ge(sm.xqq, ti + 1)
            if ti >= 2:
                t.wait_ge(sm.xqe, ti - 1)
            for j in range(8):
                ins_ = t.transpose(pT[:, r * 1024 + j * 128:r * 1024 + (j + 1) * 128],
                                   xqst[:, r * D + j * 128:r * D + (j + 1) * 128],
                                   idb_s[:, :])
            ins_.then_inc(sm.xqt, 1)

    if "xqT" in dbg:
        @blk.sync
        def _(sync):
            sync.wait_ge(sm.xqe, NTR)
            sync.dma_start(out=dbgd["xqT"], in_=xqT[:, :]).then_inc(sm.dbgs, 16)
        ndbg[0] += 16

    mark("P1P2S1")
    left1.close()
    left4 = ExitStack()
    eL4 = left4.enter_context
    arena = eL4(nc.sbuf_tensor("arena", [128, NARE * T], BF16, side="left"))
    kscr = eL4(nc.sbuf_tensor("kscr", [128, 2 * T], I16, side="left"))
    kscr3 = eL4(nc.sbuf_tensor("kscr3", [128, 2 * T], I16, side="left"))
    left4b = ExitStack()
    eL4b = left4b.enter_context
    kscru = eL4b(nc.sbuf_tensor("kscru", [128, 2 * T], U16, side="left"))
    upsb = eL4b(nc.sbuf_tensor("upsb", [128, 2 * T], BF16, side="left"))
    sscr = eL4b(nc.sbuf_tensor("sscr", [128, 2 * T], F32, side="left"))
    wgr = eL4b(nc.sbuf_tensor("wgr", [128, 4 * 1024], BF16, side="left"))

    PS5 = lambda i: (i + NARE - 1) % NARE
    HT = T // 2
    @blk.sync
    def _(sync):
        sync.wait_ge(sm.agc, 1)
        for f in range(48):
            if f >= 4:
                sync.wait_ge(sm.gm, f - 3)
            fl = f % NFC
            s_ = fl // 3
            a_ = fl % 3
            tile = (3 + a_) if f < NFC else a_
            sync.dma_start(out=wgr[:, (f % 4) * 1024:(f % 4 + 1) * 1024],
                           in_=bass.AP(agout.tensor, s_ * AGW + tile * 1024,
                                       [[AGCOLS, 128], [1, 1024]])).then_inc(wglS[f % 4], 16)

    S2A_PAIRS = [(M_G, C_SG, C_ISG)]
    S2B_PAIRS = [(M_U, C_SU, C_ISU), (M_G2, C_SG2, C_ISG2)]
    s2a_v2v = spc["v2"] + 1
    s2b_v2v = spc["v2"] + 2
    s2v = {}

    @blk.tensor
    def _(t):
        t.wait_ge(sm.xqe, NTR)
        for f in range(48):
            t.wait_ge(wglS[f % 4], 16 * (f // 4 + 1))
            for tg in range(4):
                if f >= 1:
                    t.wait_ge(sm.gev, (f - 1) * 4 + tg + 1)
                for k in range(KD):
                    ins_ = t.matmul(pS[:, tg * 512:(tg + 1) * 512],
                                    wgr[:, (f % 4) * 1024 + k * 128:(f % 4) * 1024 + (k + 1) * 128],
                                    xqT.rearrange("p (a b) -> p a b", a=KD)[:, k, tg * 512:(tg + 1) * 512],
                                    start=(k == 0), stop=(k == KD - 1))
            ins_.then_inc(sm.gm, 1)
            if f == NFC - 1:
                t.wait_ge(sm.gmx, NFC)
                sp_t1_emit(t, S2A_PAIRS).then_inc(sm.sp_t1, 1)
                s2v["a_t1"] = spc["t1"]
            elif f == NFC + 2:
                t.wait_ge(sm.sp_v2, s2a_v2v)
                sp_t2_emit(t, S2A_PAIRS).then_inc(sm.sp_t2, 1)
                s2v["a_t2"] = spc["t2"]
        t.wait_ge(sm.gmx, 48)
        t.wait_ge(sm.bv2, NFC)
        sp_t1_emit(t, S2B_PAIRS).then_inc(sm.sp_t1, 1)
        s2v["b_t1"] = spc["t1"]
        t.wait_ge(sm.sp_v2, s2b_v2v)
        sp_t2_emit(t, S2B_PAIRS).then_inc(sm.sp_t2, 1)
        s2v["b_t2"] = spc["t2"]

    @blk.scalar
    def _(s):
        s.wait_ge(sm.x2w0, 16 * 8)
        s.wait_ge(sm.x2w1, 16 * 8)
        for f in range(NFC):
            s.wait_ge(sm.gm, f + 1)
            for tg in range(4):
                s.activation(arena[:, f * T + tg * 512:f * T + (tg + 1) * 512],
                             pS[:, tg * 512:(tg + 1) * 512],
                             AF.Copy).then_inc(sm.gev, 1)

    @blk.vector
    def _(v):
        for f in range(NFC):
            v.wait_ge(sm.gev, f * 4 + 4)
            v.tensor_reduce(gcol[:, 1:2], arena[:, f * T:(f + 1) * T], AX.X,
                            ALU.max, apply_absolute_value=True)
            v.drain()
            v.tensor_tensor(macc[:, M_G:M_G + 1], macc[:, M_G:M_G + 1],
                            gcol[:, 1:2], ALU.max)
            v.drain()
            v.engine_nop().then_inc(sm.gmx, 1)

    sync_point(1, S2A_PAIRS, t1val=s2v["a_t1"], t2val=s2v["a_t2"])
    v3_s2a = spc["v3"]

    @blk.gpsimd
    def _(g):
        for fu in range(NFC):
            g.wait_ge(sm.gmx, NFC + fu + 1)
            g.dma_start(out=bass.AP(uph.tensor, fu * 128 * T, [[T, 128], [1, T]]),
                        in_=upsb[:, (fu % 2) * T:(fu % 2 + 1) * T]).then_inc(upwS[fu % 2], 16)

    mark("P3P4")
    @blk.scalar
    def _(s):
        for jj in range(NFC + 1):
            if jj < NFC:
                j = jj
                s.wait_ge(sm.gm, NFC + j + 1)
                if j >= 2:
                    s.wait_ge(upwS[j % 2], 16 * ((j - 2) // 2 + 1))
                for tg in range(4):
                    s.activation(upsb[:, (j % 2) * T + tg * 512:(j % 2) * T + (tg + 1) * 512],
                                 pS[:, tg * 512:(tg + 1) * 512],
                                 AF.Copy).then_inc(sm.gev, 1)
            if jj == 0:
                continue
            j = jj - 1
            if j == 0:
                s.wait_ge(sm.sp_v3, v3_s2a)
            s.wait_ge(sm.bv1, j + 1)
            if j >= 2:
                s.wait_ge(sm.bv2, j - 1)
            s.activation(arena[:, j * T:(j + 1) * T], kscr[:, (j % 2) * T:(j % 2 + 1) * T],
                         AF.Copy, scale=svec[:, C_SG:C_SG + 1])
            s.drain()
            s.activation(sscr[:, (j % 2) * T:(j % 2 + 1) * T], arena[:, j * T:(j + 1) * T],
                         AF.Sigmoid)
            s.drain()
            s.activation(kscru[:, (j % 2) * T:(j % 2 + 1) * T],
                         sscr[:, (j % 2) * T:(j % 2 + 1) * T],
                         AF.Copy, scale=65536.0).then_inc(sm.bs1, 1)

    @blk.vector
    def _(v):
        for jj in range(NFC + 1):
            if jj < NFC:
                j = jj
                v.wait_ge(sm.gev, (NFC + j) * 4 + 4)
                v.tensor_reduce(gcol[:, 1:2], upsb[:, (j % 2) * T:(j % 2 + 1) * T],
                                AX.X, ALU.max, apply_absolute_value=True)
                v.drain()
                v.tensor_tensor(macc[:, M_U:M_U + 1], macc[:, M_U:M_U + 1],
                                gcol[:, 1:2], ALU.max)
                v.drain()
                v.engine_nop().then_inc(sm.gmx, 1)
                if j == 0:
                    v.wait_ge(sm.sp_v3, v3_s2a)
                if j >= 2:
                    v.wait_ge(sm.bs1, j - 1)
                v.tensor_scalar(kscr[:, (j % 2) * T:(j % 2 + 1) * T],
                                arena[:, j * T:(j + 1) * T],
                                svec[:, C_ISG:C_ISG + 1], None, ALU.mult)
                v.drain()
                v.engine_nop().then_inc(sm.bv1, 1)
            if jj == 0:
                continue
            j = jj - 1
            v.wait_ge(sm.bs1, j + 1)
            sl = PS5(j)
            v.scalar_tensor_tensor(arena[:, sl * T:(sl + 1) * T],
                                   kscru[:, (j % 2) * T:(j % 2 + 1) * T], 1.0 / 65536.0,
                                   arena[:, j * T:(j + 1) * T], ALU.mult, ALU.mult)
            v.drain()
            v.tensor_reduce(gcol[:, 1:2], arena[:, sl * T:(sl + 1) * T], AX.X,
                            ALU.max, apply_absolute_value=True)
            v.drain()
            v.tensor_tensor(macc[:, M_G2:M_G2 + 1], macc[:, M_G2:M_G2 + 1],
                            gcol[:, 1:2], ALU.max)
            v.drain()
            v.engine_nop().then_inc(sm.bv2, 1)

    if "arena" in dbg:
        @blk.sync
        def _(sync):
            sync.wait_ge(sm.gmx, 48)
            sync.dma_start(out=dbgd["arena"], in_=arena[:, :]).then_inc(sm.dbgs, 16)
        ndbg[0] += 16
    if "uph" in dbg:
        @blk.sync
        def _(sync):
            sync.wait_ge(sm.upw0, 16 * 12)
            sync.wait_ge(sm.upw1, 16 * 12)
            sync.dma_start(out=dbgd["uph"], in_=uph).then_inc(sm.dbgs, 16)
        ndbg[0] += 16

    if "parena" in dbg:
        @blk.sync
        def _(sync):
            sync.wait_ge(sm.bv2, NFC)
            sync.dma_start(out=dbgd["parena"], in_=arena[:, :]).then_inc(sm.dbgs, 16)
        ndbg[0] += 16

    sync_point(2, S2B_PAIRS, t1val=s2v["b_t1"], t2val=s2v["b_t2"])
    v3_s2b = spc["v3"]

    mark("P5")
    @blk.sync
    def _(sync):
        sync.wait_ge(sm.upw0, 16 * 12)
        sync.wait_ge(sm.upw1, 16 * 12)
        for i in range(NFC):
            if i >= 2:
                sync.wait_ge(sm.cv2, i - 1)
            sync.dma_start(out=upsb[:, (i % 2) * T:(i % 2 + 1) * T],
                           in_=bass.AP(uph.tensor, i * 128 * T, [[T, 128], [1, T]])
                           ).then_inc(uprdS[i % 2], 16)

    @blk.vector
    def _(v):
        for jj in range(NFC + 1):
            if jj < NFC:
                i = jj
                sl = PS5(i)
                v.tensor_scalar(kscr[:, (i % 2) * T:(i % 2) * T + HT],
                                arena[:, sl * T:sl * T + HT],
                                svec[:, C_ISG2:C_ISG2 + 1], None, ALU.mult)
                v.drain()
            if jj == 0:
                continue
            i = jj - 1
            r = i % 2
            sl = PS5(i)
            v.wait_ge(sm.cs1, i + 1)
            v.scalar_tensor_tensor(arena[:, sl * T:(sl + 1) * T],
                                   kscr[:, r * T:(r + 1) * T], svec[:, C_SG2:C_SG2 + 1],
                                   upsb[:, r * T:(r + 1) * T], ALU.mult, ALU.mult)
            v.drain()
            v.tensor_reduce(gcol[:, 1:2], arena[:, sl * T:(sl + 1) * T], AX.X,
                            ALU.max, apply_absolute_value=True)
            v.drain()
            v.tensor_tensor(macc[:, M_O:M_O + 1], macc[:, M_O:M_O + 1],
                            gcol[:, 1:2], ALU.max)
            v.drain()
            v.engine_nop().then_inc(sm.cv2, 1)

    @blk.scalar
    def _(s):
        s.wait_ge(sm.sp_v3, v3_s2b)
        for i in range(NFC):
            r = i % 2
            if i >= 2:
                s.wait_ge(sm.cv2, i - 1)
            sl = PS5(i)
            s.activation(kscr[:, r * T + HT:(r + 1) * T], arena[:, sl * T + HT:(sl + 1) * T],
                         AF.Copy, scale=svec[:, C_ISG2:C_ISG2 + 1])
            s.drain()
            s.wait_ge(uprdS[r], 16 * (i // 2 + 1))
            s.activation(kscr3[:, r * T:(r + 1) * T], upsb[:, r * T:(r + 1) * T],
                         AF.Copy, scale=svec[:, C_ISU:C_ISU + 1])
            s.drain()
            s.activation(upsb[:, r * T:(r + 1) * T], kscr3[:, r * T:(r + 1) * T],
                         AF.Copy, scale=svec[:, C_SU:C_SU + 1]).then_inc(sm.cs1, 1)

    if "oarena" in dbg:
        @blk.sync
        def _(sync):
            sync.wait_ge(sm.cv2, NFC)
            sync.dma_start(out=dbgd["oarena"], in_=arena[:, :]).then_inc(sm.dbgs, 16)
        ndbg[0] += 16

    sync_point(3, [(M_O, C_SO, C_ISO)], sm.cv2, NFC)

    if "sc" in dbg:
        @blk.sync
        def _(sync):
            sync.wait_ge(sm.sp_v3, spc["v3"])
            sync.dma_start(out=dbgd["sc"], in_=svec[:, :]).then_inc(sm.dbgs, 16)
        ndbg[0] += 16

    mark("P6")
    right2.close()
    left4b.close()
    right3 = ExitStack()
    wdn = right3.enter_context(nc.sbuf_tensor("wdn", [128, NFC * D], BF16, side="right"))
    left5 = ExitStack()
    eL5 = left5.enter_context
    f4r = eL5(nc.sbuf_tensor("f4r", [128, 2 * D], F32, side="left"))
    x2rr = eL5(nc.sbuf_tensor("x2rr", [128, 2 * D], F32, side="left"))
    orng = eL5(nc.sbuf_tensor("orng", [128, 2 * D], F32, side="left"))

    @blk.gpsimd
    def _(g):
        g.wait_ge(sm.agc, 1)
        g.wait_ge(sm.gm, 48)
        for i in range(NFC):
            g.dma_start(out=wdn[:, i * D:(i + 1) * D].rearrange("p (a b) -> p a b", a=8),
                        in_=bass.AP(agout.tensor, (48 + i) * 128,
                                    [[AGCOLS, 128], [AGW, 8], [1, 128]])
                        ).then_inc(sm.wdl, 16)


    @blk.vector
    def _(v):
        v.wait_ge(sm.sp_v3, spc["v3"])
        for i in range(NFC):
            r = i % 2
            sl = PS5(i)
            v.tensor_scalar(kscr[:, r * T:r * T + HT], arena[:, sl * T:sl * T + HT],
                            svec[:, C_ISO:C_ISO + 1], None, ALU.mult)
            v.drain()
            v.tensor_scalar(arena[:, sl * T:sl * T + HT], kscr[:, r * T:r * T + HT],
                            svec[:, C_SO:C_SO + 1], None, ALU.mult)
            v.drain()
            v.engine_nop().then_inc(sm.ov2, 1)

    @blk.scalar
    def _(s):
        s.wait_ge(sm.sp_v3, spc["v3"])
        for i in range(NFC):
            r = i % 2
            sl = PS5(i)
            s.activation(kscr3[:, r * T:r * T + HT], arena[:, sl * T + HT:(sl + 1) * T],
                         AF.Copy, scale=svec[:, C_ISO:C_ISO + 1])
            s.drain()
            s.activation(arena[:, sl * T + HT:(sl + 1) * T], kscr3[:, r * T:r * T + HT],
                         AF.Copy, scale=svec[:, C_SO:C_SO + 1]).then_inc(sm.os1, 1)

    @blk.tensor
    def _(t):
        t.wait_ge(sm.wdl, 16 * NFC)
        for ti in range(NTR):
            r = ti % 2
            if ti >= 2:
                t.wait_ge(sm.dlv1, ti - 1)
                t.wait_ge(sm.dls2, ti - 1)
            for i in range(NFC):
                if ti == 0:
                    t.wait_ge(sm.os1, i + 1)
                    t.wait_ge(sm.ov2, i + 1)
                sl = PS5(i)
                for half in range(2):
                    ins_ = t.matmul(pS[:, r * 1024 + half * 512:r * 1024 + (half + 1) * 512],
                                    arena[:, sl * T + ti * 128:sl * T + (ti + 1) * 128],
                                    wdn[:, i * D + half * 512:i * D + (half + 1) * 512],
                                    start=(i == 0), stop=(i == NFC - 1))
            ins_.then_inc(sm.dm, 1)

    @blk.sync
    def _(sync):
        sync.wait_ge(sm.cv2, NFC)
        for ti in range(NTR):
            if ti >= 2:
                sync.wait_ge(sm.oad, ti - 1)
            sync.dma_start(out=x2rr[:, (ti % 2) * D:(ti % 2 + 1) * D],
                           in_=x2h[ti * 128:(ti + 1) * 128, :]).then_inc(x2rdS[ti % 2], 16)
            if ti >= 1:
                sync.wait_ge(sm.oad, ti)
                sync.dma_start(out=out[(ti - 1) * 128:ti * 128, :],
                               in_=orng[:, ((ti - 1) % 2) * D:((ti - 1) % 2 + 1) * D]).then_inc(ostS[(ti - 1) % 2], 16)
        sync.wait_ge(sm.oad, NTR)
        sync.dma_start(out=out[(NTR - 1) * 128:NTR * 128, :],
                       in_=orng[:, ((NTR - 1) % 2) * D:((NTR - 1) % 2 + 1) * D]).then_inc(ostS[(NTR - 1) % 2], 16)
        sync.wait_ge(sm.ost0, 16 * 8)
        sync.wait_ge(sm.ost1, 16 * 8)
        if ndbg[0]:
            sync.wait_ge(sm.dbgs, ndbg[0])

    @blk.vector
    def _(v):
        for t in range(NTR + 2):
            if t < NTR:
                r = t % 2
                st = stats[:, r * 14:r * 14 + 14]
                v.wait_ge(sm.dm, t + 1)
                if t >= 2:
                    v.wait_ge(sm.dlv2, t - 1)
                srcp = pS[:, r * 1024:(r + 1) * 1024]
                v.bn_stats(st[:, 0:6], srcp[:, 0:512])
                v.bn_stats(st[:, 6:12], srcp[:, 512:1024])
                v.drain()
                v.bn_aggr(st[:, 12:14], st[:, 0:12])
                v.drain()
                v.engine_nop().then_inc(sm.dlv1, 1)
            if 1 <= t <= NTR:
                u = t - 1
                r = u % 2
                st = stats[:, r * 14:r * 14 + 14]
                v.wait_ge(sm.dls1, u + 1)
                v.scalar_tensor_tensor(lncol[:, r * 6 + 1:r * 6 + 2], st[:, 12:13],
                                       -1.0, lncol[:, r * 6:r * 6 + 1], ALU.mult, ALU.mult)
                v.drain()
                v.engine_nop().then_inc(sm.dlv2, 1)
            if t >= 2:
                u = t - 2
                r = u % 2
                v.wait_ge(sm.dls2, u + 1)
                v.wait_ge(x2rdS[r], 16 * (u // 2 + 1))
                if u >= 2:
                    v.wait_ge(ostS[r], 16 * ((u - 2) // 2 + 1))
                v.tensor_tensor(orng[:, r * D:(r + 1) * D], f4r[:, r * D:(r + 1) * D],
                                x2rr[:, r * D:(r + 1) * D], ALU.add)
                v.drain()
                v.engine_nop().then_inc(sm.oad, 1)

    @blk.scalar
    def _(s):
        for t in range(NTR + 1):
            if t < NTR:
                r = t % 2
                st = stats[:, r * 14:r * 14 + 14]
                s.wait_ge(sm.dlv1, t + 1)
                s.activation(lncol[:, r * 6 + 2:r * 6 + 3], st[:, 13:14], AF.Ln, bias=cstv[:, 0:1])
                s.drain()
                s.activation(lncol[:, r * 6:r * 6 + 1], lncol[:, r * 6 + 2:r * 6 + 3],
                             AF.Exp, scale=-0.5).then_inc(sm.dls1, 1)
            if 1 <= t <= NTR:
                u = t - 1
                r = u % 2
                s.wait_ge(sm.dlv2, u + 1)
                if u >= 2:
                    s.wait_ge(sm.oad, u - 1)
                s.activation(f4r[:, r * D:(r + 1) * D], pS[:, r * 1024:(r + 1) * 1024],
                             AF.Identity, bias=lncol[:, r * 6 + 1:r * 6 + 2],
                             scale=lncol[:, r * 6:r * 6 + 1]).then_inc(sm.dls2, 1)

    mark("P7")
    left5.close()
    left4.close()
    right3.close()
    top.close()
    return nc, dbgd


def _host_prepare(L0_attn_x, L0_attn_Wqkv, L0_attn_bqkv, L0_attn_Wo, L0_attn_bo,
                  L0_ffn_w_gate, L0_ffn_w_up, L0_ffn_w_down):
    x = np.asarray(L0_attn_x, dtype=np.float32)
    Wqkv = np.asarray(L0_attn_Wqkv, dtype=np.float32)
    bqkv = np.asarray(L0_attn_bqkv, dtype=np.float32)
    Wo = np.ascontiguousarray(np.asarray(L0_attn_Wo, dtype=np.float32))
    bo = np.asarray(L0_attn_bo, dtype=np.float32)
    w_gate = np.asarray(L0_ffn_w_gate, dtype=np.float32)
    w_up = np.asarray(L0_ffn_w_up, dtype=np.float32)
    w_down = np.asarray(L0_ffn_w_down, dtype=np.float32)

    qcols = np.concatenate([np.arange(192 * h, 192 * h + 64) for h in range(H)])
    kcols = np.concatenate([np.arange(192 * h + 64, 192 * h + 128) for h in range(H)])
    vcols = np.concatenate([np.arange(192 * h + 128, 192 * h + 192) for h in range(H)])
    wqk = np.ascontiguousarray(Wqkv[:, np.concatenate([qcols, kcols])])
    wv = np.ascontiguousarray(Wqkv[:, vcols])
    bqk = bqkv[np.concatenate([qcols, kcols])]
    bqkT = np.ascontiguousarray(bqk.reshape(16, 128).T)
    bv = np.ascontiguousarray(bqkv[vcols]).reshape(1, D)
    identb = np.eye(128, dtype=BF)
    identf = np.eye(128, dtype=np.float32)

    in_maps = []
    for c in range(NCORE):
        b, hh = c // 2, c % 2
        xpad = np.zeros((TP, D), np.float32)
        xpad[PAD:PAD + T] = x[b, 2048 * hh:2048 * hh + T]
        if hh == 0:
            xpad[PAD + T:PAD + T + 16] = x[b, 2048:2064]
        else:
            xpad[0:PAD] = x[b, 2032:2048]
        m = np.zeros((KB, NBLK * QB), np.float32)
        k_ = np.arange(KB)[:, None]
        q_ = np.arange(QB)[None, :]
        band = (k_ >= q_) & (k_ <= q_ + 2 * W)
        for blki in range(NBLK):
            pk = blki * QB + k_
            gk = 2048 * hh + pk - PAD
            valid = band & (gk >= 0) & (gk < S)
            qreal = (blki * QB + q_) < T
            mm = (valid & qreal).astype(np.float32)
            mm[0, ~qreal[0, :]] = 1.0
            m[:, blki * QB:(blki + 1) * QB] = mm
        in_maps.append(dict(
            xp=xpad, wqk=wqk, wvw=wv, bqkT=bqkT, bvin=bv, woin=Wo,
            boin=bo.reshape(1, D),
            wup_s=np.ascontiguousarray(w_up[UPS * c:UPS * (c + 1)]),
            wgate_s=np.ascontiguousarray(w_gate[UPS * c:UPS * (c + 1)]),
            wdown_s=np.ascontiguousarray(w_down[DNS * c:DNS * (c + 1)]),
            bandmask=m.astype(BF), idb=identb, idf=identf,
        ))
    return in_maps


_BUILT = [None]


def kernel(**inputs):
    in_maps = _host_prepare(**inputs)
    if _BUILT[0] is None:
        _BUILT[0] = build()[0]
    nc = _BUILT[0]
    res = run_bass_kernel_spmd(nc, in_maps, core_ids=list(range(NCORE)))
    y = np.zeros((B, S, D), np.float32)
    for c in range(NCORE):
        b, hh = c // 2, c % 2
        y[b, 2048 * hh:2048 * hh + T] = res.results[c]["out"]
    return y


# version 21
# speedup vs baseline: 1.0684x; 1.0623x over previous
import numpy as np
import ml_dtypes
from contextlib import ExitStack

import concourse.bass as bass
import concourse.mybir as mybir
from concourse.bass_utils import run_bass_kernel_spmd

F32 = mybir.dt.float32
BF16 = mybir.dt.bfloat16
I32 = mybir.dt.int32
I16 = mybir.dt.int16
U16 = mybir.dt.uint16
AF = mybir.ActivationFunctionType
ALU = mybir.AluOpType
AX = mybir.AxisListType
BF = ml_dtypes.bfloat16

B, S, D = 4, 4096, 1024
H, HD = 16, 64
W = 16
NCORE = 8
T = 2048
PAD = 16
TP = 2176
NT = TP // 128
NTR = T // 128
QB, KB = 96, 128
NBLK = 22
TQ = QB * NBLK
FF = 3072
NFC = FF // 128
KD = D // 128
UPS = FF // NCORE
DNS = D // NCORE
AGCOLS = 9216
AGW = 128 * AGCOLS
EXPB = -4.0
QA = 32767.0
EPS = 1e-5
TGS = [512, 512, 512, 512, 128]
TGO = [0, 512, 1024, 1536, 2048]
NTG = 5
NARE = 25
C_SX, C_ISX, C_SG, C_ISG, C_SU, C_ISU, C_SG2, C_ISG2, C_SO, C_ISO = range(10)
C_SGQ, C_SGU = 10, 11
M_X, M_G, M_U, M_G2, M_O = range(5)
SHUF = [4 * (h % 4) + h // 4 for h in range(H)]


def build(dbg=()):
    nc = bass.Bass("TRN2", target_bir_lowering=False)
    top = ExitStack()
    ap = lambda t: t.ap()

    xp = ap(nc.dram_tensor("xp", [TP, D], F32, kind="ExternalInput"))
    wqk = ap(nc.dram_tensor("wqk", [D, 2 * D], F32, kind="ExternalInput"))
    wvw = ap(nc.dram_tensor("wvw", [D, D], F32, kind="ExternalInput"))
    bqkT = ap(nc.dram_tensor("bqkT", [128, 16], F32, kind="ExternalInput"))
    bvin = ap(nc.dram_tensor("bvin", [1, D], F32, kind="ExternalInput"))
    woin = ap(nc.dram_tensor("woin", [D, D], F32, kind="ExternalInput"))
    boin = ap(nc.dram_tensor("boin", [1, D], F32, kind="ExternalInput"))
    wup_s = ap(nc.dram_tensor("wup_s", [UPS, D], F32, kind="ExternalInput"))
    wgate_s = ap(nc.dram_tensor("wgate_s", [UPS, D], F32, kind="ExternalInput"))
    wdown_s = ap(nc.dram_tensor("wdown_s", [DNS, FF], F32, kind="ExternalInput"))
    bandmask = ap(nc.dram_tensor("bandmask", [KB, NBLK * QB], BF16, kind="ExternalInput"))
    idb = ap(nc.dram_tensor("idb", [128, 128], BF16, kind="ExternalInput"))
    idf = ap(nc.dram_tensor("idf", [128, 128], F32, kind="ExternalInput"))
    out = ap(nc.dram_tensor("out", [T, D], F32, kind="ExternalOutput"))

    qh = ap(nc.dram_tensor("qh", [128, KD * TP], BF16))
    kh = ap(nc.dram_tensor("kh", [128, KD * TP], BF16))
    vh = ap(nc.dram_tensor("vh", [TP, H * 65], BF16))
    x2h = ap(nc.dram_tensor("x2h", [T, D], F32))
    uph = ap(nc.dram_tensor("uph", [NFC, 128 * T], BF16))
    agin = ap(nc.dram_tensor("agin", [1, AGW], BF16))
    agout = ap(nc.dram_tensor("agout", [NCORE, AGW], BF16))
    cc = [(ap(nc.dram_tensor(f"ccin{i}", [1, 2], F32)),
           ap(nc.dram_tensor(f"ccout{i}", [1, 2], F32))) for i in range(4)]
    ccw = (ap(nc.dram_tensor("ccwin", [1, 2], F32)),
           ap(nc.dram_tensor("ccwout", [1, 2], F32)))

    dbgd = {}
    def mkdbg(name, shape, dt):
        if name in dbg:
            dbgd[name] = ap(nc.dram_tensor("dbg_" + name, shape, dt, kind="ExternalOutput"))
    mkdbg("qh", [128, KD * TP], BF16)
    mkdbg("kh", [128, KD * TP], BF16)
    mkdbg("vh", [TP, H * 65], BF16)
    mkdbg("ag", [NCORE, AGW], BF16)
    mkdbg("attnoT", [128, KD * TQ], BF16)
    mkdbg("x2", [T, D], F32)
    mkdbg("xqT", [128, KD * T], BF16)
    mkdbg("sc", [128, 16], F32)
    mkdbg("arena", [128, NARE * T], BF16)
    mkdbg("parena", [128, NARE * T], BF16)
    mkdbg("oarena", [128, NARE * T], BF16)
    mkdbg("uph", [NFC, 128 * T], BF16)

    eL = top.enter_context
    sbL = lambda n_, s_, d_=BF16: eL(nc.sbuf_tensor(n_, s_, d_, side="left"))
    idb_s = sbL("idb_s", [128, 128])
    idf_s = sbL("idf_s", [128, 128], F32)
    bqkT_s = sbL("bqkT_s", [128, 16], F32)
    bv_s = sbL("bv_s", [1, D])
    bo_s = sbL("bo_s", [1, D])
    onesb = sbL("onesb", [1, 512])
    onesf = sbL("onesf", [1, 128], F32)
    svec = sbL("svec", [128, 16], F32)
    sscal = sbL("sscal", [1, 16], F32)
    stats = sbL("stats", [128, 4 * 14], F32)
    lncol = sbL("lncol", [128, 2 * 6], F32)
    macc = sbL("macc", [128, 8], F32)
    gcol = sbL("gcol", [128, 4], F32)
    cstv = sbL("cstv", [128, 2], F32)
    masks = sbL("masks", [KB, NBLK * QB])

    pS = eL(nc.psum_tensor("pS", [128, 2048], F32))
    pV = eL(nc.psum_tensor("pV", [128, 1024], F32))
    pT = eL(nc.psum_tensor("pT", [128, 2048], BF16))

    class NS:
        pass
    sm = NS()
    for n_ in ["cst", "cbv", "gld0", "gld1", "wqv", "wqt", "wqe", "agw0", "agw1", "agc",
               "xin0", "xin1", "lnv1", "lns1", "lnv2", "lns2", "xtt", "xte",
               "qkl0", "qkl1", "qkl2", "qkl3", "wvl", "qkm", "qke",
               "qss0", "qss1", "vm", "ve", "vss0", "vss1", "vss2", "vss3",
               "abi0", "abi1", "qkb", "exps", "pnv", "ptt", "pte", "pvm", "pve",
               "wol", "xrd0", "xrd1", "wom", "l3v1", "l3s1", "l3v2", "l3s2",
               "x2a", "x2w0", "x2w1", "l2v1", "l2s1", "l2v2", "l2s2", "mxv",
               "sp_t1", "sp_v1", "sp_g", "sp_c", "sp_v2", "sp_t2", "sp_v3",
               "xqk", "xqq", "xqt", "xqe", "qkev",
               "wgl0", "wgl1", "wgl2", "wgl3", "gm", "gev", "gmx",
               "upw0", "upw1",
               "bv1", "bs1", "bv2",
               "uprd0", "uprd1", "cs1", "cv2",
               "wdl", "ov2", "os1", "dm", "dlv1", "dls1", "dlv2", "dls2",
               "x2rd0", "x2rd1", "oad", "ost0", "ost1", "dbgs"]:
        setattr(sm, n_, eL(nc.semaphore(n_)))
    gld = [sm.gld0, sm.gld1]
    agwS = [sm.agw0, sm.agw1]
    xinS = [sm.xin0, sm.xin1]
    qklS = [sm.qkl0, sm.qkl1, sm.qkl2, sm.qkl3]
    qssS = [sm.qss0, sm.qss1]
    vssS = [sm.vss0, sm.vss1, sm.vss2, sm.vss3]
    abiS = [sm.abi0, sm.abi1]
    xrdS = [sm.xrd0, sm.xrd1]
    x2wS = [sm.x2w0, sm.x2w1]
    wglS = [sm.wgl0, sm.wgl1, sm.wgl2, sm.wgl3]
    upwS = [sm.upw0, sm.upw1]
    uprdS = [sm.uprd0, sm.uprd1]
    x2rdS = [sm.x2rd0, sm.x2rd1]
    ostS = [sm.ost0, sm.ost1]
    ndbg = [0]
    marks = []
    def mark(name):
        n = sum(len(b.instructions) for b in nc.m.functions[0].blocks)
        marks.append((name, n))
    build.marks = marks

    blk = eL(nc.Block())


    right0 = ExitStack()
    eR = right0.enter_context
    sbR = lambda n_, s_, d_=BF16: eR(nc.sbuf_tensor(n_, s_, d_, side="right"))
    xlnT = sbR("xlnT", [128, KD * TP])
    xring = sbR("xring", [128, 2 * D], F32)
    lring = sbR("lring", [128, 2 * D])
    wkr = sbR("wkr", [128, 4 * KD * 128])
    wvr = sbR("wvr", [128, KD * D])
    qkst = sbR("qkst", [128, 2 * TP])
    vst_s = sbR("vst_s", [128, 4 * H * 65])
    r1 = ExitStack()
    eR1 = r1.enter_context
    sbR1 = lambda n_, s_, d_=BF16: eR1(nc.sbuf_tensor(n_, s_, d_, side="right"))
    wqr = sbR1("wqr", [128, 2 * 1024], F32)
    wqq = sbR1("wqq", [128, 9 * 1024])
    agst = sbR1("agst", [128, 2 * 1024])
    wqs = sbR1("wqs", [128, 2 * 64], F32)
    kq1 = sbR1("kq1", [128, 2 * 1024], I32)

    @blk.sync
    def _(sync):
        sync.dma_start(out=idb_s[:, :], in_=idb).then_inc(sm.cst, 16)
        sync.dma_start(out=idf_s[:, :], in_=idf).then_inc(sm.cst, 16)
        sync.dma_start(out=bqkT_s[:, :], in_=bqkT).then_inc(sm.cst, 16)
        sync.dma_start(out=masks[:, :], in_=bandmask).then_inc(sm.cst, 16)

    @blk.gpsimd
    def _(g):
        g.memset(onesb[:, :], 1.0)
        g.memset(onesf[:, :], 1.0)
        g.memset(macc[:, :], 0.0)
        g.memset(sscal[:, :], 0.0)
        g.memset(cstv[:, 0:1], EPS)
        g.memset(cstv[:, 1:2], EXPB)
        g.memset(vst_s.rearrange("p (s h c) -> p s h c", s=4, c=65)[:, :, :, 64:65], 1.0)
        g.engine_nop().then_inc(sm.cst, 1)
        g.dma_start(out=bv_s[:, :], in_=bvin).then_inc(sm.cbv, 16)
        g.dma_start(out=bo_s[:, :], in_=boin).then_inc(sm.cbv, 16)
        for gi in range(4):
            g.dma_start(out=wkr[:, gi * (KD * 128):(gi + 1) * (KD * 128)].rearrange("p (a b) -> p a b", a=KD),
                        in_=wqk.rearrange("(a p) b -> p a b", p=128)[:, :, gi * 128:(gi + 1) * 128]
                        ).then_inc(qklS[gi], 16)
        for ti in range(2):
            g.dma_start(out=wqr[:, ti * 1024:(ti + 1) * 1024],
                        in_=wup_s[ti * 128:(ti + 1) * 128, :]).then_inc(gld[ti], 16)

    mark("Pm1")
    @blk.sync
    def _(sync):
        for ti in range(NT):
            if ti >= 2:
                sync.wait_ge(sm.lns2, ti - 1)
            sync.dma_start(out=xring[:, (ti % 2) * D:(ti % 2 + 1) * D],
                           in_=xp[ti * 128:(ti + 1) * 128, :]).then_inc(xinS[ti % 2], 16)

    def _ln_evac(v, ti):
        r = ti % 2
        v.wait_ge(sm.xtt, ti + 1)
        v.tensor_copy(xlnT.rearrange("p (a b) -> p a b", a=KD)[:, :, ti * 128:(ti + 1) * 128],
                      pT[:, r * 1024:(r + 1) * 1024].rearrange("p (a b) -> p a b", a=KD))
        v.drain()
        v.engine_nop().then_inc(sm.xte, 1)

    @blk.vector
    def _(v):
        for ti in range(NT):
            r = ti % 2
            st = stats[:, r * 14:r * 14 + 14]
            v.wait_ge(xinS[ti % 2], 16 * (ti // 2 + 1))
            xt_ = xring[:, r * D:(r + 1) * D]
            v.bn_stats(st[:, 0:6], xt_[:, 0:512])
            v.bn_stats(st[:, 6:12], xt_[:, 512:1024])
            v.drain()
            v.bn_aggr(st[:, 12:14], st[:, 0:12])
            v.drain()
            v.engine_nop().then_inc(sm.lnv1, 1)
            v.wait_ge(sm.lns1, ti + 1)
            v.scalar_tensor_tensor(lncol[:, r * 6 + 1:r * 6 + 2], st[:, 12:13],
                                   -1.0, lncol[:, r * 6:r * 6 + 1], ALU.mult, ALU.mult)
            v.drain()
            v.engine_nop().then_inc(sm.lnv2, 1)
            if ti >= 1:
                _ln_evac(v, ti - 1)
        _ln_evac(v, NT - 1)

    @blk.scalar
    def _(s):
        s.wait_ge(sm.cst, 65)
        for ti in range(NT):
            r = ti % 2
            st = stats[:, r * 14:r * 14 + 14]
            s.wait_ge(sm.lnv1, ti + 1)
            s.activation(lncol[:, r * 6 + 2:r * 6 + 3], st[:, 13:14], AF.Ln, bias=cstv[:, 0:1])
            s.drain()
            s.activation(lncol[:, r * 6:r * 6 + 1], lncol[:, r * 6 + 2:r * 6 + 3],
                         AF.Exp, scale=-0.5).then_inc(sm.lns1, 1)
            s.wait_ge(sm.lnv2, ti + 1)
            if ti >= 2:
                s.wait_ge(sm.xtt, ti - 1)
            s.activation(lring[:, r * D:(r + 1) * D], xring[:, r * D:(r + 1) * D],
                         AF.Identity, bias=lncol[:, r * 6 + 1:r * 6 + 2],
                         scale=lncol[:, r * 6:r * 6 + 1]).then_inc(sm.lns2, 1)

    TGTILES = [4, 8, 12, 16, 17]

    def _qk0_tg(t, tg):
        t.wait_ge(sm.xte, TGTILES[tg])
        for k in range(KD):
            wch = wkr[:, k * 128:(k + 1) * 128]
            dst = pS[:, TGO[tg]:TGO[tg] + TGS[tg]] if tg < 4 else pV[:, 0:128]
            ins_ = t.matmul(dst, wch,
                            xlnT.rearrange("p (a b) -> p a b", a=KD)[:, k, TGO[tg]:TGO[tg] + TGS[tg]],
                            start=(k == 0), stop=(k == KD - 1))
        return ins_

    @blk.tensor
    def _(t):
        t.wait_ge(sm.cst, 65)
        for ti in range(NT):
            r = ti % 2
            t.wait_ge(sm.lns2, ti + 1)
            if ti >= 2:
                t.wait_ge(sm.xte, ti - 1)
            for j in range(8):
                ins_ = t.transpose(pT[:, r * 1024 + j * 128:r * 1024 + (j + 1) * 128],
                                   lring[:, r * D + j * 128:r * D + (j + 1) * 128],
                                   idb_s[:, :])
            ins_.then_inc(sm.xtt, 1)
            if ti == 5:
                t.wait_ge(qklS[0], 16)
                _qk0_tg(t, 0)
            elif ti == 9:
                _qk0_tg(t, 1)
            elif ti == 13:
                _qk0_tg(t, 2)

    def _gld_load(g, ti):
        if ti < 3:
            srcw = wup_s[ti * 128:(ti + 1) * 128, :]
        elif ti < 6:
            srcw = wgate_s[(ti - 3) * 128:(ti - 2) * 128, :]
        else:
            srcw = wdown_s[:, (ti - 6) * 1024:(ti - 5) * 1024]
        g.wait_ge(sm.wqv, ti - 1)
        g.dma_start(out=wqr[:, (ti % 2) * 1024:(ti % 2 + 1) * 1024],
                    in_=srcw).then_inc(gld[ti % 2], 16)

    @blk.gpsimd
    def _(g):
        for gi in range(4, 16):
            g.wait_ge(sm.qkm, gi - 3)
            g.dma_start(out=wkr[:, (gi % 4) * (KD * 128):(gi % 4 + 1) * (KD * 128)].rearrange("p (a b) -> p a b", a=KD),
                        in_=wqk.rearrange("(a p) b -> p a b", p=128)[:, :, gi * 128:(gi + 1) * 128]
                        ).then_inc(qklS[gi % 4], 16)
            if 4 <= gi <= 10:
                _gld_load(g, gi - 2)
            if 5 <= gi <= 12:
                ti = gi - 5
                g.wait_ge(sm.wqe, ti + 1)
                g.dma_start(out=bass.AP(agin.tensor, ti * 1024, [[AGCOLS, 128], [1, 1024]]),
                            in_=agst[:, (ti % 2) * 1024:(ti % 2 + 1) * 1024]).then_inc(agwS[ti % 2], 16)
        for ti in range(8, 9):
            g.wait_ge(sm.wqe, ti + 1)
            g.dma_start(out=bass.AP(agin.tensor, ti * 1024, [[AGCOLS, 128], [1, 1024]]),
                        in_=agst[:, (ti % 2) * 1024:(ti % 2 + 1) * 1024]).then_inc(agwS[ti % 2], 16)
        for k in range(KD):
            g.dma_start(out=wvr[:, k * D:(k + 1) * D],
                        in_=wvw[k * 128:(k + 1) * 128, :]).then_inc(sm.wvl, 16)


    @blk.tensor
    def _(t):
        ins_ = _qk0_tg(t, 3)
        ins_ = _qk0_tg(t, 4)
        ins_.then_inc(sm.qkm, 1)
        for gi in range(1, 16):
            t.wait_ge(qklS[gi % 4], 16 * (gi // 4 + 1))
            t.wait_ge(sm.qke, gi * 3)
            t.wait_ge(sm.qkev, gi * 2)
            for k in range(KD):
                wch = wkr[:, (gi % 4) * (KD * 128) + k * 128:(gi % 4) * (KD * 128) + (k + 1) * 128]
                for tg in range(NTG):
                    dst = pS[:, TGO[tg]:TGO[tg] + TGS[tg]] if tg < 4 else pV[:, 0:128]
                    ins_ = t.matmul(dst, wch,
                                    xlnT.rearrange("p (a b) -> p a b", a=KD)[:, k, TGO[tg]:TGO[tg] + TGS[tg]],
                                    start=(k == 0), stop=(k == KD - 1))
            ins_.then_inc(sm.qkm, 1)
            if 2 <= gi <= 10:
                ti = gi - 2
                if ti == 0:
                    t.wait_ge(sm.xte, NT)
                t.wait_ge(sm.wqv, ti + 1)
                if ti >= 2:
                    t.wait_ge(sm.wqe, ti - 1)
                for j in range(8):
                    ins_ = t.transpose(pT[:, (ti % 2) * 1024 + j * 128:(ti % 2) * 1024 + (j + 1) * 128],
                                       wqq[:, ti * 1024 + j * 128:ti * 1024 + (j + 1) * 128],
                                       idb_s[:, :])
                ins_.then_inc(sm.wqt, 1)
        t.wait_ge(sm.cbv, 32)
        t.wait_ge(sm.qke, 16 * 3)
        t.wait_ge(sm.qkev, 16 * 2)
        for ti in range(NT):
            t.wait_ge(sm.wvl, 16 * KD)
            t.wait_ge(sm.ve, ti)
            t.wait_ge(sm.qkm, 16)
            for k in range(KD):
                for half in range(2):
                    t.matmul(pV[:, half * 512:(half + 1) * 512],
                             xlnT.rearrange("p (a b) -> p a b", a=KD)[:, k, ti * 128:(ti + 1) * 128],
                             wvr[:, k * D + half * 512:k * D + (half + 1) * 512],
                             start=(k == 0), stop=False)
            for half in range(2):
                ins_ = t.matmul(pV[:, half * 512:(half + 1) * 512],
                                onesb[:, 0:128], bv_s[:, half * 512:(half + 1) * 512],
                                start=False, stop=True)
            ins_.then_inc(sm.vm, 1)

    @blk.scalar
    def _(s):
        for gi in range(16):
            r = gi % 2
            if gi >= 2:
                s.wait_ge(qssS[gi % 2], 16 * ((gi - 2) // 2 + 1))
            s.wait_ge(sm.qkm, gi + 1)
            for tg in range(2, NTG):
                src = pS[:, TGO[tg]:TGO[tg] + TGS[tg]] if tg < 4 else pV[:, 0:128]
                s.activation(qkst[:, r * TP + TGO[tg]:r * TP + TGO[tg] + TGS[tg]],
                             src, AF.Identity, bias=bqkT_s[:, gi:gi + 1]).then_inc(sm.qke, 1)
            if 2 <= gi <= 10:
                ti = gi - 2
                s.wait_ge(sm.wqt, ti + 1)
                if ti >= 2:
                    s.wait_ge(agwS[ti % 2], 16 * ((ti - 2) // 2 + 1))
                s.activation(agst[:, (ti % 2) * 1024:(ti % 2 + 1) * 1024],
                             pT[:, (ti % 2) * 1024:(ti % 2 + 1) * 1024],
                             AF.Copy).then_inc(sm.wqe, 1)

    @blk.vector
    def _(v):
        for gi in range(16):
            r = gi % 2
            if gi >= 2:
                v.wait_ge(qssS[gi % 2], 16 * ((gi - 2) // 2 + 1))
            v.wait_ge(sm.qkm, gi + 1)
            for tg in range(2):
                v.tensor_scalar(qkst[:, r * TP + TGO[tg]:r * TP + TGO[tg] + TGS[tg]],
                                pS[:, TGO[tg]:TGO[tg] + TGS[tg]],
                                bqkT_s[:, gi:gi + 1], None, ALU.add)
                v.drain()
                v.engine_nop().then_inc(sm.qkev, 1)
            if gi < 9:
                ti = gi
                v.wait_ge(gld[ti % 2], 16 * (ti // 2 + 1))
                srcw = wqr[:, (ti % 2) * 1024:(ti % 2 + 1) * 1024]
                bm = wqs[:, (ti % 2) * 64:(ti % 2) * 64 + 32]
                iv = wqs[:, (ti % 2) * 64 + 32:(ti % 2) * 64 + 64]
                v.tensor_reduce(bm, srcw.rearrange("p (a b) -> p a b", b=32),
                                AX.X, ALU.max, apply_absolute_value=True)
                v.drain()
                v.tensor_scalar(bm, bm, 1.0 / 7.0, 1e-12, ALU.mult, ALU.max)
                v.drain()
                v.reciprocal(iv, bm)
                v.drain()
                v.tensor_tensor(kq1[:, (ti % 2) * 1024:(ti % 2 + 1) * 1024], srcw,
                                iv.rearrange("p a -> p a ()").broadcast_to([128, 32, 32]),
                                ALU.mult)
                v.drain()
                v.tensor_tensor(wqq[:, ti * 1024:(ti + 1) * 1024],
                                kq1[:, (ti % 2) * 1024:(ti % 2 + 1) * 1024],
                                bm.rearrange("p a -> p a ()").broadcast_to([128, 32, 32]),
                                ALU.mult)
                v.drain()
                v.engine_nop().then_inc(sm.wqv, 1)

    @blk.sync
    def _(sync):
        for gi in range(16):
            r = gi % 2
            sync.wait_ge(sm.qke, gi * 3 + 3)
            sync.wait_ge(sm.qkev, gi * 2 + 2)
            dst = qh if gi < KD else kh
            gl = gi % KD
            sync.dma_start(out=bass.AP(dst.tensor, gl * TP, [[KD * TP, 128], [1, TP]]),
                           in_=qkst[:, r * TP:(r + 1) * TP]).then_inc(qssS[gi % 2], 16)
        sync.wait_ge(sm.cst, 65)
        for ti in range(NT):
            sync.wait_ge(sm.ve, ti + 1)
            sync.dma_start(out=vh[ti * 128:(ti + 1) * 128, :],
                           in_=vst_s[:, (ti % 4) * (H * 65):(ti % 4 + 1) * (H * 65)]).then_inc(vssS[ti % 4], 16)

    @blk.vector
    def _(v):
        VW = H * 65
        for ti in range(NT):
            r = ti % 4
            v.wait_ge(sm.vm, ti + 1)
            if ti >= 4:
                v.wait_ge(vssS[(ti - 4) % 4], 16 * ((ti - 4) // 4 + 1))
            v.tensor_copy(vst_s[:, r * VW:(r + 1) * VW].rearrange("p (h c) -> p h c", c=65)[:, :, 0:64],
                          pV[:, 0:1024].rearrange("p (h c) -> p h c", c=64))
            v.drain()
            v.engine_nop().then_inc(sm.ve, 1)

    for nm_, src_ in (("qh", qh), ("kh", kh), ("vh", vh)):
        if nm_ in dbg:
            @blk.sync
            def _(sync, nm_=nm_, src_=src_):
                sync.wait_ge(sm.qss0, 16 * 8)
                sync.wait_ge(sm.qss1, 16 * 8)
                sync.wait_ge(sm.vss0, 16 * 5)
                sync.wait_ge(sm.vss1, 16 * 4)
                sync.wait_ge(sm.vss2, 16 * 4)
                sync.wait_ge(sm.vss3, 16 * 4)
                sync.dma_start(out=dbgd[nm_], in_=src_).then_inc(sm.dbgs, 16)
            ndbg[0] += 16
    if "ag" in dbg:
        @blk.sync
        def _(sync):
            sync.wait_ge(sm.agc, 1)
            sync.dma_start(out=dbgd["ag"], in_=agout).then_inc(sm.dbgs, 16)
        ndbg[0] += 16

    r1.close()
    right0.close()
    right1 = ExitStack()
    eR = right1.enter_context
    sbR = lambda n_, s_, d_=BF16: eR(nc.sbuf_tensor(n_, s_, d_, side="right"))
    qring = sbR("qring", [128, 2 * KD * QB])
    kring = sbR("kring", [128, 2 * KD * KB])
    vring = sbR("vring", [128, 2 * H * 65])
    prb = sbR("prb", [128, 2 * H * QB])
    atm = sbR("atm", [128, 2 * H * 64])
    rcp = sbR("rcp", [128, 4 * 16], F32)

    left1 = ExitStack()
    xln2 = left1.enter_context(nc.sbuf_tensor("xln2", [128, NTR * D], BF16, side="left"))
    left2 = ExitStack()
    eL2 = left2.enter_context
    attnoT = eL2(nc.sbuf_tensor("attnoT", [128, KD * TQ], BF16, side="left"))
    wor = eL2(nc.sbuf_tensor("wor", [128, KD * D], BF16, side="left"))
    f3r = eL2(nc.sbuf_tensor("f3r", [128, 2 * D], F32, side="left"))
    x2r = eL2(nc.sbuf_tensor("x2r", [128, 2 * D], F32, side="left"))

    @blk.sync
    def _(sync):
        sync.wait_ge(sm.qss0, 16 * 8)
        sync.wait_ge(sm.qss1, 16 * 8)
        vm_seen = -1
        for b in range(NBLK):
            r = b % 2
            if b >= 2:
                sync.wait_ge(sm.qkb, b - 1)
                sync.wait_ge(sm.pvm, b - 1)
            m_ = min((QB * b + KB - 1) // 128, NT - 1)
            while vm_seen < m_:
                vm_seen += 1
                sync.wait_ge(vssS[vm_seen % 4], 16 * (vm_seen // 4 + 1))
            sync.dma_start(out=qring[:, r * KD * QB:(r + 1) * KD * QB].rearrange("p (a b) -> p a b", a=KD),
                           in_=bass.AP(qh.tensor, QB * b + PAD,
                                       [[KD * TP, 128], [TP, KD], [1, QB]])).then_inc(abiS[b % 2], 16)
            sync.dma_start(out=kring[:, r * KD * KB:(r + 1) * KD * KB].rearrange("p (a b) -> p a b", a=KD),
                           in_=bass.AP(kh.tensor, QB * b,
                                       [[KD * TP, 128], [TP, KD], [1, KB]])).then_inc(abiS[b % 2], 16)
            sync.dma_start(out=vring[:, r * (H * 65):(r + 1) * (H * 65)],
                           in_=vh[QB * b:QB * b + KB, :]).then_inc(abiS[b % 2], 16)

    SC = [512 * (h % 3) + 96 * (h // 3) if h < 15 else 1536 for h in range(H)]
    PC = [480 * (h % 3) + 96 * (h // 3) if h < 15 else 1440 for h in range(H)]
    PVSEQ = "ABABABCABABABCAB"
    _pvslot = {}
    _cnt = {"A": 0, "B": 0, "C": 0}
    for _h, _rg in enumerate(PVSEQ):
        _pvslot[_h] = (_rg, _cnt[_rg])
        _cnt[_rg] += 1

    def _pvc(h):
        rg, sl = _pvslot[h]
        if rg == "A":
            return pV, 65 * sl
        if rg == "B":
            return pV, 512 + 65 * sl
        return pS, 1664 + 65 * sl
    STIX = {h: {"A": 0, "B": 7, "C": 14}[rg] + sl for h, (rg, sl) in _pvslot.items()}
    PVW = H * 65

    @blk.tensor
    def _(t):
        t.wait_ge(sm.qke, 16 * 3)
        t.wait_ge(sm.qkev, 16 * 2)
        for i in range(NBLK + 2):
            if i < NBLK:
                b = i
                r = b % 2
                t.wait_ge(abiS[r], 48 * (b // 2 + 1))
                if b >= 1:
                    t.wait_ge(sm.exps, b)
                for h in range(H):
                    pp = 64 * (h % 2)
                    g = h // 2
                    ins_ = t.matmul(pS[:, SC[h]:SC[h] + QB],
                                    kring[pp:pp + 64, r * KD * KB + g * KB:r * KD * KB + (g + 1) * KB],
                                    qring[pp:pp + 64, r * KD * QB + g * QB:r * KD * QB + (g + 1) * QB],
                                    start=True, stop=True)
                ins_.then_inc(sm.qkb, 1)
            if 1 <= i <= NBLK:
                b = i - 1
                r = b % 2
                t.wait_ge(sm.pnv, b + 1)
                if b >= 1:
                    t.wait_ge(sm.pve, b)
                for h in range(H):
                    pt_, pc_ = _pvc(h)
                    ins_ = t.matmul(pt_[0:QB, pc_:pc_ + 65],
                                    prb[:, r * (H * QB) + PC[h]:r * (H * QB) + PC[h] + QB],
                                    vring[:, r * PVW + h * 65:r * PVW + (h + 1) * 65],
                                    start=True, stop=True)
                ins_.then_inc(sm.pvm, 1)
            if i >= 2:
                b = i - 2
                r = b % 2
                t.wait_ge(sm.pve, b + 1)
                if b >= 1:
                    t.wait_ge(sm.pte, b)
                for h in range(H):
                    ac = r * (H * 64) + STIX[h] * 64
                    ins_ = t.transpose(pT[64 * (h % 2):64 * (h % 2) + 64, (h // 2) * QB:(h // 2 + 1) * QB],
                                       atm[0:QB, ac:ac + 64],
                                       idb_s[0:QB, 0:QB],
                                       tile_position=(0, 64 * (h % 2)))
                ins_.then_inc(sm.ptt, 1)

    @blk.scalar
    def _(s):
        for b in range(NBLK):
            r = b % 2
            s.wait_ge(sm.qkb, b + 1)
            if b >= 2:
                s.wait_ge(sm.pvm, b - 1)
            for bk in range(3):
                s.activation(prb[:, r * (H * QB) + 480 * bk:r * (H * QB) + 480 * (bk + 1)],
                             pS[:, 512 * bk:512 * bk + 480],
                             AF.Exp, scale=0.125, bias=cstv[:, 1:2])
            s.activation(prb[:, r * (H * QB) + 1440:r * (H * QB) + 1536],
                         pS[:, 1536:1632],
                         AF.Exp, scale=0.125, bias=cstv[:, 1:2]).then_inc(sm.exps, 1)

    @blk.vector
    def _(v):
        v.wait_ge(sm.cst, 65)
        for i in range(NBLK + 2):
            if i < NBLK:
                b = i
                r = b % 2
                v.wait_ge(sm.exps, b + 1)
                v.tensor_tensor(prb[:, r * (H * QB):(r + 1) * (H * QB)].rearrange("p (a b) -> p a b", a=H),
                                prb[:, r * (H * QB):(r + 1) * (H * QB)].rearrange("p (a b) -> p a b", a=H),
                                masks[:, b * QB:(b + 1) * QB].rearrange("p b -> p () b").broadcast_to([KB, H, QB]),
                                ALU.mult)
                v.drain()
                v.engine_nop().then_inc(sm.pnv, 1)
            if 1 <= i <= NBLK:
                b = i - 1
                r = b % 2
                v.wait_ge(sm.pvm, b + 1)
                sg_ = rcp[0:QB, 32 + r * 16:32 + r * 16 + 16]
                v.tensor_copy(sg_[:, 0:7].rearrange("p (h c) -> p h c", c=1),
                              pV[0:QB, 0:455].rearrange("p (h c) -> p h c", c=65)[:, :, 64:65])
                v.tensor_copy(sg_[:, 7:14].rearrange("p (h c) -> p h c", c=1),
                              pV[0:QB, 512:967].rearrange("p (h c) -> p h c", c=65)[:, :, 64:65])
                v.tensor_copy(sg_[:, 14:16].rearrange("p (h c) -> p h c", c=1),
                              pS[0:QB, 1664:1794].rearrange("p (h c) -> p h c", c=65)[:, :, 64:65])
                v.drain()
                v.reciprocal(rcp[0:QB, r * 16:r * 16 + 16], sg_)
                v.drain()
                ab = r * (H * 64)
                v.tensor_tensor(atm[0:QB, ab:ab + 448].rearrange("p (h c) -> p h c", c=64),
                                pV[0:QB, 0:455].rearrange("p (h c) -> p h c", c=65)[:, :, 0:64],
                                rcp[0:QB, r * 16:r * 16 + 7].rearrange("p h -> p h ()").broadcast_to([QB, 7, 64]),
                                ALU.mult)
                v.tensor_tensor(atm[0:QB, ab + 448:ab + 896].rearrange("p (h c) -> p h c", c=64),
                                pV[0:QB, 512:967].rearrange("p (h c) -> p h c", c=65)[:, :, 0:64],
                                rcp[0:QB, r * 16 + 7:r * 16 + 14].rearrange("p h -> p h ()").broadcast_to([QB, 7, 64]),
                                ALU.mult)
                v.tensor_tensor(atm[0:QB, ab + 896:ab + 1024].rearrange("p (h c) -> p h c", c=64),
                                pS[0:QB, 1664:1794].rearrange("p (h c) -> p h c", c=65)[:, :, 0:64],
                                rcp[0:QB, r * 16 + 14:r * 16 + 16].rearrange("p h -> p h ()").broadcast_to([QB, 2, 64]),
                                ALU.mult)
                v.drain()
                v.engine_nop().then_inc(sm.pve, 1)
            if i >= 2:
                b = i - 2
                v.wait_ge(sm.ptt, b + 1)
                v.tensor_copy(attnoT.rearrange("p (a b) -> p a b", a=KD)[:, :, b * QB:(b + 1) * QB],
                              pT[:, 0:768].rearrange("p (a b) -> p a b", a=KD))
                v.drain()
                v.engine_nop().then_inc(sm.pte, 1)

    if "attnoT" in dbg:
        @blk.sync
        def _(sync):
            sync.wait_ge(sm.pte, NBLK)
            sync.dma_start(out=dbgd["attnoT"], in_=attnoT[:, :]).then_inc(sm.dbgs, 16)
        ndbg[0] += 16

    @blk.gpsimd
    def _(g):
        g.wait_ge(sm.agw0, 16 * 5)
        g.wait_ge(sm.agw1, 16 * 4)
        for k in range(KD):
            g.dma_start(out=wor[:, k * D:(k + 1) * D],
                        in_=woin[k * 128:(k + 1) * 128, :]).then_inc(sm.wol, 16)
        g.wait_ge(sm.cst, 65)
        g.dma_start(out=ccw[0], in_=sscal[0:1, 0:2]).then_inc(sm.sp_g, 16)
        g.wait_ge(sm.sp_g, 16)
        g.wait_ge(sm.agw0, 16 * 5)
        g.wait_ge(sm.agw1, 16 * 4)
        g.collective_compute("AllReduce", ALU.max,
                             replica_groups=[list(range(NCORE))],
                             ins=[ccw[0]], outs=[ccw[1]]).then_inc(sm.sp_c, 1)
        g.collective_compute("AllGather", ALU.bypass,
                             replica_groups=[list(range(NCORE))],
                             ins=[agin], outs=[agout]).then_inc(sm.agc, 1)

    @blk.sync
    def _(sync):
        for ti in range(NTR):
            if ti >= 2:
                sync.wait_ge(sm.x2a, ti - 1)
            sync.dma_start(out=xring[:, (ti % 2) * D:(ti % 2 + 1) * D],
                           in_=xp[PAD + ti * 128:PAD + (ti + 1) * 128, :]).then_inc(xrdS[ti % 2], 16)
            if ti >= 1:
                sync.wait_ge(sm.x2a, ti)
                sync.dma_start(out=x2h[(ti - 1) * 128:ti * 128, :],
                               in_=x2r[:, ((ti - 1) % 2) * D:((ti - 1) % 2 + 1) * D]).then_inc(x2wS[(ti - 1) % 2], 16)
        sync.wait_ge(sm.x2a, NTR)
        sync.dma_start(out=x2h[(NTR - 1) * 128:NTR * 128, :],
                       in_=x2r[:, ((NTR - 1) % 2) * D:((NTR - 1) % 2 + 1) * D]).then_inc(x2wS[(NTR - 1) % 2], 16)

    @blk.tensor
    def _(t):
        t.wait_ge(sm.wol, 16 * KD)
        t.wait_ge(sm.pte, NBLK)
        t.wait_ge(sm.pve, NBLK)
        t.wait_ge(sm.exps, NBLK)
        for ti in range(NTR):
            r = ti % 2
            if ti >= 2:
                t.wait_ge(sm.l3v1, ti - 1)
                t.wait_ge(sm.l3s2, ti - 1)
            for k in range(KD):
                for half in range(2):
                    t.matmul(pS[:, r * 1024 + half * 512:r * 1024 + (half + 1) * 512],
                             attnoT.rearrange("p (a b) -> p a b", a=KD)[:, k, ti * 128:(ti + 1) * 128],
                             wor[:, k * D + half * 512:k * D + (half + 1) * 512],
                             start=(k == 0), stop=False)
            for half in range(2):
                ins_ = t.matmul(pS[:, r * 1024 + half * 512:r * 1024 + (half + 1) * 512],
                                onesb[:, 0:128], bo_s[:, half * 512:(half + 1) * 512],
                                start=False, stop=True)
            ins_.then_inc(sm.wom, 1)

    @blk.vector
    def _(v):
        for t in range(NTR + 2):
            if t < NTR:
                r = t % 2
                st = stats[:, r * 14:r * 14 + 14]
                v.wait_ge(sm.wom, t + 1)
                if t >= 2:
                    v.wait_ge(sm.l3v2, t - 1)
                srcp = pS[:, r * 1024:(r + 1) * 1024]
                v.bn_stats(st[:, 0:6], srcp[:, 0:512])
                v.bn_stats(st[:, 6:12], srcp[:, 512:1024])
                v.drain()
                v.bn_aggr(st[:, 12:14], st[:, 0:12])
                v.drain()
                v.engine_nop().then_inc(sm.l3v1, 1)
            if 1 <= t <= NTR:
                u = t - 1
                r = u % 2
                st = stats[:, r * 14:r * 14 + 14]
                st2 = stats[:, (2 + r) * 14:(2 + r) * 14 + 14]
                v.wait_ge(sm.l3s1, u + 1)
                v.scalar_tensor_tensor(lncol[:, r * 6 + 1:r * 6 + 2], st[:, 12:13],
                                       -1.0, lncol[:, r * 6:r * 6 + 1], ALU.mult, ALU.mult)
                v.drain()
                v.engine_nop().then_inc(sm.l3v2, 1)
                v.wait_ge(sm.l3s2, u + 1)
                v.wait_ge(xrdS[r], 16 * (u // 2 + 1))
                if u >= 2:
                    v.wait_ge(x2wS[r], 16 * ((u - 2) // 2 + 1))
                    v.wait_ge(sm.l2s2, u - 1)
                v.tensor_tensor(x2r[:, r * D:(r + 1) * D], f3r[:, r * D:(r + 1) * D],
                                xring[:, r * D:(r + 1) * D], ALU.add)
                v.drain()
                v.engine_nop().then_inc(sm.x2a, 1)
                v.bn_stats(st2[:, 0:6], x2r[:, r * D:r * D + 512])
                v.bn_stats(st2[:, 6:12], x2r[:, r * D + 512:r * D + 1024])
                v.drain()
                v.bn_aggr(st2[:, 12:14], st2[:, 0:12])
                v.drain()
                v.engine_nop().then_inc(sm.l2v1, 1)
            if t >= 2:
                u = t - 2
                r = u % 2
                st2 = stats[:, (2 + r) * 14:(2 + r) * 14 + 14]
                v.wait_ge(sm.l2s1, u + 1)
                v.scalar_tensor_tensor(lncol[:, r * 6 + 4:r * 6 + 5], st2[:, 12:13],
                                       -1.0, lncol[:, r * 6 + 3:r * 6 + 4], ALU.mult, ALU.mult)
                v.drain()
                v.engine_nop().then_inc(sm.l2v2, 1)
                v.wait_ge(sm.l2s2, u + 1)
                v.tensor_reduce(gcol[:, 0:1], xln2[:, u * D:(u + 1) * D], AX.X,
                                ALU.max, apply_absolute_value=True)
                v.drain()
                v.tensor_tensor(macc[:, M_X:M_X + 1], macc[:, M_X:M_X + 1],
                                gcol[:, 0:1], ALU.max)
                v.drain()
                v.engine_nop().then_inc(sm.mxv, 1)

    @blk.scalar
    def _(s):
        for t in range(NTR + 2):
            if t < NTR:
                r = t % 2
                st = stats[:, r * 14:r * 14 + 14]
                s.wait_ge(sm.l3v1, t + 1)
                s.activation(lncol[:, r * 6 + 2:r * 6 + 3], st[:, 13:14], AF.Ln, bias=cstv[:, 0:1])
                s.drain()
                s.activation(lncol[:, r * 6:r * 6 + 1], lncol[:, r * 6 + 2:r * 6 + 3],
                             AF.Exp, scale=-0.5).then_inc(sm.l3s1, 1)
            if 1 <= t <= NTR:
                u = t - 1
                r = u % 2
                st2 = stats[:, (2 + r) * 14:(2 + r) * 14 + 14]
                s.wait_ge(sm.l3v2, u + 1)
                if u >= 2:
                    s.wait_ge(sm.x2a, u - 1)
                s.activation(f3r[:, r * D:(r + 1) * D], pS[:, r * 1024:(r + 1) * 1024],
                             AF.Identity, bias=lncol[:, r * 6 + 1:r * 6 + 2],
                             scale=lncol[:, r * 6:r * 6 + 1]).then_inc(sm.l3s2, 1)
                s.wait_ge(sm.l2v1, u + 1)
                s.activation(lncol[:, r * 6 + 5:r * 6 + 6], st2[:, 13:14], AF.Ln, bias=cstv[:, 0:1])
                s.drain()
                s.activation(lncol[:, r * 6 + 3:r * 6 + 4], lncol[:, r * 6 + 5:r * 6 + 6],
                             AF.Exp, scale=-0.5).then_inc(sm.l2s1, 1)
            if t >= 2:
                u = t - 2
                r = u % 2
                s.wait_ge(sm.l2v2, u + 1)
                s.activation(xln2[:, u * D:(u + 1) * D], x2r[:, r * D:(r + 1) * D],
                             AF.Identity, bias=lncol[:, r * 6 + 4:r * 6 + 5],
                             scale=lncol[:, r * 6 + 3:r * 6 + 4]).then_inc(sm.l2s2, 1)

    if "x2" in dbg:
        @blk.sync
        def _(sync):
            sync.wait_ge(sm.x2w0, 16 * 8)
            sync.wait_ge(sm.x2w1, 16 * 8)
            sync.dma_start(out=dbgd["x2"], in_=x2h).then_inc(sm.dbgs, 16)
        ndbg[0] += 16

    spc = dict(t1=0, v1=0, g=16, c=1, v2=0, t2=0, v3=0)

    def sp_t1_emit(t, pairs):
        for i, (mc, _, _) in enumerate(pairs):
            ins_ = t.transpose(pV[0:1, 512 + i * 128:512 + (i + 1) * 128],
                               macc[:, mc:mc + 1], idf_s[:, :])
        spc["t1"] += 1
        return ins_

    def sp_t2_emit(t, pairs):
        lo = min(min(sc_, ic_) for _, sc_, ic_ in pairs)
        hi = max(max(sc_, ic_) for _, sc_, ic_ in pairs) + 1
        ins_ = t.matmul(pV[:, 512 + lo:512 + hi], onesf[:, :], sscal[0:1, lo:hi],
                        start=True, stop=True)
        spc["t2"] += 1
        return ins_

    def sync_point(spi, pairs, wait_sem=None, wait_val=None, t1val=None, t2val=None):
        n = len(pairs)
        cin, cout = cc[spi]
        inline_pe = t1val is not None
        if not inline_pe:
            @blk.tensor
            def _(t):
                t.wait_ge(wait_sem, wait_val)
                ins_ = sp_t1_emit(t, pairs)
                ins_.then_inc(sm.sp_t1, 1)
            t1val = spc["t1"]

        @blk.vector
        def _(v):
            v.wait_ge(sm.sp_t1, t1val)
            for i in range(n):
                v.tensor_reduce(sscal[0:1, 12 + i:13 + i],
                                pV[0:1, 512 + i * 128:512 + (i + 1) * 128],
                                AX.X, ALU.max)
            v.drain()
            v.engine_nop().then_inc(sm.sp_v1, 1)
        spc["v1"] += 1

        @blk.gpsimd
        def _(g):
            g.wait_ge(sm.sp_v1, spc["v1"])
            g.dma_start(out=cin, in_=sscal[0:1, 12:14]).then_inc(sm.sp_g, 16)
            g.wait_ge(sm.sp_g, spc["g"] + 16)
            g.collective_compute("AllReduce", ALU.max,
                                 replica_groups=[list(range(NCORE))],
                                 ins=[cin], outs=[cout]).then_inc(sm.sp_c, 1)
            g.wait_ge(sm.sp_c, spc["c"] + 1)
            g.dma_start(out=sscal[0:1, 12:14], in_=cout).then_inc(sm.sp_g, 16)
        spc["g"] += 32
        spc["c"] += 1

        @blk.vector
        def _(v):
            v.wait_ge(sm.sp_g, spc["g"])
            for i, (_, sc_, ic_) in enumerate(pairs):
                v.tensor_scalar(sscal[0:1, sc_:sc_ + 1], sscal[0:1, 12 + i:13 + i],
                                1.0 / QA, 1e-12, ALU.mult, ALU.max)
                v.drain()
                v.reciprocal(sscal[0:1, ic_:ic_ + 1], sscal[0:1, sc_:sc_ + 1])
                v.drain()
            v.engine_nop().then_inc(sm.sp_v2, 1)
        spc["v2"] += 1

        if not inline_pe:
            @blk.tensor
            def _(t):
                t.wait_ge(sm.sp_v2, spc["v2"])
                ins_ = sp_t2_emit(t, pairs)
                ins_.then_inc(sm.sp_t2, 1)
            t2val = spc["t2"]

        @blk.vector
        def _(v):
            v.wait_ge(sm.sp_t2, t2val)
            lo = min(min(sc_, ic_) for _, sc_, ic_ in pairs)
            hi = max(max(sc_, ic_) for _, sc_, ic_ in pairs) + 1
            v.tensor_copy(svec[:, lo:hi], pV[:, 512 + lo:512 + hi])
            v.drain()
            v.engine_nop().then_inc(sm.sp_v3, 1)
        spc["v3"] += 1

    sync_point(0, [(M_X, C_SX, C_ISX)], sm.mxv, NTR)

    mark("P0")
    right1.close()
    left2.close()
    right2 = ExitStack()
    xqT = right2.enter_context(nc.sbuf_tensor("xqT", [128, KD * T], BF16, side="right"))
    kq3 = right2.enter_context(nc.sbuf_tensor("kq3", [128, 1024], I16, side="right"))
    xqst = right2.enter_context(nc.sbuf_tensor("xqst", [128, 2 * D], BF16, side="right"))

    def _xq_evac(v, ti):
        r = ti % 2
        v.wait_ge(sm.xqt, ti + 1)
        v.tensor_copy(xqT.rearrange("p (a b) -> p a b", a=KD)[:, :, ti * 128:(ti + 1) * 128],
                      pT[:, r * 1024:(r + 1) * 1024].rearrange("p (a b) -> p a b", a=KD))
        v.drain()
        v.engine_nop().then_inc(sm.xqe, 1)

    @blk.vector
    def _(v):
        v.wait_ge(sm.sp_v3, spc["v3"])
        for ti in range(NTR):
            if ti >= 1:
                v.wait_ge(sm.xqq, ti)
            v.tensor_scalar(kq3[:, :], xln2[:, ti * D:(ti + 1) * D],
                            svec[:, C_ISX:C_ISX + 1], None, ALU.mult)
            v.drain()
            v.engine_nop().then_inc(sm.xqk, 1)
            if ti >= 1:
                _xq_evac(v, ti - 1)
        _xq_evac(v, NTR - 1)

    @blk.scalar
    def _(s):
        for ti in range(NTR):
            r = ti % 2
            s.wait_ge(sm.xqk, ti + 1)
            if ti >= 2:
                s.wait_ge(sm.xqt, ti - 1)
            s.activation(xqst[:, r * D:(r + 1) * D], kq3[:, :],
                         AF.Copy, scale=svec[:, C_SX:C_SX + 1]).then_inc(sm.xqq, 1)

    @blk.tensor
    def _(t):
        for ti in range(NTR):
            r = ti % 2
            t.wait_ge(sm.xqq, ti + 1)
            if ti >= 2:
                t.wait_ge(sm.xqe, ti - 1)
            for j in range(8):
                ins_ = t.transpose(pT[:, r * 1024 + j * 128:r * 1024 + (j + 1) * 128],
                                   xqst[:, r * D + j * 128:r * D + (j + 1) * 128],
                                   idb_s[:, :])
            ins_.then_inc(sm.xqt, 1)

    if "xqT" in dbg:
        @blk.sync
        def _(sync):
            sync.wait_ge(sm.xqe, NTR)
            sync.dma_start(out=dbgd["xqT"], in_=xqT[:, :]).then_inc(sm.dbgs, 16)
        ndbg[0] += 16

    mark("P1P2S1")
    left1.close()
    left4 = ExitStack()
    eL4 = left4.enter_context
    arena = eL4(nc.sbuf_tensor("arena", [128, NARE * T], BF16, side="left"))
    kscr = eL4(nc.sbuf_tensor("kscr", [128, 2 * T], I16, side="left"))
    kscr3 = eL4(nc.sbuf_tensor("kscr3", [128, 2 * T], I16, side="left"))
    left4b = ExitStack()
    eL4b = left4b.enter_context
    kscru = eL4b(nc.sbuf_tensor("kscru", [128, 2 * T], U16, side="left"))
    upsb = eL4b(nc.sbuf_tensor("upsb", [128, 2 * T], BF16, side="left"))
    sscr = eL4b(nc.sbuf_tensor("sscr", [128, 2 * T], F32, side="left"))
    wgr = eL4b(nc.sbuf_tensor("wgr", [128, 4 * 1024], BF16, side="left"))

    PS5 = lambda i: (i + NARE - 1) % NARE
    HT = T // 2
    @blk.sync
    def _(sync):
        sync.wait_ge(sm.agc, 1)
        for f in range(48):
            if f >= 4:
                sync.wait_ge(sm.gm, f - 3)
            fl = f % NFC
            s_ = fl // 3
            a_ = fl % 3
            tile = (3 + a_) if f < NFC else a_
            sync.dma_start(out=wgr[:, (f % 4) * 1024:(f % 4 + 1) * 1024],
                           in_=bass.AP(agout.tensor, s_ * AGW + tile * 1024,
                                       [[AGCOLS, 128], [1, 1024]])).then_inc(wglS[f % 4], 16)

    S2A_PAIRS = [(M_G, C_SG, C_ISG)]
    S2B_PAIRS = [(M_U, C_SU, C_ISU), (M_G2, C_SG2, C_ISG2)]
    s2a_v2v = spc["v2"] + 1
    s2b_v2v = spc["v2"] + 2
    s2v = {}

    @blk.tensor
    def _(t):
        t.wait_ge(sm.xqe, NTR)
        for f in range(48):
            t.wait_ge(wglS[f % 4], 16 * (f // 4 + 1))
            for tg in range(4):
                if f >= 1:
                    t.wait_ge(sm.gev, (f - 1) * 4 + tg + 1)
                for k in range(KD):
                    ins_ = t.matmul(pS[:, tg * 512:(tg + 1) * 512],
                                    wgr[:, (f % 4) * 1024 + k * 128:(f % 4) * 1024 + (k + 1) * 128],
                                    xqT.rearrange("p (a b) -> p a b", a=KD)[:, k, tg * 512:(tg + 1) * 512],
                                    start=(k == 0), stop=(k == KD - 1))
            ins_.then_inc(sm.gm, 1)
            if f == NFC - 1:
                t.wait_ge(sm.gmx, NFC)
                sp_t1_emit(t, S2A_PAIRS).then_inc(sm.sp_t1, 1)
                s2v["a_t1"] = spc["t1"]
            elif f == NFC + 2:
                t.wait_ge(sm.sp_v2, s2a_v2v)
                sp_t2_emit(t, S2A_PAIRS).then_inc(sm.sp_t2, 1)
                s2v["a_t2"] = spc["t2"]
        t.wait_ge(sm.gmx, 48)
        t.wait_ge(sm.bv2, NFC)
        sp_t1_emit(t, S2B_PAIRS).then_inc(sm.sp_t1, 1)
        s2v["b_t1"] = spc["t1"]
        t.wait_ge(sm.sp_v2, s2b_v2v)
        sp_t2_emit(t, S2B_PAIRS).then_inc(sm.sp_t2, 1)
        s2v["b_t2"] = spc["t2"]

    @blk.scalar
    def _(s):
        s.wait_ge(sm.x2w0, 16 * 8)
        s.wait_ge(sm.x2w1, 16 * 8)
        for f in range(NFC):
            s.wait_ge(sm.gm, f + 1)
            for tg in range(4):
                s.activation(arena[:, f * T + tg * 512:f * T + (tg + 1) * 512],
                             pS[:, tg * 512:(tg + 1) * 512],
                             AF.Copy).then_inc(sm.gev, 1)

    @blk.vector
    def _(v):
        for f in range(NFC):
            v.wait_ge(sm.gev, f * 4 + 4)
            v.tensor_reduce(gcol[:, 1:2], arena[:, f * T:(f + 1) * T], AX.X,
                            ALU.max, apply_absolute_value=True)
            v.drain()
            v.tensor_tensor(macc[:, M_G:M_G + 1], macc[:, M_G:M_G + 1],
                            gcol[:, 1:2], ALU.max)
            v.drain()
            v.engine_nop().then_inc(sm.gmx, 1)

    sync_point(1, S2A_PAIRS, t1val=s2v["a_t1"], t2val=s2v["a_t2"])
    v3_s2a = spc["v3"]

    @blk.gpsimd
    def _(g):
        for fu in range(NFC):
            g.wait_ge(sm.gmx, NFC + fu + 1)
            g.dma_start(out=bass.AP(uph.tensor, fu * 128 * T, [[T, 128], [1, T]]),
                        in_=upsb[:, (fu % 2) * T:(fu % 2 + 1) * T]).then_inc(upwS[fu % 2], 16)

    mark("P3P4")
    @blk.scalar
    def _(s):
        for jj in range(NFC + 1):
            if jj < NFC:
                j = jj
                s.wait_ge(sm.gm, NFC + j + 1)
                if j >= 2:
                    s.wait_ge(upwS[j % 2], 16 * ((j - 2) // 2 + 1))
                for tg in range(4):
                    s.activation(upsb[:, (j % 2) * T + tg * 512:(j % 2) * T + (tg + 1) * 512],
                                 pS[:, tg * 512:(tg + 1) * 512],
                                 AF.Copy).then_inc(sm.gev, 1)
            if jj == 0:
                continue
            j = jj - 1
            if j == 0:
                s.wait_ge(sm.sp_v3, v3_s2a)
            s.wait_ge(sm.bv1, j + 1)
            if j >= 2:
                s.wait_ge(sm.bv2, j - 1)
            s.activation(sscr[:, (j % 2) * T:(j % 2 + 1) * T],
                         kscr[:, (j % 2) * T:(j % 2 + 1) * T],
                         AF.Sigmoid, scale=svec[:, C_SG:C_SG + 1])
            s.drain()
            s.activation(kscru[:, (j % 2) * T:(j % 2 + 1) * T],
                         sscr[:, (j % 2) * T:(j % 2 + 1) * T],
                         AF.Copy, scale=65536.0).then_inc(sm.bs1, 1)

    @blk.vector
    def _(v):
        for jj in range(NFC + 1):
            if jj < NFC:
                j = jj
                v.wait_ge(sm.gev, (NFC + j) * 4 + 4)
                v.tensor_reduce(gcol[:, 1:2], upsb[:, (j % 2) * T:(j % 2 + 1) * T],
                                AX.X, ALU.max, apply_absolute_value=True)
                v.drain()
                v.tensor_tensor(macc[:, M_U:M_U + 1], macc[:, M_U:M_U + 1],
                                gcol[:, 1:2], ALU.max)
                v.drain()
                v.engine_nop().then_inc(sm.gmx, 1)
                if j == 0:
                    v.wait_ge(sm.sp_v3, v3_s2a)
                    v.tensor_scalar(svec[:, C_SGQ:C_SGQ + 1], svec[:, C_SG:C_SG + 1],
                                    1.0 / 65536.0, None, ALU.mult)
                    v.drain()
                if j >= 2:
                    v.wait_ge(sm.bv2, j - 1)
                    v.wait_ge(sm.bs1, j - 1)
                v.tensor_scalar(kscr[:, (j % 2) * T:(j % 2 + 1) * T],
                                arena[:, j * T:(j + 1) * T],
                                svec[:, C_ISG:C_ISG + 1], None, ALU.mult)
                v.drain()
                v.engine_nop().then_inc(sm.bv1, 1)
            if jj == 0:
                continue
            j = jj - 1
            v.wait_ge(sm.bs1, j + 1)
            sl = PS5(j)
            v.scalar_tensor_tensor(arena[:, sl * T:(sl + 1) * T],
                                   kscr[:, (j % 2) * T:(j % 2 + 1) * T],
                                   svec[:, C_SGQ:C_SGQ + 1],
                                   kscru[:, (j % 2) * T:(j % 2 + 1) * T],
                                   ALU.mult, ALU.mult)
            v.drain()
            v.tensor_reduce(gcol[:, 1:2], arena[:, sl * T:(sl + 1) * T], AX.X,
                            ALU.max, apply_absolute_value=True)
            v.drain()
            v.tensor_tensor(macc[:, M_G2:M_G2 + 1], macc[:, M_G2:M_G2 + 1],
                            gcol[:, 1:2], ALU.max)
            v.drain()
            v.engine_nop().then_inc(sm.bv2, 1)

    if "arena" in dbg:
        @blk.sync
        def _(sync):
            sync.wait_ge(sm.gmx, 48)
            sync.dma_start(out=dbgd["arena"], in_=arena[:, :]).then_inc(sm.dbgs, 16)
        ndbg[0] += 16
    if "uph" in dbg:
        @blk.sync
        def _(sync):
            sync.wait_ge(sm.upw0, 16 * 12)
            sync.wait_ge(sm.upw1, 16 * 12)
            sync.dma_start(out=dbgd["uph"], in_=uph).then_inc(sm.dbgs, 16)
        ndbg[0] += 16

    if "parena" in dbg:
        @blk.sync
        def _(sync):
            sync.wait_ge(sm.bv2, NFC)
            sync.dma_start(out=dbgd["parena"], in_=arena[:, :]).then_inc(sm.dbgs, 16)
        ndbg[0] += 16

    sync_point(2, S2B_PAIRS, t1val=s2v["b_t1"], t2val=s2v["b_t2"])
    v3_s2b = spc["v3"]

    mark("P5")
    @blk.sync
    def _(sync):
        sync.wait_ge(sm.upw0, 16 * 12)
        sync.wait_ge(sm.upw1, 16 * 12)
        for i in range(NFC):
            if i >= 2:
                sync.wait_ge(sm.cv2, i - 1)
            sync.dma_start(out=upsb[:, (i % 2) * T:(i % 2 + 1) * T],
                           in_=bass.AP(uph.tensor, i * 128 * T, [[T, 128], [1, T]])
                           ).then_inc(uprdS[i % 2], 16)

    @blk.vector
    def _(v):
        for jj in range(NFC + 1):
            if jj == 0:
                v.tensor_tensor(svec[:, C_SGU:C_SGU + 1], svec[:, C_SG2:C_SG2 + 1],
                                svec[:, C_SU:C_SU + 1], ALU.mult)
                v.drain()
            if jj < NFC:
                i = jj
                sl = PS5(i)
                v.tensor_scalar(kscr[:, (i % 2) * T:(i % 2) * T + HT],
                                arena[:, sl * T:sl * T + HT],
                                svec[:, C_ISG2:C_ISG2 + 1], None, ALU.mult)
                v.drain()
            if jj == 0:
                continue
            i = jj - 1
            r = i % 2
            sl = PS5(i)
            v.wait_ge(sm.cs1, i + 1)
            v.scalar_tensor_tensor(arena[:, sl * T:(sl + 1) * T],
                                   kscr[:, r * T:(r + 1) * T], svec[:, C_SGU:C_SGU + 1],
                                   kscr3[:, r * T:(r + 1) * T], ALU.mult, ALU.mult)
            v.drain()
            v.tensor_reduce(gcol[:, 1:2], arena[:, sl * T:(sl + 1) * T], AX.X,
                            ALU.max, apply_absolute_value=True)
            v.drain()
            v.tensor_tensor(macc[:, M_O:M_O + 1], macc[:, M_O:M_O + 1],
                            gcol[:, 1:2], ALU.max)
            v.drain()
            v.engine_nop().then_inc(sm.cv2, 1)

    @blk.scalar
    def _(s):
        s.wait_ge(sm.sp_v3, v3_s2b)
        for i in range(NFC):
            r = i % 2
            if i >= 2:
                s.wait_ge(sm.cv2, i - 1)
            sl = PS5(i)
            s.activation(kscr[:, r * T + HT:(r + 1) * T], arena[:, sl * T + HT:(sl + 1) * T],
                         AF.Copy, scale=svec[:, C_ISG2:C_ISG2 + 1])
            s.drain()
            s.wait_ge(uprdS[r], 16 * (i // 2 + 1))
            s.activation(kscr3[:, r * T:(r + 1) * T], upsb[:, r * T:(r + 1) * T],
                         AF.Copy, scale=svec[:, C_ISU:C_ISU + 1]).then_inc(sm.cs1, 1)
            s.drain()

    if "oarena" in dbg:
        @blk.sync
        def _(sync):
            sync.wait_ge(sm.cv2, NFC)
            sync.dma_start(out=dbgd["oarena"], in_=arena[:, :]).then_inc(sm.dbgs, 16)
        ndbg[0] += 16

    sync_point(3, [(M_O, C_SO, C_ISO)], sm.cv2, NFC)

    if "sc" in dbg:
        @blk.sync
        def _(sync):
            sync.wait_ge(sm.sp_v3, spc["v3"])
            sync.dma_start(out=dbgd["sc"], in_=svec[:, :]).then_inc(sm.dbgs, 16)
        ndbg[0] += 16

    mark("P6")
    right2.close()
    left4b.close()
    right3 = ExitStack()
    wdn = right3.enter_context(nc.sbuf_tensor("wdn", [128, NFC * D], BF16, side="right"))
    left5 = ExitStack()
    eL5 = left5.enter_context
    f4r = eL5(nc.sbuf_tensor("f4r", [128, 2 * D], F32, side="left"))
    x2rr = eL5(nc.sbuf_tensor("x2rr", [128, 2 * D], F32, side="left"))
    orng = eL5(nc.sbuf_tensor("orng", [128, 2 * D], F32, side="left"))

    @blk.gpsimd
    def _(g):
        g.wait_ge(sm.agc, 1)
        g.wait_ge(sm.gm, 48)
        for i in range(NFC):
            g.dma_start(out=wdn[:, i * D:(i + 1) * D].rearrange("p (a b) -> p a b", a=8),
                        in_=bass.AP(agout.tensor, (48 + i) * 128,
                                    [[AGCOLS, 128], [AGW, 8], [1, 128]])
                        ).then_inc(sm.wdl, 16)


    @blk.vector
    def _(v):
        v.wait_ge(sm.sp_v3, spc["v3"])
        for i in range(NFC):
            r = i % 2
            sl = PS5(i)
            v.tensor_scalar(kscr[:, r * T:r * T + HT], arena[:, sl * T:sl * T + HT],
                            svec[:, C_ISO:C_ISO + 1], None, ALU.mult)
            v.drain()
            v.tensor_scalar(arena[:, sl * T:sl * T + HT], kscr[:, r * T:r * T + HT],
                            svec[:, C_SO:C_SO + 1], None, ALU.mult)
            v.drain()
            v.engine_nop().then_inc(sm.ov2, 1)

    @blk.scalar
    def _(s):
        s.wait_ge(sm.sp_v3, spc["v3"])
        for i in range(NFC):
            r = i % 2
            sl = PS5(i)
            s.activation(kscr3[:, r * T:r * T + HT], arena[:, sl * T + HT:(sl + 1) * T],
                         AF.Copy, scale=svec[:, C_ISO:C_ISO + 1])
            s.drain()
            s.activation(arena[:, sl * T + HT:(sl + 1) * T], kscr3[:, r * T:r * T + HT],
                         AF.Copy, scale=svec[:, C_SO:C_SO + 1]).then_inc(sm.os1, 1)

    @blk.tensor
    def _(t):
        t.wait_ge(sm.wdl, 16 * NFC)
        for ti in range(NTR):
            r = ti % 2
            if ti >= 2:
                t.wait_ge(sm.dlv1, ti - 1)
                t.wait_ge(sm.dls2, ti - 1)
            for i in range(NFC):
                if ti == 0:
                    t.wait_ge(sm.os1, i + 1)
                    t.wait_ge(sm.ov2, i + 1)
                sl = PS5(i)
                for half in range(2):
                    ins_ = t.matmul(pS[:, r * 1024 + half * 512:r * 1024 + (half + 1) * 512],
                                    arena[:, sl * T + ti * 128:sl * T + (ti + 1) * 128],
                                    wdn[:, i * D + half * 512:i * D + (half + 1) * 512],
                                    start=(i == 0), stop=(i == NFC - 1))
            ins_.then_inc(sm.dm, 1)

    @blk.sync
    def _(sync):
        sync.wait_ge(sm.cv2, NFC)
        for ti in range(NTR):
            if ti >= 2:
                sync.wait_ge(sm.oad, ti - 1)
            sync.dma_start(out=x2rr[:, (ti % 2) * D:(ti % 2 + 1) * D],
                           in_=x2h[ti * 128:(ti + 1) * 128, :]).then_inc(x2rdS[ti % 2], 16)
            if ti >= 1:
                sync.wait_ge(sm.oad, ti)
                sync.dma_start(out=out[(ti - 1) * 128:ti * 128, :],
                               in_=orng[:, ((ti - 1) % 2) * D:((ti - 1) % 2 + 1) * D]).then_inc(ostS[(ti - 1) % 2], 16)
        sync.wait_ge(sm.oad, NTR)
        sync.dma_start(out=out[(NTR - 1) * 128:NTR * 128, :],
                       in_=orng[:, ((NTR - 1) % 2) * D:((NTR - 1) % 2 + 1) * D]).then_inc(ostS[(NTR - 1) % 2], 16)
        sync.wait_ge(sm.ost0, 16 * 8)
        sync.wait_ge(sm.ost1, 16 * 8)
        if ndbg[0]:
            sync.wait_ge(sm.dbgs, ndbg[0])

    @blk.vector
    def _(v):
        for ti in range(NTR):
            r = ti % 2
            st = stats[:, r * 14:r * 14 + 14]
            v.wait_ge(sm.dm, ti + 1)
            src = pS[:, r * 1024:(r + 1) * 1024]
            v.bn_stats(st[:, 0:6], src[:, 0:512])
            v.bn_stats(st[:, 6:12], src[:, 512:1024])
            v.drain()
            v.bn_aggr(st[:, 12:14], st[:, 0:12])
            v.drain()
            v.engine_nop().then_inc(sm.dlv1, 1)
            v.wait_ge(sm.dls1, ti + 1)
            v.scalar_tensor_tensor(lncol[:, r * 6 + 1:r * 6 + 2], st[:, 12:13],
                                   -1.0, lncol[:, r * 6:r * 6 + 1], ALU.mult, ALU.mult)
            v.drain()
            v.engine_nop().then_inc(sm.dlv2, 1)
            v.wait_ge(sm.dls2, ti + 1)
            v.wait_ge(x2rdS[ti % 2], 16 * (ti // 2 + 1))
            if ti >= 2:
                v.wait_ge(ostS[ti % 2], 16 * ((ti - 2) // 2 + 1))
            v.tensor_tensor(orng[:, r * D:(r + 1) * D], f4r[:, r * D:(r + 1) * D],
                            x2rr[:, r * D:(r + 1) * D], ALU.add)
            v.drain()
            v.engine_nop().then_inc(sm.oad, 1)

    @blk.scalar
    def _(s):
        for ti in range(NTR):
            r = ti % 2
            st = stats[:, r * 14:r * 14 + 14]
            s.wait_ge(sm.dlv1, ti + 1)
            s.activation(lncol[:, r * 6 + 2:r * 6 + 3], st[:, 13:14], AF.Ln, bias=cstv[:, 0:1])
            s.drain()
            s.activation(lncol[:, r * 6:r * 6 + 1], lncol[:, r * 6 + 2:r * 6 + 3],
                         AF.Exp, scale=-0.5).then_inc(sm.dls1, 1)
            s.wait_ge(sm.dlv2, ti + 1)
            if ti >= 2:
                s.wait_ge(sm.oad, ti - 1)
            s.activation(f4r[:, r * D:(r + 1) * D], pS[:, r * 1024:(r + 1) * 1024],
                         AF.Identity, bias=lncol[:, r * 6 + 1:r * 6 + 2],
                         scale=lncol[:, r * 6:r * 6 + 1]).then_inc(sm.dls2, 1)

    mark("P7")
    left5.close()
    left4.close()
    right3.close()
    top.close()
    return nc, dbgd


def _host_prepare(L0_attn_x, L0_attn_Wqkv, L0_attn_bqkv, L0_attn_Wo, L0_attn_bo,
                  L0_ffn_w_gate, L0_ffn_w_up, L0_ffn_w_down):
    x = np.asarray(L0_attn_x, dtype=np.float32)
    Wqkv = np.asarray(L0_attn_Wqkv, dtype=np.float32)
    bqkv = np.asarray(L0_attn_bqkv, dtype=np.float32)
    Wo = np.ascontiguousarray(np.asarray(L0_attn_Wo, dtype=np.float32))
    bo = np.asarray(L0_attn_bo, dtype=np.float32)
    w_gate = np.asarray(L0_ffn_w_gate, dtype=np.float32)
    w_up = np.asarray(L0_ffn_w_up, dtype=np.float32)
    w_down = np.asarray(L0_ffn_w_down, dtype=np.float32)

    qcols = np.concatenate([np.arange(192 * h, 192 * h + 64) for h in range(H)])
    kcols = np.concatenate([np.arange(192 * h + 64, 192 * h + 128) for h in range(H)])
    vcols = np.concatenate([np.arange(192 * h + 128, 192 * h + 192) for h in range(H)])
    wqk = np.ascontiguousarray(Wqkv[:, np.concatenate([qcols, kcols])])
    wv = np.ascontiguousarray(Wqkv[:, vcols])
    bqk = bqkv[np.concatenate([qcols, kcols])]
    bqkT = np.ascontiguousarray(bqk.reshape(16, 128).T)
    bv = np.ascontiguousarray(bqkv[vcols]).reshape(1, D)
    identb = np.eye(128, dtype=BF)
    identf = np.eye(128, dtype=np.float32)

    in_maps = []
    for c in range(NCORE):
        b, hh = c // 2, c % 2
        xpad = np.zeros((TP, D), np.float32)
        xpad[PAD:PAD + T] = x[b, 2048 * hh:2048 * hh + T]
        if hh == 0:
            xpad[PAD + T:PAD + T + 16] = x[b, 2048:2064]
        else:
            xpad[0:PAD] = x[b, 2032:2048]
        m = np.zeros((KB, NBLK * QB), np.float32)
        k_ = np.arange(KB)[:, None]
        q_ = np.arange(QB)[None, :]
        band = (k_ >= q_) & (k_ <= q_ + 2 * W)
        for blki in range(NBLK):
            pk = blki * QB + k_
            gk = 2048 * hh + pk - PAD
            valid = band & (gk >= 0) & (gk < S)
            qreal = (blki * QB + q_) < T
            mm = (valid & qreal).astype(np.float32)
            mm[0, ~qreal[0, :]] = 1.0
            m[:, blki * QB:(blki + 1) * QB] = mm
        in_maps.append(dict(
            xp=xpad, wqk=wqk, wvw=wv, bqkT=bqkT, bvin=bv, woin=Wo,
            boin=bo.reshape(1, D),
            wup_s=np.ascontiguousarray(w_up[UPS * c:UPS * (c + 1)]),
            wgate_s=np.ascontiguousarray(w_gate[UPS * c:UPS * (c + 1)]),
            wdown_s=np.ascontiguousarray(w_down[DNS * c:DNS * (c + 1)]),
            bandmask=m.astype(BF), idb=identb, idf=identf,
        ))
    return in_maps


_BUILT = [None]


def kernel(**inputs):
    in_maps = _host_prepare(**inputs)
    if _BUILT[0] is None:
        _BUILT[0] = build()[0]
    nc = _BUILT[0]
    res = run_bass_kernel_spmd(nc, in_maps, core_ids=list(range(NCORE)))
    y = np.zeros((B, S, D), np.float32)
    for c in range(NCORE):
        b, hh = c // 2, c % 2
        y[b, 2048 * hh:2048 * hh + T] = res.results[c]["out"]
    return y
